# version 05
# baseline (speedup 1.0000x reference)
import math
import numpy as np

import concourse.bass as bass
import concourse.mybir as mybir
from concourse.bass_utils import run_bass_kernel_spmd

F32 = mybir.dt.float32
BF16 = mybir.dt.bfloat16
I32 = mybir.dt.int32
AF = mybir.ActivationFunctionType
OP = mybir.AluOpType
AX = mybir.AxisListType

LAMBDA_INIT = 0.8 - 0.6 * math.exp(-0.3 * 6)
EPS = 1e-5

CFG = dict(D=1024, S=2048, HID=4096, BPC=2, ST=256, QB=512)


class Sem:
    __slots__ = ("h", "n", "name")

    def __init__(self, nc, name):
        self.h = nc.alloc_semaphore(name)
        self.n = 0
        self.name = name


class Ctx:

    def __init__(self):
        self.active = None
        self.eng = None
        self._waited = {}
        self._need_drain = False

    def begin(self, name, eng):
        self.active = name
        self.eng = eng
        self._waited = {}
        self._need_drain = False

    def op(self, engine, emit, waits=(), inc=None, inc_k=1):
        if engine == self.active:
            if self.active in ('V', 'S') and self._need_drain:
                self.eng.drain()
                self._need_drain = False
            for w in waits:
                if w is None:
                    continue
                sem, val = w
                assert val <= sem.n, (
                    f"wait-ahead on {sem.name}: {val} > {sem.n}")
                if val <= 0 or self._waited.get(sem.name, -1) >= val:
                    continue
                self.eng.wait_ge(sem.h, val)
                self._waited[sem.name] = val
            inst = emit(self.eng)
            self._need_drain = True
            if inc is not None:
                if isinstance(inst, list):
                    inst = inst[-1]
                inst.then_inc(inc.h, inc_k)
        if inc is not None:
            inc.n += inc_k


def _fv(ap2d, nblk, blk_cols):
    return ap2d.rearrange("p (f t) -> p f t", f=nblk, t=blk_cols)


def build_nc(cfg):
    D, S, HID, BPC = cfg["D"], cfg["S"], cfg["HID"], cfg["BPC"]
    ST, QB = cfg["ST"], cfg["QB"]
    FB = D // 128
    HB = HID // 128
    NH = D // 128
    NQB = S // QB
    KT = S // 128
    TPQ = QB // 128
    HPG = min(4, NH)
    NG = NH // HPG
    VCW = min(512, D)
    NV = D // VCW
    assert 2 * HPG <= 8 and NH <= 8 and S % QB == 0 and D % VCW == 0

    nc = bass.Bass("TRN2", target_bir_lowering=False, debug=False)

    xT = nc.dram_tensor("xT", [BPC, D, S], F32, kind="ExternalInput").ap()
    wqT = nc.dram_tensor("wqT", [D, D], F32, kind="ExternalInput").ap()
    wkT = nc.dram_tensor("wkT", [D, D], F32, kind="ExternalInput").ap()
    wvT = nc.dram_tensor("wvT", [D, D], F32, kind="ExternalInput").ap()
    woT = nc.dram_tensor("woT", [D, D], F32, kind="ExternalInput").ap()
    w1T = nc.dram_tensor("w1T", [D, HID], F32, kind="ExternalInput").ap()
    w2T = nc.dram_tensor("w2T", [HID, D], F32, kind="ExternalInput").ap()
    b1d = nc.dram_tensor("b1", [HID], F32, kind="ExternalInput").ap()
    b2d = nc.dram_tensor("b2", [D], F32, kind="ExternalInput").ap()
    lqkd = nc.dram_tensor("lqk", [1, 256], F32, kind="ExternalInput").ap()
    maskd = nc.dram_tensor("mask", [128, 896], BF16, kind="ExternalInput").ap()
    indd = nc.dram_tensor("ind", [128, 64], BF16, kind="ExternalInput").ap()
    seld = nc.dram_tensor("sel", [8, 8 * 128], F32, kind="ExternalInput").ap()

    outT = nc.dram_tensor("outT", [BPC, D, S], F32, kind="ExternalOutput").ap()
    g_scr = nc.dram_tensor("g_scr", [BPC, D, S], F32, kind="Internal").ap()
    xn_scr = nc.dram_tensor("xn_scr", [BPC, D, S], BF16, kind="Internal").ap()
    DBG = bool(cfg.get("debug"))
    if DBG:
        NQB_ = S // QB
        NG_ = (D // 128) // min(4, D // 128)
        HPG_ = min(4, D // 128)
        dbg_u = nc.dram_tensor("dbg_u", [BPC, NQB_, NG_, 128, 2 * HPG_ * QB],
                               F32, kind="ExternalOutput").ap()
        dbg_d = nc.dram_tensor("dbg_d", [BPC, NQB_, NG_, 8, QB], F32,
                               kind="ExternalOutput").ap()
        dbg_o = nc.dram_tensor("dbg_o", [BPC, NQB_, 128, (D // 128) * QB],
                               BF16, kind="ExternalOutput").ap()
        dbg_k = nc.dram_tensor("dbg_k", [BPC, 128, (D // 128) * S], BF16,
                               kind="ExternalOutput").ap()
        dbg_v = nc.dram_tensor("dbg_v", [BPC, 128, (S // 128) * D], BF16,
                               kind="ExternalOutput").ap()
        dbg_q = nc.dram_tensor("dbg_q", [BPC, NQB_, 128, (D // 128) * QB],
                               BF16, kind="ExternalOutput").ap()
        dbg_xc = nc.dram_tensor("dbg_xc", [BPC, NQB_, 128, (D // 128) * QB],
                                BF16, kind="ExternalOutput").ap()
        NT_ = (BPC * S) // ST
        dbg_hq = nc.dram_tensor("dbg_hq", [NT_, 128, (HID // 128) * ST],
                                BF16, kind="ExternalOutput").ap()
        dbg_xq = nc.dram_tensor("dbg_xq", [NT_, 128, (D // 128) * ST],
                                BF16, kind="ExternalOutput").ap()
        dbg_ub = nc.dram_tensor("dbg_ub", [4, 128, ST], F32,
                                kind="ExternalOutput").ap()

    c_w1q = 0
    c_w2q = c_w1q + FB * HID
    c_hqb = c_w2q + HB * D
    c_xqb = c_hqb + HB * ST
    c_xnst = c_xqb + FB * ST
    a_cols = c_xnst + 4 * ST

    c_wq = 0
    c_xnc = c_wq + 4 * FB * D
    c_kt = c_xnc + FB * QB
    c_v = c_kt + NH * S
    c_qt = c_v + KT * D
    c_e = c_qt + 2 * NH * QB
    c_sqr = c_e + 4 * QB
    b_cols = c_sqr + QB
    BFC = max(a_cols, b_cols)

    QW = min(4 * ST, 1024)
    f_xts = 0
    f_xnf = f_xts + FB * ST
    f_c = f_xnf + 2 * ST
    f_u = f_c + QW
    f_t1 = f_u + 2 * QW
    f_t2 = f_t1 + QW
    f_t3 = f_c
    f_gs = f_t2 + QW
    f_sq = f_gs + 4 * ST
    f_mbc = f_sq + 4 * ST
    a_fcols = f_mbc + 2 * ST

    f_qf = f_u
    f_qc = f_qf + 512
    f_qc2 = f_qc + 512
    f_qc3 = f_qc2 + 512

    f_uall = 0
    f_gs2 = f_uall + 2 * HPG * QB
    f_xts2 = f_gs2 + 2 * QB
    f_outst = f_xts2 + 2 * QB
    f_ebt = f_outst + 2 * QB
    f_d8 = f_ebt + QB
    f_al8 = f_d8 + QB
    f_ms8 = f_al8 + QB
    f_bt8 = f_ms8 + QB
    b_fcols = f_bt8 + QB
    f_mv = a_fcols
    f_lnt = f_mv + 3 * ST
    a_fcols = f_lnt + ST
    F32C = max(a_fcols, b_fcols)

    AB = nc.alloc_sbuf_tensor("AB", [128, BFC], BF16).ap()
    AF4 = nc.alloc_sbuf_tensor("AF4", [128, F32C], F32).ap()
    KI = AB[:, c_e:c_e + 2048].bitcast(I32)

    mask_sb = nc.alloc_sbuf_tensor("mask_sb", [128, 896], BF16).ap()
    ind_sb = nc.alloc_sbuf_tensor("ind_sb", [128, 64], BF16).ap()
    sel_sb = nc.alloc_sbuf_tensor("sel_sb", [8, 8 * 128], F32).ap()
    ones_bf = nc.alloc_sbuf_tensor("ones_bf", [128, 1], BF16).ap()
    ones_cf = nc.alloc_sbuf_tensor("ones_cf", [128, 1], F32).ap()
    ones1f = nc.alloc_sbuf_tensor("ones1f", [1, 128], F32).ap()
    b1q = nc.alloc_sbuf_tensor("b1q", [128, HB], F32).ap()
    bias1 = nc.alloc_sbuf_tensor("bias1", [128, HB], F32).ap()
    b2q = nc.alloc_sbuf_tensor("b2q", [128, FB], F32).ap()
    lam_in = AF4[0:1, f_mv:f_mv + 256]
    lam_sb = nc.alloc_sbuf_tensor("lam_sb", [1, 8], F32).ap()
    lam_col = nc.alloc_sbuf_tensor("lam_col", [128, 1], F32).ap()
    d8 = AF4[0:8, f_d8:f_d8 + QB]
    al8 = AF4[0:8, f_al8:f_al8 + QB]
    ms8 = AF4[0:8, f_ms8:f_ms8 + QB]
    bt8 = AF4[0:8, f_bt8:f_bt8 + QB]
    mv_sb = AF4[0:1, f_mv:f_mv + 4 * ST]
    lnt = AF4[0:1, f_lnt:f_lnt + ST]
    fc = nc.alloc_sbuf_tensor("fc", [128, 2], F32).ap()

    PA = nc.alloc_psum_tensor("PA", [128, 1024], F32).ap()
    PB = nc.alloc_psum_tensor("PB", [128, 1024], F32).ap()
    ps = [PA[:, 0:512], PA[:, 512:1024], PB[:, 0:512], PB[:, 512:1024]]
    ps += [nc.alloc_psum_tensor(f"ps{i}", [128, 512], F32).ap()
           for i in range(4, 8)]
    psU0, psU1, psD, psM = ps[4], ps[5], ps[6], ps[7]

    sT = Sem(nc, "sT")
    sS = Sem(nc, "sS")
    sV = Sem(nc, "sV")
    qx = Sem(nc, "qx")
    qn = [Sem(nc, "qn0"), Sem(nc, "qn1")]
    qg = [Sem(nc, f"qg{i}") for i in range(4)]
    qc = Sem(nc, "qc")
    qe = [Sem(nc, "qe0"), Sem(nc, "qe1")]
    qo = [Sem(nc, "qo0"), Sem(nc, "qo1")]
    qk = Sem(nc, "qk")
    qw = [Sem(nc, "qw0"), Sem(nc, "qw1")]
    qa = Sem(nc, "qa")
    qd = Sem(nc, "qd")

    ctx = Ctx()

    bank_free = [None] * 8

    def schedule():
        o = ctx.op
        for i in range(8):
            bank_free[i] = None

        o('G', lambda e: e.dma_start(mask_sb, maskd), inc=qk, inc_k=16)
        o('G', lambda e: e.dma_start(ind_sb, indd), inc=qk, inc_k=16)
        o('G', lambda e: e.dma_start(sel_sb, seld), inc=qk, inc_k=16)
        o('G', lambda e: e.dma_start(lam_in, lqkd), inc=qk, inc_k=16)
        with nc.allow_non_contiguous_dma("tiny bias loads"):
            o('G', lambda e: e.dma_start(
                b1q, b1d.rearrange("(c p) -> p c", p=128)), inc=qk, inc_k=16)
            o('G', lambda e: e.dma_start(
                b2q, b2d.rearrange("(c p) -> p c", p=128)), inc=qk, inc_k=16)
        const_loaded = (qk, qk.n)
        o('V', lambda e: e.memset(ones_bf, 1.0))
        o('V', lambda e: e.memset(ones_cf, 1.0))
        o('V', lambda e: e.memset(fc[:, 0:1], float(EPS)))
        o('V', lambda e: e.memset(fc[:, 1:2], -32.0))
        o('V', lambda e: e.memset(ones1f, 1.0), inc=sV)

        def quant_inplace(dst, src, sc, lo, hi, n, ioff=0):
            ii = KI[:, ioff:ioff + n]
            fb_ = AF4[:, f_qf:f_qf + n]
            cb_ = AF4[:, f_qc:f_qc + n]
            cb2 = AF4[:, f_qc2:f_qc2 + n]
            pb_ = AF4[:, f_qc3:f_qc3 + n]
            o('V', lambda e: e.tensor_scalar(
                src, src, float(sc), None, OP.mult))
            o('V', lambda e: e.tensor_copy(ii, src))
            o('V', lambda e: e.tensor_copy(fb_, ii))
            o('V', lambda e: e.tensor_tensor(cb_, src, fb_, OP.subtract))
            o('V', lambda e: e.tensor_scalar(ii, ii, 1, None, OP.bitwise_and))
            o('V', lambda e: e.tensor_copy(pb_, ii))
            o('V', lambda e: e.tensor_scalar(
                cb2, cb_, 0.5, None, OP.is_equal))
            o('V', lambda e: e.tensor_scalar(
                src, cb_, -0.5, None, OP.is_equal))
            o('V', lambda e: e.tensor_tensor(cb2, cb2, src, OP.subtract))
            o('V', lambda e: e.tensor_tensor(cb2, cb2, pb_, OP.mult))
            o('V', lambda e: e.tensor_scalar(
                pb_, cb_, 0.5, None, OP.is_gt))
            o('V', lambda e: e.tensor_scalar(
                cb_, cb_, -0.5, None, OP.is_lt))
            o('V', lambda e: e.tensor_tensor(pb_, pb_, cb_, OP.subtract))
            o('V', lambda e: e.tensor_tensor(pb_, pb_, cb2, OP.add))
            o('V', lambda e: e.tensor_tensor(fb_, fb_, pb_, OP.add))
            o('V', lambda e: e.tensor_scalar(
                fb_, fb_, 1.0 / sc, None, OP.mult))
            o('V', lambda e: e.tensor_scalar(
                dst, fb_, float(lo), float(hi), OP.max, OP.min), inc=sV)

        o('V', lambda e: e.engine_nop(), waits=[const_loaded])
        quant_inplace(b1q, b1q, 64.0, -2.0, 127.0 / 64.0, HB)
        quant_inplace(b2q, b2q, 64.0, -2.0, 127.0 / 64.0, FB)
        o('V', lambda e: e.tensor_scalar(
            bias1, b1q, 16.0, 512.0, OP.mult, OP.add), inc=sV)

        o('V', lambda e: e.tensor_tensor(
            lam_in[:, 0:64], lam_in[:, 0:64], lam_in[:, 64:128], OP.mult))
        o('V', lambda e: e.tensor_tensor(
            lam_in[:, 128:192], lam_in[:, 128:192], lam_in[:, 192:256], OP.mult))
        o('V', lambda e: e.reduce_sum(lam_sb[:, 0:1], lam_in[:, 0:64], axis=AX.X))
        o('V', lambda e: e.reduce_sum(lam_sb[:, 1:2], lam_in[:, 128:192], axis=AX.X),
          inc=sV)
        o('S', lambda e: e.activation(lam_sb[:, 2:3], lam_sb[:, 0:1], AF.Exp),
          waits=[(sV, sV.n)])
        o('S', lambda e: e.activation(lam_sb[:, 3:4], lam_sb[:, 1:2], AF.Exp),
          inc=sS)
        o('V', lambda e: e.tensor_tensor(
            lam_sb[:, 4:5], lam_sb[:, 2:3], lam_sb[:, 3:4], OP.subtract),
          waits=[(sS, sS.n)])
        o('V', lambda e: e.tensor_scalar(
            lam_sb[:, 4:5], lam_sb[:, 4:5], LAMBDA_INIT, None, OP.add), inc=sV)
        o('T', lambda e: e.matmul(ps[0][:, 0:1], ones1f, lam_sb[:, 4:5],
                                  start=True, stop=True),
          waits=[(sV, sV.n)], inc=sT)
        o('S', lambda e: e.copy(lam_col, ps[0][:, 0:1]),
          waits=[(sT, sT.n)], inc=sS)
        bank_free[0] = (sS, sS.n)

        w1q = AB[:, c_w1q:c_w1q + FB * HID]
        w2q = AB[:, c_w2q:c_w2q + HB * D]
        WSTW = 512
        CW1 = min(WSTW, HID)
        CW2 = min(WSTW, D)
        wst = [AF4[:, f_t1:f_t1 + WSTW],
               AF4[:, f_t1 + WSTW:f_t1 + 2 * WSTW]]
        wst_free = [None, None]
        wjobs = ([("w1", kb, cc) for kb in range(FB)
                  for cc in range(HID // CW1)] +
                 [("w2", hb, cc) for hb in range(HB)
                  for cc in range(D // CW2)])
        w1q_ready = None
        for i, (w, b, cc) in enumerate(wjobs):
            sb = i % 2
            if w == "w1":
                src = w1T[b * 128:(b + 1) * 128, cc * CW1:(cc + 1) * CW1]
                dst = w1q[:, b * HID + cc * CW1:b * HID + (cc + 1) * CW1]
                n = CW1
            else:
                src = w2T[b * 128:(b + 1) * 128, cc * CW2:(cc + 1) * CW2]
                dst = w2q[:, b * D + cc * CW2:b * D + (cc + 1) * CW2]
                n = CW2
            o('G', lambda e, src=src, d=wst[sb], n=n: e.dma_start(
                d[:, 0:n], src), waits=[wst_free[sb]], inc=qw[sb], inc_k=16)
            o('V', lambda e: e.engine_nop(), waits=[(qw[sb], qw[sb].n)])
            quant_inplace(dst, wst[sb][:, 0:n], 64.0, -2.0, 127.0 / 64.0, n)
            wst_free[sb] = (sV, sV.n)
            if w == "w1" and b == FB - 1 and cc == HID // CW1 - 1:
                w1q_ready = (sV, sV.n)
        w2q_ready = (sV, sV.n)

        xts = AF4[:, f_xts:f_xts + FB * ST]
        xnf = AF4[:, f_xnf:f_xnf + 2 * ST]
        cbuf = AF4[:, f_c:f_c + QW]
        ubuf = [AF4[:, f_u:f_u + QW], AF4[:, f_u + QW:f_u + 2 * QW]]
        t1b = AF4[:, f_t1:f_t1 + QW]
        t2b = AF4[:, f_t2:f_t2 + QW]
        t3b = AF4[:, f_t3:f_t3 + QW]
        gs = [AF4[:, f_gs + i * ST:f_gs + (i + 1) * ST] for i in range(4)]
        xqb = AB[:, c_xqb:c_xqb + FB * ST]
        hqb = AB[:, c_hqb:c_hqb + HB * ST]
        xnst = [AB[:, c_xnst:c_xnst + 2 * ST],
                AB[:, c_xnst + 2 * ST:c_xnst + 4 * ST]]
        sqfb = [AF4[:, f_sq:f_sq + 2 * ST], AF4[:, f_sq + 2 * ST:f_sq + 4 * ST]]
        m_bc = AF4[:, f_mbc:f_mbc + ST]
        r_bc = AF4[:, f_mbc + ST:f_mbc + 2 * ST]

        def bc2(ap):
            return ap.unsqueeze(1).to_broadcast([128, 2, ST])

        def pv2(ap2d, w):
            return ap2d.rearrange("p (f c) -> p f c", f=2, c=w)

        gs_free = [None] * 4
        dbg_ffn_free = [None]
        xnst_free = [None, None]
        ub_free = [None, None]
        xq_free = None
        hq_free = None
        xnf_free = None
        stats_bank_free = None
        xnst_i = 0
        gs_i = 0
        NQD = HB // 4

        supertiles = [(s, t0) for s in range(BPC) for t0 in range(0, S, ST)]
        o('Q', lambda e: e.dma_start(
            _fv(xts, FB, ST),
            xT[supertiles[0][0], :, supertiles[0][1]:supertiles[0][1] + ST]
            .rearrange("(f p) t -> p f t", p=128)), inc=qx, inc_k=16)
        xts_load = (qx, qx.n)
        sq_mm_done = [None, None]

        for ti, (s, t0) in enumerate(supertiles):
            for fp in range(FB // 2):
                sqi = fp % 2
                o('S', lambda e, fp=fp, sqi=sqi: e.activation(
                    sqfb[sqi], xts[:, fp * 2 * ST:(fp + 1) * 2 * ST],
                    AF.Square), waits=[xts_load, sq_mm_done[sqi]], inc=sS)
                w_t = [xts_load]
                if fp == 0:
                    w_t += [stats_bank_free, bank_free[6], bank_free[7]]
                sq_done = (sS, sS.n)
                for h2 in range(2):
                    fb = fp * 2 + h2
                    o('T', lambda e, fb=fb: e.matmul(
                        ps[6][0:1, 0:ST], ones_cf,
                        xts[:, fb * ST:(fb + 1) * ST],
                        start=(fb == 0), stop=(fb == FB - 1)),
                      waits=(w_t if fb == 0 else []))
                    o('T', lambda e, fb=fb, sqi=sqi, h2=h2: e.matmul(
                        ps[7][0:1, 0:ST], ones_cf,
                        sqfb[sqi][:, h2 * ST:(h2 + 1) * ST],
                        start=(fb == 0), stop=(fb == FB - 1)),
                      waits=([sq_done] if h2 == 0 else []), inc=sT)
                sq_mm_done[sqi] = (sT, sT.n)
            m_row = mv_sb[:, 0:ST]
            r_row = mv_sb[:, ST:2 * ST]
            v_row = mv_sb[:, 2 * ST:3 * ST]
            o('V', lambda e: e.tensor_scalar(
                m_row, ps[6][0:1, 0:ST], 1.0 / D, None, OP.mult),
              waits=[(sT, sT.n)])
            o('V', lambda e: e.tensor_tensor(lnt, m_row, m_row, OP.mult))
            o('V', lambda e: e.scalar_tensor_tensor(
                v_row, ps[7][0:1, 0:ST], 1.0 / D, lnt, OP.mult, OP.subtract),
              inc=sV)
            o('S', lambda e: e.activation(
                r_row, v_row, AF.Sqrt, bias=fc[0:1, 0:1], scale=1.0),
              waits=[(sV, sV.n)], inc=sS)
            o('V', lambda e: e.reciprocal(r_row, r_row),
              waits=[(sS, sS.n)], inc=sV)
            o('T', lambda e: e.matmul(ps[6][:, 0:ST], ones1f, m_row,
                                      start=True, stop=True),
              waits=[(sV, sV.n)])
            o('T', lambda e: e.matmul(ps[7][:, 0:ST], ones1f, r_row,
                                      start=True, stop=True), inc=sT)
            o('S', lambda e: e.copy(m_bc, ps[6][:, 0:ST]),
              waits=[(sT, sT.n), xq_free])
            o('S', lambda e: e.copy(r_bc, ps[7][:, 0:ST]), inc=sS)
            stats_bank_free = (sS, sS.n)
            bcast_done = (sS, sS.n)
            bank_free[6] = (sS, sS.n)
            bank_free[7] = (sS, sS.n)
            for fp in range(FB // 2):
                fb0 = fp * 2
                xsl = xts[:, fb0 * ST:(fb0 + 2) * ST]
                o('V', lambda e, xsl=xsl: e.tensor_tensor(
                    pv2(cbuf[:, 0:2 * ST], ST), pv2(xsl, ST), bc2(m_bc),
                    OP.subtract),
                  waits=[bcast_done, xnf_free if fp == 0 else None])
                o('V', lambda e: e.tensor_tensor(
                    pv2(xnf, ST), pv2(cbuf[:, 0:2 * ST], ST), bc2(r_bc),
                    OP.mult))
                xi = xnst_i
                o('V', lambda e, xi=xi: e.tensor_copy(xnst[xi], xnf),
                  waits=[xnst_free[xi]], inc=sV)
                for h2 in range(2):
                    o('Q', lambda e, s=s, t0=t0, fb=fb0 + h2, xi=xi, h2=h2:
                      e.dma_start(
                        xn_scr[s, fb * 128:(fb + 1) * 128, t0:t0 + ST],
                        xnst[xi][:, h2 * ST:(h2 + 1) * ST]),
                      waits=([(sV, sV.n)] if h2 == 0 else []),
                      inc=qn[xi], inc_k=16)
                xnst_free[xi] = (qn[xi], qn[xi].n)
                xnst_i ^= 1
                w_q = [xq_free, dbg_ffn_free[0]] if fp == 0 else []
                W2 = 2 * ST
                o('V', lambda e: e.tensor_scalar(
                    t1b[:, 0:W2], xnf, 16.0, 512.0, OP.mult, OP.add),
                  waits=w_q)
                o('V', lambda e: e.tensor_copy(KI[:, 0:W2], t1b[:, 0:W2]))
                o('V', lambda e: e.tensor_copy(t2b[:, 0:W2], KI[:, 0:W2]))
                o('V', lambda e: e.tensor_tensor(
                    t1b[:, 0:W2], t1b[:, 0:W2], t2b[:, 0:W2], OP.subtract))
                o('V', lambda e: e.tensor_scalar(
                    t1b[:, 0:W2], t1b[:, 0:W2], 0.5, None, OP.is_ge))
                o('V', lambda e: e.tensor_tensor(
                    t1b[:, 0:W2], t1b[:, 0:W2], t2b[:, 0:W2], OP.add))
                o('V', lambda e: e.tensor_scalar(
                    t1b[:, 0:W2], t1b[:, 0:W2], -512.0, 1.0 / 16.0,
                    OP.add, OP.mult))
                o('V', lambda e, fb0=fb0: e.tensor_scalar(
                    xqb[:, fb0 * ST:(fb0 + 2) * ST], t1b[:, 0:W2],
                    -8.0, 127.0 / 16.0, OP.max, OP.min), inc=sV)
            xq_ready = (sV, sV.n)
            xts_free = (sV, sV.n)
            if ti + 1 < len(supertiles):
                s2, t02 = supertiles[ti + 1]
                o('Q', lambda e, s2=s2, t02=t02: e.dma_start(
                    _fv(xts, FB, ST),
                    xT[s2, :, t02:t02 + ST].rearrange("(f p) t -> p f t", p=128)),
                  waits=[xts_free], inc=qx, inc_k=16)
                xts_load = (qx, qx.n)

            for qd_ in range(NQD):
                pw = qd_ % 2
                PQ = PA if pw == 0 else PB
                w_m = [xq_ready, bank_free[pw * 2], bank_free[pw * 2 + 1]]
                if qd_ == 0 and dbg_ffn_free[0]:
                    w_m.append(dbg_ffn_free[0])
                if ti == 0 and qd_ == 0:
                    w_m.append(w1q_ready)
                for j4 in range(4):
                    hb = qd_ * 4 + j4
                    for kb in range(FB):
                        o('T', lambda e, hb=hb, kb=kb, j4=j4, PQ=PQ: e.matmul(
                            PQ[:, j4 * ST:(j4 + 1) * ST],
                            w1q[:, kb * HID + hb * 128:
                                kb * HID + (hb + 1) * 128],
                            xqb[:, kb * ST:(kb + 1) * ST],
                            start=(j4 % 2 == 0 and kb == 0),
                            stop=(j4 % 2 == 1 and kb == FB - 1)),
                          waits=(w_m if (j4 == 0 and kb == 0) else []),
                          inc=(sT if kb == FB - 1 else None))
                ub = ubuf[pw]
                for j4 in range(4):
                    hb = qd_ * 4 + j4
                    o('S', lambda e, hb=hb, j4=j4, PQ=PQ, ub=ub: e.activation(
                        ub[:, j4 * ST:(j4 + 1) * ST], PQ[:, j4 * ST:(j4 + 1) * ST],
                        AF.Identity, bias=bias1[:, hb:hb + 1], scale=16.0),
                      waits=([(sT, sT.n), ub_free[pw]] if j4 == 0 else []),
                      inc=(sS if j4 == 3 else None))
                bank_free[pw * 2] = (sS, sS.n)
                bank_free[pw * 2 + 1] = (sS, sS.n)
                o('V', lambda e, ub=ub: e.tensor_copy(KI[:, 0:QW], ub),
                  waits=[(sS, sS.n)])
                o('V', lambda e: e.tensor_copy(t1b, KI[:, 0:QW]))
                o('V', lambda e, ub=ub: e.tensor_tensor(
                    t2b, ub, t1b, OP.subtract), inc=sV)
                ub_free[pw] = (sV, sV.n)
                o('V', lambda e: e.tensor_scalar(
                    KI[:, 0:QW], KI[:, 0:QW], 1, None, OP.bitwise_and))
                o('V', lambda e: e.tensor_copy(t3b, KI[:, 0:QW]))
                o('V', lambda e: e.tensor_scalar(
                    t3b, t3b, -1.0 / 128.0, 0.5 + 1.0 / 128.0,
                    OP.mult, OP.add))
                o('V', lambda e: e.tensor_tensor(t2b, t2b, t3b, OP.is_ge))
                o('V', lambda e: e.tensor_tensor(t1b, t1b, t2b, OP.add),
                  inc=sV)
                o('S', lambda e: e.activation(
                    cbuf, t1b, AF.Relu, bias=fc[:, 1:2], scale=1.0 / 16.0),
                  waits=[(sV, sV.n)], inc=sS)
                o('V', lambda e, qd_=qd_: e.tensor_scalar(
                    hqb[:, qd_ * QW:(qd_ + 1) * QW], cbuf, 127.0 / 16.0, None,
                    OP.min),
                  waits=[(sS, sS.n), hq_free if qd_ == 0 else None,
                         dbg_ffn_free[0] if qd_ == 0 else None], inc=sV)
            hq_ready = (sV, sV.n)
            if DBG:
                o('Q', lambda e, ti=ti: e.dma_start(dbg_hq[ti], hqb),
                  waits=[hq_ready, (qd, qd.n)], inc=qd, inc_k=16)
                o('Q', lambda e, ti=ti: e.dma_start(dbg_xq[ti], xqb),
                  waits=[(qd, qd.n)], inc=qd, inc_k=16)
                dbg_ffn_free[0] = (qd, qd.n)
            xq_free = (sT, sT.n)

            for db in range(FB):
                pw = db % 2
                gps = ps[4 + pw]
                w_m = [hq_ready, bank_free[4 + pw]]
                if ti == 0 and db == 0:
                    w_m.append(w2q_ready)
                for hb in range(HB):
                    o('T', lambda e, db=db, hb=hb, gps=gps: e.matmul(
                        gps[:, 0:ST],
                        w2q[:, hb * D + db * 128:hb * D + (db + 1) * 128],
                        hqb[:, hb * ST:(hb + 1) * ST],
                        start=(hb == 0), stop=(hb == HB - 1)),
                      waits=(w_m if hb == 0 else []),
                      inc=(sT if hb == HB - 1 else None))
                gi = gs_i
                o('S', lambda e, gps=gps, gi=gi, db=db: e.activation(
                    gs[gi], gps[:, 0:ST], AF.Silu, bias=b2q[:, db:db + 1],
                    scale=1.0), waits=[(sT, sT.n), gs_free[gi]], inc=sS)
                bank_free[4 + pw] = (sS, sS.n)
                o('Q', lambda e, s=s, t0=t0, db=db, gi=gi: e.dma_start(
                    g_scr[s, db * 128:(db + 1) * 128, t0:t0 + ST], gs[gi]),
                  waits=[(sS, sS.n)], inc=qg[gi], inc_k=16)
                gs_free[gi] = (qg[gi], qg[gi].n)
                gs_i = (gs_i + 1) % 4
            hq_free = (sT, sT.n)
            xnf_free = (sS, sS.n)

        a_done_T = (sT, sT.n)
        a_done_S = (sS, sS.n)
        a_done_V = (sV, sV.n)
        a_qdone = [(w.h if False else w, w.n) for w in (qn[0], qn[1], qg[0], qg[1], qg[2], qg[3])]

        wq_sb = AB[:, c_wq + 0 * FB * D:c_wq + 1 * FB * D]
        wk_sb = AB[:, c_wq + 1 * FB * D:c_wq + 2 * FB * D]
        wv_sb = AB[:, c_wq + 2 * FB * D:c_wq + 3 * FB * D]
        wo_sb = AB[:, c_wq + 3 * FB * D:c_wq + 4 * FB * D]
        for wsb, wdr in ((wq_sb, wqT), (wk_sb, wkT), (wv_sb, wvT),
                         (wo_sb, woT)):
            o('G', lambda e, wsb=wsb, wdr=wdr: e.dma_start(
                _fv(wsb, FB, D), wdr.rearrange("(f p) c -> p f c", p=128)),
              waits=[a_done_T, a_done_S, a_done_V] + a_qdone +
                    ([(qd, qd.n)] if DBG else []),
              inc=qa, inc_k=16)
        attw_ready = (qa, qa.n)

        xnc = AB[:, c_xnc:c_xnc + FB * QB]
        kt_sb = AB[:, c_kt:c_kt + NH * S]
        v_sb = AB[:, c_v:c_v + KT * D]
        qt_sb = [AB[:, c_qt:c_qt + NH * QB],
                 AB[:, c_qt + NH * QB:c_qt + 2 * NH * QB]]
        e_sb = [AB[:, c_e:c_e + 2 * QB], AB[:, c_e + 2 * QB:c_e + 4 * QB]]
        sqr_sb = AB[:, c_sqr:c_sqr + QB]
        u_all = AF4[:, f_uall:f_uall + 2 * HPG * QB]
        gs2 = [AF4[:, f_gs2 + i * QB:f_gs2 + (i + 1) * QB] for i in range(2)]
        xts2 = [AF4[:, f_xts2 + i * QB:f_xts2 + (i + 1) * QB] for i in range(2)]
        outst = [AF4[:, f_outst + i * QB:f_outst + (i + 1) * QB]
                 for i in range(2)]
        ebt = AF4[:, f_ebt:f_ebt + QB]

        st = {
            "xnc_load": None, "xnc_free": None,
            "qt_free": [None, None], "obf_free": [None, None],
            "obf_dump": [None, None],
            "e_free": [None, None],
            "U_free": None, "D_free": None, "M_free": None,
            "u_free": None, "sq_free": None,
            "gx_free": [None, None], "out_free": [None, None],
            "kv_ready": {},
        }

        def setup_chunk(sq_, c):
            if DBG:
                o('Q', lambda e, sq_=sq_, c=c: e.dma_start(
                    dbg_xc[sq_, c], xnc),
                  waits=[st["xnc_load"], (qd, qd.n)], inc=qd, inc_k=16)
            for hb in range(NH):
                pw = hb % 2
                kps = ps[pw * 2]
                for kb in range(FB):
                    o('T', lambda e, hb=hb, kb=kb, kps=kps: e.matmul(
                        kps[:, 0:QB],
                        wk_sb[:, kb * D + hb * 128:kb * D + (hb + 1) * 128],
                        xnc[:, kb * QB:(kb + 1) * QB],
                        start=(kb == 0), stop=(kb == FB - 1)),
                      waits=([st["xnc_load"], attw_ready, bank_free[pw * 2]]
                             if kb == 0 else []),
                      inc=(sT if kb == FB - 1 else None))
                o('S', lambda e, hb=hb, c=c, kps=kps: e.copy(
                    kt_sb[:, hb * S + c * QB:hb * S + (c + 1) * QB],
                    kps[:, 0:QB]), waits=[(sT, sT.n)], inc=sS)
                bank_free[pw * 2] = (sS, sS.n)
            for tt4 in range(TPQ):
                for half in range(NV):
                    pw = half % 2
                    vps = ps[pw * 2 + 1]
                    for kb in range(FB):
                        o('T', lambda e, tt4=tt4, half=half, kb=kb, vps=vps:
                          e.matmul(
                            vps[:, 0:VCW],
                            xnc[:, kb * QB + tt4 * 128:
                                kb * QB + tt4 * 128 + 128],
                            wv_sb[:, kb * D + half * VCW:
                                  kb * D + half * VCW + VCW],
                            start=(kb == 0), stop=(kb == FB - 1)),
                          waits=([st["xnc_load"], bank_free[pw * 2 + 1]]
                                 if kb == 0 else []),
                          inc=(sT if kb == FB - 1 else None))
                    o('S', lambda e, c=c, tt4=tt4, half=half, vps=vps: e.copy(
                        v_sb[:, (c * TPQ + tt4) * D + half * VCW:
                             (c * TPQ + tt4) * D + half * VCW + VCW],
                        vps[:, 0:VCW]), waits=[(sT, sT.n)], inc=sS)
                    bank_free[pw * 2 + 1] = (sS, sS.n)
            qi = c % 2
            for hb in range(NH):
                pw = hb % 2
                qps = ps[pw * 2]
                w0 = [st["xnc_load"], bank_free[pw * 2]]
                if hb == 0:
                    w0 += [st["qt_free"][qi], st["obf_free"][qi],
                           st["obf_dump"][qi]]
                for kb in range(FB):
                    o('T', lambda e, hb=hb, kb=kb, qps=qps: e.matmul(
                        qps[:, 0:QB],
                        wq_sb[:, kb * D + hb * 128:kb * D + (hb + 1) * 128],
                        xnc[:, kb * QB:(kb + 1) * QB],
                        start=(kb == 0), stop=(kb == FB - 1)),
                      waits=(w0 if kb == 0 else []),
                      inc=(sT if kb == FB - 1 else None))
                o('S', lambda e, hb=hb, qi=qi, qps=qps: e.copy(
                    qt_sb[qi][:, hb * QB:(hb + 1) * QB], qps[:, 0:QB]),
                  waits=[(sT, sT.n)], inc=sS)
                bank_free[pw * 2] = (sS, sS.n)
            st["xnc_free"] = (sT, sT.n)
            st["kv_ready"][c] = (sS, sS.n)

        if DBG:
            o('V', lambda e: e.memset(u_all, 0.0),
              waits=[a_done_V], inc=sV)
            o('V', lambda e: e.memset(AF4[0:8, f_d8:f_d8 + QB], 0.0), inc=sV)
            st["u_free"] = (sV, sV.n)
        for s in range(BPC):
            st["kv_ready"].clear()
            for fb in range(FB):
                o('Q', lambda e, s=s, fb=fb: e.dma_start(
                    xnc[:, fb * QB:(fb + 1) * QB],
                    xn_scr[s, fb * 128:(fb + 1) * 128, 0:QB]),
                  waits=(([st["xnc_free"], a_done_S] + a_qdone)
                         if fb == 0 else []), inc=qc, inc_k=16)
            st["xnc_load"] = (qc, qc.n)
            setup_chunk(s, 0)

            for qb in range(NQB):
                q0 = qb * QB
                nk = q0 // 128 + TPQ
                qi = qb % 2
                gx_load = [None, None]
                for pb in range(2):
                    o('Q', lambda e, s=s, q0=q0, pb=pb: e.dma_start(
                        gs2[pb], g_scr[s, pb * 128:(pb + 1) * 128, q0:q0 + QB]),
                      waits=[st["gx_free"][pb]] + a_qdone, inc=qe[pb], inc_k=16)
                    o('Q', lambda e, s=s, q0=q0, pb=pb: e.dma_start(
                        xts2[pb], xT[s, pb * 128:(pb + 1) * 128, q0:q0 + QB]),
                      inc=qe[pb], inc_k=16)
                    gx_load[pb] = (qe[pb], qe[pb].n)

                obf = qt_sb[qi]
                if DBG:
                    o('Q', lambda e, s=s, qb=qb, qi=qi: e.dma_start(
                        dbg_q[s, qb], qt_sb[qi]),
                      waits=[st["kv_ready"][qb], (qd, qd.n)], inc=qd, inc_k=16)
                    if qb == NQB - 1:
                        o('Q', lambda e, s=s: e.dma_start(dbg_k[s], kt_sb),
                          waits=[(qd, qd.n)], inc=qd, inc_k=16)
                        o('Q', lambda e, s=s: e.dma_start(dbg_v[s], v_sb),
                          waits=[(qd, qd.n)], inc=qd, inc_k=16)
                for g in range(NG):
                    for hl in range(HPG):
                        h = g * HPG + hl
                        for j in range(nk):
                            p = j % 2
                            w_s = [st["kv_ready"][qb], bank_free[p * 2]]
                            if j == 0:
                                w_s += ([st["U_free"], bank_free[4],
                                         bank_free[5]] +
                                        ([st["D_free"]] if hl == 0 else []))
                            o('T', lambda e, h=h, j=j, p=p, qi=qi: e.matmul(
                                ps[p * 2][:, 0:QB],
                                kt_sb[0:64, h * S + j * 128:h * S + j * 128 + 128],
                                qt_sb[qi][0:64, h * QB:(h + 1) * QB],
                                start=True, stop=True),
                              waits=w_s + [bank_free[p * 2 + 1]])
                            o('T', lambda e, h=h, j=j, p=p, qi=qi: e.matmul(
                                ps[p * 2 + 1][:, 0:QB],
                                kt_sb[64:128, h * S + j * 128:h * S + j * 128 + 128],
                                qt_sb[qi][64:128, h * QB:(h + 1) * QB],
                                start=True, stop=True), inc=sT)
                            scores_done = (sT, sT.n)
                            PSpair = PA if p == 0 else PB
                            o('S', lambda e, p=p, PSpair=PSpair: e.activation(
                                e_sb[p], PSpair, AF.Exp, scale=0.125),
                              waits=[scores_done, st["e_free"][p]], inc=sS)
                            bank_free[p * 2] = (sS, sS.n)
                            bank_free[p * 2 + 1] = (sS, sS.n)
                            e_ready = (sS, sS.n)
                            if j >= nk - TPQ:
                                soff = 384 + (q0 - j * 128)
                                o('V', lambda e, p=p, soff=soff:
                                  e.tensor_tensor(
                                    e_sb[p].rearrange("p (f c) -> p f c", f=2),
                                    e_sb[p].rearrange("p (f c) -> p f c", f=2),
                                    mask_sb[:, soff:soff + QB].unsqueeze(1)
                                    .to_broadcast([128, 2, QB]), OP.mult),
                                  waits=[e_ready], inc=sV)
                                e_ready = (sV, sV.n)
                            m0, m1 = HPG + hl, hl
                            o('T', lambda e, h=h, j=j, p=p, nk=nk: e.matmul(
                                psU0[:, 0:QB],
                                v_sb[:, j * D + h * 128:j * D + (h + 1) * 128],
                                e_sb[p][:, 0:QB],
                                start=(j == 0), stop=(j == nk - 1)),
                              waits=[e_ready])
                            o('T', lambda e, h=h, j=j, p=p, nk=nk: e.matmul(
                                psU1[:, 0:QB],
                                v_sb[:, j * D + h * 128:j * D + (h + 1) * 128],
                                e_sb[p][:, QB:2 * QB],
                                start=(j == 0), stop=(j == nk - 1)))
                            o('T', lambda e, j=j, p=p, m0=m0, hl=hl:
                              e.matmul(
                                psD[0:8, 0:QB], ind_sb[:, m0 * 8:m0 * 8 + 8],
                                e_sb[p][:, 0:QB],
                                start=(hl == 0 and j == 0), stop=False))
                            o('T', lambda e, j=j, p=p, m1=m1, hl=hl, nk=nk:
                              e.matmul(
                                psD[0:8, 0:QB], ind_sb[:, m1 * 8:m1 * 8 + 8],
                                e_sb[p][:, QB:2 * QB],
                                start=False,
                                stop=(hl == HPG - 1 and j == nk - 1)), inc=sT)
                            st["e_free"][p] = (sT, sT.n)
                        ucol = hl * 2
                        o('S', lambda e, ucol=ucol: e.copy(
                            u_all[:, ucol * QB:(ucol + 1) * QB], psU0[:, 0:QB]),
                          waits=[(sT, sT.n), st["u_free"]])
                        o('S', lambda e, ucol=ucol: e.copy(
                            u_all[:, (ucol + 1) * QB:(ucol + 2) * QB],
                            psU1[:, 0:QB]), inc=sS)
                        st["U_free"] = (sS, sS.n)
                    o('S', lambda e: e.copy(d8[0:2 * HPG, :],
                                            psD[0:2 * HPG, 0:QB]),
                      waits=[(sT, sT.n)], inc=sS)
                    st["D_free"] = (sS, sS.n)
                    if DBG:
                        o('Q', lambda e, s=s, qb=qb, g=g: e.dma_start(
                            dbg_u[s, qb, g], u_all),
                          waits=[(sS, sS.n), (qd, qd.n)], inc=qd, inc_k=16)
                        o('Q', lambda e, s=s, qb=qb, g=g: e.dma_start(
                            dbg_d[s, qb, g], d8), waits=[(qd, qd.n)],
                          inc=qd, inc_k=16)
                    dbg_done = (qd, qd.n)
                    o('V', lambda e: e.reciprocal(
                        al8[0:2 * HPG, :], d8[0:2 * HPG, :]),
                      waits=[(sS, sS.n)])
                    o('V', lambda e: e.tensor_scalar(
                        al8[0:HPG, :], al8[0:HPG, :],
                        lam_col[0:HPG, 0:1], None, OP.mult), inc=sV)
                    alpha_ready = (sV, sV.n)
                    for hl in range(HPG):
                        h = g * HPG + hl
                        ucol = hl * 2
                        o('T', lambda e, hl=hl: e.matmul(
                            ps[0][:, 0:QB],
                            sel_sb[0:2 * HPG,
                                   (HPG + hl) * 128:(HPG + hl + 1) * 128],
                            al8[0:2 * HPG, :], start=True, stop=True),
                          waits=[alpha_ready, bank_free[0], bank_free[1]])
                        o('T', lambda e, hl=hl: e.matmul(
                            ps[1][:, 0:QB],
                            sel_sb[0:2 * HPG, hl * 128:(hl + 1) * 128],
                            al8[0:2 * HPG, :], start=True, stop=True), inc=sT)
                        bc_a = (sT, sT.n)
                        o('V', lambda e, ucol=ucol: e.tensor_tensor(
                            ebt, u_all[:, (ucol + 1) * QB:(ucol + 2) * QB],
                            ps[1][:, 0:QB], OP.mult),
                          waits=[bc_a] + ([dbg_done] if DBG else []))
                        o('V', lambda e, ucol=ucol: e.tensor_tensor(
                            u_all[:, ucol * QB:(ucol + 1) * QB],
                            u_all[:, ucol * QB:(ucol + 1) * QB],
                            ps[0][:, 0:QB], OP.mult))
                        o('V', lambda e, ucol=ucol: e.tensor_tensor(
                            u_all[:, ucol * QB:(ucol + 1) * QB],
                            u_all[:, ucol * QB:(ucol + 1) * QB], ebt,
                            OP.subtract), inc=sV)
                        bank_free[0] = (sV, sV.n)
                        bank_free[1] = (sV, sV.n)
                        o_done = (sV, sV.n)
                        o('S', lambda e, ucol=ucol: e.activation(
                            sqr_sb, u_all[:, ucol * QB:(ucol + 1) * QB],
                            AF.Square), waits=[o_done, st["sq_free"]], inc=sS)
                        w_ms = [(sS, sS.n)]
                        if hl == 0:
                            w_ms.append(st["M_free"])
                        o('T', lambda e, hl=hl: e.matmul(
                            psM[0:8, 0:QB], ind_sb[:, hl * 8:hl * 8 + 8],
                            sqr_sb, start=(hl == 0), stop=(hl == HPG - 1)),
                          waits=w_ms, inc=sT)
                        st["sq_free"] = (sT, sT.n)
                    o('S', lambda e: e.copy(ms8[0:HPG, :], psM[0:HPG, 0:QB]),
                      waits=[(sT, sT.n)], inc=sS)
                    st["M_free"] = (sS, sS.n)
                    o('V', lambda e: e.tensor_scalar(
                        bt8[0:HPG, :], ms8[0:HPG, :], 1.0 / 128.0, float(EPS),
                        OP.mult, OP.add), waits=[(sS, sS.n)], inc=sV)
                    o('S', lambda e: e.activation(
                        bt8[0:HPG, :], bt8[0:HPG, :], AF.Sqrt),
                      waits=[(sV, sV.n)], inc=sS)
                    o('V', lambda e: e.reciprocal(bt8[0:HPG, :], bt8[0:HPG, :]),
                      waits=[(sS, sS.n)], inc=sV)
                    beta_ready = (sV, sV.n)
                    for hl in range(HPG):
                        h = g * HPG + hl
                        ucol = hl * 2
                        o('T', lambda e, hl=hl: e.matmul(
                            ps[2][:, 0:QB],
                            sel_sb[0:HPG, hl * 128:(hl + 1) * 128],
                            bt8[0:HPG, :], start=True, stop=True),
                          waits=[beta_ready, bank_free[2]], inc=sT)
                        bc_b = (sT, sT.n)
                        o('V', lambda e, h=h, ucol=ucol:
                          e.scalar_tensor_tensor(
                            obf[:, h * QB:(h + 1) * QB],
                            u_all[:, ucol * QB:(ucol + 1) * QB],
                            1.0 - LAMBDA_INIT, ps[2][:, 0:QB],
                            OP.mult, OP.mult), waits=[bc_b], inc=sV)
                        bank_free[2] = (sV, sV.n)
                    st["u_free"] = (sV, sV.n)
                obf_done = (sV, sV.n)
                if DBG:
                    o('Q', lambda e, s=s, qb=qb: e.dma_start(
                        dbg_o[s, qb], obf), waits=[obf_done, (qd, qd.n)],
                      inc=qd, inc_k=16)
                    st["obf_dump"][qi] = (qd, qd.n)
                st["qt_free"][qi] = (sT, sT.n)

                for db in range(FB):
                    wps = ps[4 + (db % 2)]
                    for kb in range(NH):
                        o('T', lambda e, db=db, kb=kb, wps=wps: e.matmul(
                            wps[:, 0:QB],
                            wo_sb[:, kb * D + db * 128:kb * D + (db + 1) * 128],
                            obf[:, kb * QB:(kb + 1) * QB],
                            start=(kb == 0), stop=(kb == NH - 1)),
                          waits=([obf_done, bank_free[4 + (db % 2)]]
                                 if kb == 0 else []),
                          inc=(sT if kb == NH - 1 else None))
                    wo_done = (sT, sT.n)
                    pb = db % 2
                    ob = outst[pb]
                    o('V', lambda e, wps=wps, pb=pb, ob=ob: e.tensor_tensor(
                        ob, wps[:, 0:QB], gs2[pb], OP.mult),
                      waits=[wo_done, gx_load[pb], st["out_free"][pb]])
                    o('V', lambda e, pb=pb, ob=ob: e.tensor_tensor(
                        ob, ob, xts2[pb], OP.add), inc=sV)
                    bank_free[4 + (db % 2)] = (sV, sV.n)
                    st["gx_free"][pb] = (sV, sV.n)
                    o('Q', lambda e, s=s, q0=q0, db=db, ob=ob: e.dma_start(
                        outT[s, db * 128:(db + 1) * 128, q0:q0 + QB], ob),
                      waits=[(sV, sV.n)], inc=qo[pb], inc_k=16)
                    st["out_free"][pb] = (qo[pb], qo[pb].n)
                    if db + 2 < FB:
                        o('Q', lambda e, s=s, q0=q0, db=db, pb=pb: e.dma_start(
                            gs2[pb], g_scr[s, (db + 2) * 128:(db + 3) * 128,
                                           q0:q0 + QB]),
                          waits=[(sV, sV.n)], inc=qe[pb], inc_k=16)
                        o('Q', lambda e, s=s, q0=q0, db=db, pb=pb: e.dma_start(
                            xts2[pb], xT[s, (db + 2) * 128:(db + 3) * 128,
                                         q0:q0 + QB]), inc=qe[pb], inc_k=16)
                        gx_load[pb] = (qe[pb], qe[pb].n)
                st["U_free"] = (sV, sV.n)
                st["obf_free"][qi] = (sT, sT.n)

                if qb + 1 < NQB:
                    for fb in range(FB):
                        o('Q', lambda e, s=s, qb=qb, fb=fb: e.dma_start(
                            xnc[:, fb * QB:(fb + 1) * QB],
                            xn_scr[s, fb * 128:(fb + 1) * 128,
                                   (qb + 1) * QB:(qb + 2) * QB]),
                          waits=([st["xnc_free"]] if fb == 0 else []),
                          inc=qc, inc_k=16)
                    st["xnc_load"] = (qc, qc.n)
                    setup_chunk(s, qb + 1)

        def _final_waits_q(e):
            for sm in (qx, qc, qn[0], qn[1], qg[0], qg[1], qg[2], qg[3],
                       qe[0], qe[1], qo[0], qo[1], qd):
                if sm.n:
                    e.wait_ge(sm.h, sm.n)
            return None
        def _final_waits_g(e):
            for sm in (qk, qw[0], qw[1], qa):
                if sm.n:
                    e.wait_ge(sm.h, sm.n)
            return None
        o('Q', _final_waits_q)
        o('G', _final_waits_g)

    sems = [sT, sS, sV, qx, qc, qk, qa, qd] + qn + qg + qe + qo + qw
    with nc.Block() as blk:
        def _run(name):
            def body(eng):
                for sm in sems:
                    sm.n = 0
                ctx.begin(name, eng)
                schedule()
            return body
        blk.sync(_run('Q'))
        blk.gpsimd(_run('G'))
        blk.tensor(_run('T'))
        blk.scalar(_run('S'))
        blk.vector(_run('V'))
    return nc


def _host_constants():
    import ml_dtypes
    mask = (np.arange(896)[None, :] - 384 >= np.arange(128)[:, None])
    mask = mask.astype(ml_dtypes.bfloat16)
    ind = np.zeros((128, 64), dtype=np.float32)
    for m in range(8):
        ind[:, m * 8 + m] = 1.0
    sel = np.zeros((8, 8 * 128), dtype=np.float32)
    for m in range(8):
        sel[m, m * 128:(m + 1) * 128] = 1.0
    return mask, ind.astype(ml_dtypes.bfloat16), sel


def kernel(**inputs):
    cfg = CFG
    BPC = cfg["BPC"]
    x = np.ascontiguousarray(np.asarray(inputs["L0_attn_x"], dtype=np.float32))
    ncores = x.shape[0] // BPC
    mask, ind, sel = _host_constants()
    shared = {
        "wqT": np.ascontiguousarray(np.asarray(inputs["L0_attn_Wq"], np.float32).T),
        "wkT": np.ascontiguousarray(np.asarray(inputs["L0_attn_Wk"], np.float32).T),
        "wvT": np.ascontiguousarray(np.asarray(inputs["L0_attn_Wv"], np.float32).T),
        "woT": np.ascontiguousarray(np.asarray(inputs["L0_attn_Wo"], np.float32).T),
        "w1T": np.ascontiguousarray(np.asarray(inputs["L0_ffn_w1"], np.float32).T),
        "w2T": np.ascontiguousarray(np.asarray(inputs["L0_ffn_w2"], np.float32).T),
        "b1": np.ascontiguousarray(np.asarray(inputs["L0_ffn_b1"], np.float32)),
        "b2": np.ascontiguousarray(np.asarray(inputs["L0_ffn_b2"], np.float32)),
        "lqk": np.ascontiguousarray(np.concatenate([
            np.asarray(inputs["L0_attn_lq1"], np.float32),
            np.asarray(inputs["L0_attn_lk1"], np.float32),
            np.asarray(inputs["L0_attn_lq2"], np.float32),
            np.asarray(inputs["L0_attn_lk2"], np.float32)])[None, :]),
        "mask": mask, "ind": ind, "sel": sel,
    }
    nc = build_nc(cfg)
    in_maps = []
    for c in range(ncores):
        xs = x[c * BPC:(c + 1) * BPC]
        m = dict(shared)
        m["xT"] = np.ascontiguousarray(xs.transpose(0, 2, 1))
        in_maps.append(m)
    res = run_bass_kernel_spmd(nc, in_maps, core_ids=list(range(ncores)))
    outs = [np.asarray(r["outT"]).transpose(0, 2, 1) for r in res.results]
    return np.ascontiguousarray(np.concatenate(outs, axis=0)).astype(np.float32)


# version 06
# speedup vs baseline: 1.2319x; 1.2319x over previous
import math
import numpy as np

import concourse.bass as bass
import concourse.mybir as mybir
from concourse.bass_utils import run_bass_kernel_spmd

F32 = mybir.dt.float32
BF16 = mybir.dt.bfloat16
I32 = mybir.dt.int32
AF = mybir.ActivationFunctionType
OP = mybir.AluOpType
AX = mybir.AxisListType

LAMBDA_INIT = 0.8 - 0.6 * math.exp(-0.3 * 6)
EPS = 1e-5
MAGIC = 12582912.0
MASKC = -400.0

CFG = dict(D=1024, S=2048, HID=4096, BPC=2, ST=256, QB=512)


class Sem:
    __slots__ = ("h", "n", "name")

    def __init__(self, nc, name):
        self.h = nc.alloc_semaphore(name)
        self.n = 0
        self.name = name


class Ctx:

    def __init__(self):
        self.active = None
        self.eng = None
        self._waited = {}
        self._need_drain = False

    def begin(self, name, eng):
        self.active = name
        self.eng = eng
        self._waited = {}
        self._need_drain = False

    def op(self, engine, emit, waits=(), inc=None, inc_k=1):
        if engine == self.active:
            if self.active in ('V', 'S') and self._need_drain:
                self.eng.drain()
                self._need_drain = False
            for w in waits:
                if w is None:
                    continue
                sem, val = w
                assert val <= sem.n, (
                    f"wait-ahead on {sem.name}: {val} > {sem.n}")
                if val <= 0 or self._waited.get(sem.name, -1) >= val:
                    continue
                self.eng.wait_ge(sem.h, val)
                self._waited[sem.name] = val
            inst = emit(self.eng)
            self._need_drain = True
            if inc is not None:
                if isinstance(inst, list):
                    inst = inst[-1]
                inst.then_inc(inc.h, inc_k)
        if inc is not None:
            inc.n += inc_k


def _fv(ap2d, nblk, blk_cols):
    return ap2d.rearrange("p (f t) -> p f t", f=nblk, t=blk_cols)


def build_nc(cfg):
    D, S, HID, BPC = cfg["D"], cfg["S"], cfg["HID"], cfg["BPC"]
    ST, QB = cfg["ST"], cfg["QB"]
    FB = D // 128
    HB = HID // 128
    NH = D // 128
    NQB = S // QB
    KT = S // 128
    TPQ = QB // 128
    HPG = min(4, NH)
    NG = NH // HPG
    VCW = min(512, D)
    NV = D // VCW
    assert 2 * HPG <= 8 and NH <= 8 and S % QB == 0 and D % VCW == 0

    nc = bass.Bass("TRN2", target_bir_lowering=False, debug=False)

    xT = nc.dram_tensor("xT", [BPC, D, S], F32, kind="ExternalInput").ap()
    wqT = nc.dram_tensor("wqT", [D, D], F32, kind="ExternalInput").ap()
    wkT = nc.dram_tensor("wkT", [D, D], F32, kind="ExternalInput").ap()
    wvT = nc.dram_tensor("wvT", [D, D], F32, kind="ExternalInput").ap()
    woT = nc.dram_tensor("woT", [D, D], F32, kind="ExternalInput").ap()
    w1T = nc.dram_tensor("w1T", [D, HID], F32, kind="ExternalInput").ap()
    w2T = nc.dram_tensor("w2T", [HID, D], F32, kind="ExternalInput").ap()
    b1d = nc.dram_tensor("b1", [HID], F32, kind="ExternalInput").ap()
    b2d = nc.dram_tensor("b2", [D], F32, kind="ExternalInput").ap()
    lqkd = nc.dram_tensor("lqk", [1, 256], F32, kind="ExternalInput").ap()
    maskd = nc.dram_tensor("mask", [128, 896], BF16, kind="ExternalInput").ap()
    trild = nc.dram_tensor("tril", [128, 128], BF16, kind="ExternalInput").ap()
    indd = nc.dram_tensor("ind", [128, 64], BF16, kind="ExternalInput").ap()
    seld = nc.dram_tensor("sel", [8, 8 * 128], F32, kind="ExternalInput").ap()

    outT = nc.dram_tensor("outT", [BPC, D, S], F32, kind="ExternalOutput").ap()
    g_scr = nc.dram_tensor("g_scr", [BPC, D, S], F32, kind="Internal").ap()
    xn_scr = nc.dram_tensor("xn_scr", [BPC, D, S], BF16, kind="Internal").ap()
    b1_scr = nc.dram_tensor("b1_scr", [HID], BF16, kind="Internal").ap()
    b2_scr = nc.dram_tensor("b2_scr", [D], BF16, kind="Internal").ap()
    DBG = bool(cfg.get("debug"))
    if DBG:
        NQB_ = S // QB
        NG_ = (D // 128) // min(4, D // 128)
        HPG_ = min(4, D // 128)
        dbg_u = nc.dram_tensor("dbg_u", [BPC, NQB_, NG_, 128, 2 * HPG_ * QB],
                               F32, kind="ExternalOutput").ap()
        dbg_d = nc.dram_tensor("dbg_d", [BPC, NQB_, NG_, 8, QB], F32,
                               kind="ExternalOutput").ap()
        dbg_o = nc.dram_tensor("dbg_o", [BPC, NQB_, 128, (D // 128) * QB],
                               BF16, kind="ExternalOutput").ap()
        dbg_k = nc.dram_tensor("dbg_k", [BPC, 128, (D // 128) * S], BF16,
                               kind="ExternalOutput").ap()
        dbg_v = nc.dram_tensor("dbg_v", [BPC, 128, (S // 128) * D], BF16,
                               kind="ExternalOutput").ap()
        dbg_q = nc.dram_tensor("dbg_q", [BPC, NQB_, 128, (D // 128) * QB],
                               BF16, kind="ExternalOutput").ap()
        dbg_xc = nc.dram_tensor("dbg_xc", [BPC, NQB_, 128, (D // 128) * QB],
                                BF16, kind="ExternalOutput").ap()
        NT_ = (BPC * S) // ST
        dbg_hq = nc.dram_tensor("dbg_hq", [NT_, 128, (HID // 128) * ST],
                                BF16, kind="ExternalOutput").ap()
        dbg_xq = nc.dram_tensor("dbg_xq", [NT_, 128, (D // 128) * ST],
                                BF16, kind="ExternalOutput").ap()
        dbg_ub = nc.dram_tensor("dbg_ub", [4, 128, ST], F32,
                                kind="ExternalOutput").ap()

    c_w1q = 0
    c_w2q = c_w1q + FB * HID
    c_hqb = c_w2q + HB * D
    c_xqb = c_hqb + HB * ST
    c_xnst = c_xqb + FB * ST
    a_cols = c_xnst + 8 * ST

    c_wq = 0
    c_xnc = c_wq + 4 * FB * D
    c_kt = c_xnc + FB * QB
    c_v = c_kt + NH * S
    c_qt = c_v + KT * D
    c_e = c_qt + 2 * NH * QB
    c_sqr = c_e + 4 * QB
    b_cols = c_sqr + QB
    BFC = max(a_cols, b_cols)

    QW = min(4 * ST, 1024)
    f_xts = 0
    f_c = f_xts + FB * ST
    f_u = f_c + QW
    f_gs = f_u + 2 * QW
    f_sq = f_gs + 4 * ST
    f_mbc = f_sq + 4 * ST
    f_wst = f_mbc + 2 * ST
    a_fcols = f_wst + 1024

    f_uall = 0
    f_gs2 = f_uall + 2 * HPG * QB
    f_xts2 = f_gs2 + 2 * QB
    f_outst = f_xts2 + 2 * QB
    f_d8 = f_outst + 2 * QB
    f_al8 = f_d8 + QB
    f_ms8 = f_al8 + QB
    f_bt8 = f_ms8 + QB
    b_fcols = f_bt8 + QB
    f_mv = a_fcols
    f_lnt = f_mv + ST
    a_fcols = f_mv + 3 * ST
    F32C = max(a_fcols, b_fcols)

    AB = nc.alloc_sbuf_tensor("AB", [128, BFC], BF16).ap()
    AF4 = nc.alloc_sbuf_tensor("AF4", [128, F32C], F32).ap()
    KI = AB[:, c_e:c_e + 2048].bitcast(I32)

    b1h = (HID // 2 + 63) // 64 * 64
    b2h = (D // 2 + 63) // 64 * 64
    scb_w = 896 + 128 + 64 + 16 + ST + 32 + 16 + b1h + b2h
    SCB = nc.alloc_sbuf_tensor("SCB", [128, scb_w], BF16).ap()
    _o = [0]

    def _scb(w):
        a = _o[0]
        _o[0] += w
        return SCB[:, a:a + w]
    mask_sb = _scb(896)
    tril_sb = _scb(128)
    ind_sb = _scb(64)
    ones_bf = _scb(16)[:, 0:1]
    ones_row = _scb(ST)
    b1qb = _scb(32)[:, 0:HB]
    b2qb = _scb(16)[:, 0:FB]
    b1r2 = _scb(b1h)
    b2r2 = _scb(b2h)
    scf_w = 1024 + 16 + 128 + 32 + 16 + 16 + 16 + 16
    SCF = nc.alloc_sbuf_tensor("SCF", [128, scf_w], F32).ap()
    _o[0] = 0

    def _scf(w):
        a = _o[0]
        _o[0] += w
        return SCF[:, a:a + w]
    sel_sb = _scf(1024)[0:8, :]
    ones_cf = _scf(16)[:, 0:1]
    ones1f = _scf(128)[0:1, :]
    b1q = _scf(32)[:, 0:HB]
    b2q = _scf(16)[:, 0:FB]
    lam_sb = _scf(16)[0:1, 0:8]
    lam_col = _scf(16)[:, 0:1]
    fc = _scf(16)[:, 0:4]
    lam_in = AF4[0:1, f_mv:f_mv + 256]
    d8 = AF4[0:8, f_d8:f_d8 + QB]
    al8 = AF4[0:8, f_al8:f_al8 + QB]
    ms8 = AF4[0:8, f_ms8:f_ms8 + QB]
    bt8 = AF4[0:8, f_bt8:f_bt8 + QB]
    mv_sb = AF4[0:1, f_mv:f_mv + 3 * ST]
    lnt = AF4[0:1, f_lnt:f_lnt + ST]

    PA = nc.alloc_psum_tensor("PA", [128, 1024], F32).ap()
    PB = nc.alloc_psum_tensor("PB", [128, 1024], F32).ap()
    PC = nc.alloc_psum_tensor("PC", [128, 1024], F32).ap()
    ps = [PA[:, 0:512], PA[:, 512:1024], PB[:, 0:512], PB[:, 512:1024],
          PC[:, 0:512], PC[:, 512:1024]]
    ps += [nc.alloc_psum_tensor(f"ps{i}", [128, 512], F32).ap()
           for i in range(6, 8)]
    psU0, psU1, psD, psM = ps[4], ps[5], ps[6], ps[7]

    sT = Sem(nc, "sT")
    sS = Sem(nc, "sS")
    sV = Sem(nc, "sV")
    qx = Sem(nc, "qx")
    qn = [Sem(nc, "qn0"), Sem(nc, "qn1")]
    qg = [Sem(nc, f"qg{i}") for i in range(4)]
    qc = Sem(nc, "qc")
    qe = [Sem(nc, "qe0"), Sem(nc, "qe1")]
    qo = [Sem(nc, "qo0"), Sem(nc, "qo1")]
    qk = Sem(nc, "qk")
    qw = [Sem(nc, "qw0"), Sem(nc, "qw1")]
    qa = Sem(nc, "qa")
    qd = Sem(nc, "qd")

    ctx = Ctx()

    bank_free = [None] * 8

    def schedule():
        o = ctx.op
        for i in range(8):
            bank_free[i] = None

        o('G', lambda e: e.dma_start(mask_sb, maskd), inc=qk, inc_k=16)
        o('G', lambda e: e.dma_start(tril_sb, trild), inc=qk, inc_k=16)
        o('G', lambda e: e.dma_start(ind_sb, indd), inc=qk, inc_k=16)
        o('G', lambda e: e.dma_start(sel_sb, seld), inc=qk, inc_k=16)
        o('G', lambda e: e.dma_start(lam_in, lqkd), inc=qk, inc_k=16)
        with nc.allow_non_contiguous_dma("tiny bias loads"):
            o('G', lambda e: e.dma_start(
                b1q, b1d.rearrange("(c p) -> p c", p=128)), inc=qk, inc_k=16)
            o('G', lambda e: e.dma_start(
                b2q, b2d.rearrange("(c p) -> p c", p=128)), inc=qk, inc_k=16)
        const_loaded = (qk, qk.n)
        o('V', lambda e: e.memset(ones_bf, 1.0))
        o('V', lambda e: e.memset(ones_cf, 1.0))
        o('V', lambda e: e.memset(ones_row, 1.0))
        o('V', lambda e: e.memset(fc[:, 0:1], float(EPS)))
        o('V', lambda e: e.memset(fc[:, 1:2], -32.0))
        o('V', lambda e: e.memset(fc[:, 2:3], 512.0 + MAGIC))
        o('V', lambda e: e.memset(ones1f, 1.0), inc=sV)

        def quantw(dst, src, n):
            ii = KI[:, 0:n]
            o('V', lambda e: e.tensor_scalar(
                src, src, 64.0, MAGIC, OP.mult, OP.add))
            o('V', lambda e: e.tensor_scalar(src, src, -MAGIC, None, OP.add))
            o('V', lambda e: e.tensor_copy(ii, src))
            o('V', lambda e: e.tensor_copy(src, ii))
            o('V', lambda e: e.tensor_scalar(
                src, src, 1.0 / 64.0, -2.0, OP.mult, OP.max))
            o('V', lambda e: e.tensor_scalar(
                dst, src, 127.0 / 64.0, None, OP.min), inc=sV)

        o('V', lambda e: e.engine_nop(), waits=[const_loaded])
        quantw(b1qb, b1q, HB)
        quantw(b2qb, b2q, FB)
        HBH = HB // 2
        FBH = FB // 2

        def b1row(hb):
            g = 64 * (hb // HBH)
            c = (hb % HBH) * 128
            return b1r2[g:g + 1, c:c + 128]

        def b2row(db):
            g = 64 * (db // FBH)
            c = (db % FBH) * 128
            return b2r2[g:g + 1, c:c + 128]

        with nc.allow_non_contiguous_dma("bias row gathers"):
            o('G', lambda e: e.dma_start(
                b1_scr.rearrange("(c p) -> p c", p=128), b1qb),
              waits=[(sV, sV.n)], inc=qk, inc_k=16)
            o('G', lambda e: e.dma_start(
                b2_scr.rearrange("(c p) -> p c", p=128), b2qb),
              inc=qk, inc_k=16)
            b1v = b1_scr.rearrange("(a b) -> a b", a=2)
            b2v = b2_scr.rearrange("(a b) -> a b", a=2)
            o('G', lambda e: e.dma_start(b1r2[0:1, 0:HID // 2], b1v[0:1, :]),
              waits=[(qk, qk.n)], inc=qk, inc_k=16)
            o('G', lambda e: e.dma_start(b1r2[64:65, 0:HID // 2], b1v[1:2, :]),
              inc=qk, inc_k=16)
            o('G', lambda e: e.dma_start(b2r2[0:1, 0:D // 2], b2v[0:1, :]),
              inc=qk, inc_k=16)
            o('G', lambda e: e.dma_start(b2r2[64:65, 0:D // 2], b2v[1:2, :]),
              inc=qk, inc_k=16)
        brows_ready = (qk, qk.n)

        o('V', lambda e: e.tensor_tensor(
            lam_in[:, 0:64], lam_in[:, 0:64], lam_in[:, 64:128], OP.mult))
        o('V', lambda e: e.tensor_tensor(
            lam_in[:, 128:192], lam_in[:, 128:192], lam_in[:, 192:256], OP.mult))
        o('V', lambda e: e.reduce_sum(lam_sb[:, 0:1], lam_in[:, 0:64], axis=AX.X))
        o('V', lambda e: e.reduce_sum(lam_sb[:, 1:2], lam_in[:, 128:192], axis=AX.X),
          inc=sV)
        o('S', lambda e: e.activation(lam_sb[:, 2:3], lam_sb[:, 0:1], AF.Exp),
          waits=[(sV, sV.n)])
        o('S', lambda e: e.activation(lam_sb[:, 3:4], lam_sb[:, 1:2], AF.Exp),
          inc=sS)
        o('V', lambda e: e.tensor_tensor(
            lam_sb[:, 4:5], lam_sb[:, 2:3], lam_sb[:, 3:4], OP.subtract),
          waits=[(sS, sS.n)])
        o('V', lambda e: e.tensor_scalar(
            lam_sb[:, 4:5], lam_sb[:, 4:5], LAMBDA_INIT, None, OP.add), inc=sV)
        o('T', lambda e: e.matmul(ps[0][:, 0:1], ones1f, lam_sb[:, 4:5],
                                  start=True, stop=True),
          waits=[(sV, sV.n)], inc=sT)
        o('S', lambda e: e.copy(lam_col, ps[0][:, 0:1]),
          waits=[(sT, sT.n)], inc=sS)
        bank_free[0] = (sS, sS.n)

        w1q = AB[:, c_w1q:c_w1q + FB * HID]
        w2q = AB[:, c_w2q:c_w2q + HB * D]
        CW1 = min(1024, HID)
        assert D <= 1024
        wst = [AF4[:, f_wst:f_wst + 1024],
               AF4[:, f_gs:f_gs + 1024]]
        wst_free = [None, None]
        wjobs = ([("w1", kb, cc) for cc in range(HID // CW1)
                  for kb in range(FB)] +
                 [("w2", hb, 0) for hb in range(HB)])
        w1cc_ready = {}
        w2_ready = {}
        wstate = {"i": 0}

        def emit_wjobs(k):
            for _ in range(k):
                i = wstate["i"]
                if i >= len(wjobs):
                    return
                wstate["i"] = i + 1
                w, b, cc = wjobs[i]
                sb = i % 2
                if w == "w1":
                    nw = CW1
                    src = w1T[b * 128:(b + 1) * 128, cc * CW1:(cc + 1) * CW1]
                    dst = w1q[:, b * HID + cc * CW1:b * HID + (cc + 1) * CW1]
                else:
                    nw = D
                    src = w2T[b * 128:(b + 1) * 128, 0:D]
                    dst = w2q[:, b * D:(b + 1) * D]
                o('G', lambda e, src=src, d=wst[sb], nw=nw: e.dma_start(
                    d[:, 0:nw], src),
                  waits=[wst_free[sb]], inc=qw[sb], inc_k=16)
                o('V', lambda e: e.engine_nop(), waits=[(qw[sb], qw[sb].n)])
                quantw(dst, wst[sb][:, 0:nw], nw)
                wst_free[sb] = (sV, sV.n)
                if w == "w1" and b == FB - 1:
                    w1cc_ready[cc] = (sV, sV.n)
                elif w == "w2":
                    w2_ready[b] = (sV, sV.n)

        emit_wjobs(FB)

        xts = AF4[:, f_xts:f_xts + FB * ST]
        cbuf = AF4[:, f_c:f_c + QW]
        ubuf = [AF4[:, f_u:f_u + QW], AF4[:, f_u + QW:f_u + 2 * QW]]
        t1b = ubuf[1]
        t2b = ubuf[0]
        gsA = AF4[:, f_gs:f_gs + 4 * ST]
        gsB = cbuf
        xqb = AB[:, c_xqb:c_xqb + FB * ST]
        hqb = AB[:, c_hqb:c_hqb + HB * ST]
        xnst = [AB[:, c_xnst:c_xnst + 4 * ST],
                AB[:, c_xnst + 4 * ST:c_xnst + 8 * ST]]
        sqfb = [AF4[:, f_sq:f_sq + 2 * ST], AF4[:, f_sq + 2 * ST:f_sq + 4 * ST]]
        m_bc = AF4[:, f_mbc:f_mbc + ST]
        r_bc = AF4[:, f_mbc + ST:f_mbc + 2 * ST]

        WQF = min(4, FB)
        def bc4(ap):
            return ap.unsqueeze(1).to_broadcast([128, WQF, ST])

        def pv4(ap2d):
            return ap2d.rearrange("p (f c) -> p f c", f=WQF, c=ST)

        gs_free = [None, None]
        cb_free = None
        dbg_ffn_free = [None]
        xnst_free = [None, None]
        ub_free = [None, None]
        xq_free = None
        hq_free = None
        xnf_free = None
        stats_bank_free = None
        xnst_i = 0
        NQD = HB // 4

        supertiles = [(s, t0) for s in range(BPC) for t0 in range(0, S, ST)]
        o('Q', lambda e: e.dma_start(
            _fv(xts, FB, ST),
            xT[supertiles[0][0], :, supertiles[0][1]:supertiles[0][1] + ST]
            .rearrange("(f p) t -> p f t", p=128)), inc=qx, inc_k=16)
        xts_load = (qx, qx.n)
        sq_mm_done = [None, None]

        for ti, (s, t0) in enumerate(supertiles):
            for fp in range(FB // 2):
                sqi = fp % 2
                o('S', lambda e, fp=fp, sqi=sqi: e.activation(
                    sqfb[sqi], xts[:, fp * 2 * ST:(fp + 1) * 2 * ST],
                    AF.Square), waits=[xts_load, sq_mm_done[sqi]], inc=sS)
                w_t = [xts_load]
                if fp == 0:
                    w_t += [stats_bank_free, bank_free[6], bank_free[7]]
                sq_done = (sS, sS.n)
                for h2 in range(2):
                    fb = fp * 2 + h2
                    o('T', lambda e, fb=fb: e.matmul(
                        ps[6][0:1, 0:ST], ones_cf,
                        xts[:, fb * ST:(fb + 1) * ST],
                        start=(fb == 0), stop=(fb == FB - 1)),
                      waits=(w_t if fb == 0 else []))
                    o('T', lambda e, fb=fb, sqi=sqi, h2=h2: e.matmul(
                        ps[7][0:1, 0:ST], ones_cf,
                        sqfb[sqi][:, h2 * ST:(h2 + 1) * ST],
                        start=(fb == 0), stop=(fb == FB - 1)),
                      waits=([sq_done] if h2 == 0 else []), inc=sT)
                sq_mm_done[sqi] = (sT, sT.n)
            m_row = mv_sb[:, 0:ST]
            r_row = mv_sb[:, ST:2 * ST]
            v_row = mv_sb[:, 2 * ST:3 * ST]
            o('V', lambda e: e.tensor_scalar(
                m_row, ps[6][0:1, 0:ST], 1.0 / D, None, OP.mult),
              waits=[(sT, sT.n)])
            o('V', lambda e: e.tensor_tensor(lnt, m_row, m_row, OP.mult))
            o('V', lambda e: e.scalar_tensor_tensor(
                v_row, ps[7][0:1, 0:ST], 1.0 / D, lnt, OP.mult, OP.subtract),
              inc=sV)
            o('S', lambda e: e.activation(
                r_row, v_row, AF.Sqrt, bias=fc[0:1, 0:1], scale=1.0),
              waits=[(sV, sV.n)], inc=sS)
            o('V', lambda e: e.reciprocal(r_row, r_row),
              waits=[(sS, sS.n)], inc=sV)
            o('T', lambda e: e.matmul(ps[6][:, 0:ST], ones1f, m_row,
                                      start=True, stop=True),
              waits=[(sV, sV.n)])
            o('T', lambda e: e.matmul(ps[7][:, 0:ST], ones1f, r_row,
                                      start=True, stop=True), inc=sT)
            o('S', lambda e: e.copy(m_bc, ps[6][:, 0:ST]),
              waits=[(sT, sT.n), xq_free])
            o('S', lambda e: e.copy(r_bc, ps[7][:, 0:ST]), inc=sS)
            stats_bank_free = (sS, sS.n)
            bcast_done = (sS, sS.n)
            bank_free[6] = (sS, sS.n)
            bank_free[7] = (sS, sS.n)
            W4 = WQF * ST
            for fp in range(FB // WQF):
                fb0 = fp * WQF
                xsl = xts[:, fb0 * ST:fb0 * ST + W4]
                o('V', lambda e, xsl=xsl: e.tensor_tensor(
                    pv4(cbuf[:, 0:W4]), pv4(xsl), bc4(m_bc), OP.subtract),
                  waits=[bcast_done, cb_free,
                         xnf_free if fp == 0 else None])
                o('V', lambda e: e.tensor_tensor(
                    pv4(cbuf[:, 0:W4]), pv4(cbuf[:, 0:W4]), bc4(r_bc),
                    OP.mult))
                xi = xnst_i
                o('V', lambda e, xi=xi: e.tensor_copy(
                    xnst[xi][:, 0:W4], cbuf[:, 0:W4]),
                  waits=[xnst_free[xi]], inc=sV)
                o('Q', lambda e, s=s, t0=t0, fb0=fb0, xi=xi: e.dma_start(
                    xn_scr[s, fb0 * 128:(fb0 + WQF) * 128, t0:t0 + ST]
                    .rearrange("(f p) t -> p f t", p=128),
                    pv4(xnst[xi][:, 0:W4])),
                  waits=[(sV, sV.n)], inc=qn[xi], inc_k=16)
                xnst_free[xi] = (qn[xi], qn[xi].n)
                xnst_i ^= 1
                w_q = [xq_free, dbg_ffn_free[0]] if fp == 0 else []
                o('V', lambda e: e.tensor_scalar(
                    t1b[:, 0:W4], cbuf[:, 0:W4], 16.0, 512.0,
                    OP.mult, OP.add),
                  waits=w_q)
                o('V', lambda e: e.tensor_copy(KI[:, 0:W4], t1b[:, 0:W4]))
                o('V', lambda e: e.tensor_copy(t2b[:, 0:W4], KI[:, 0:W4]))
                o('V', lambda e: e.tensor_tensor(
                    t1b[:, 0:W4], t1b[:, 0:W4], t2b[:, 0:W4], OP.subtract))
                o('V', lambda e: e.tensor_scalar(
                    t1b[:, 0:W4], t1b[:, 0:W4], 0.5, None, OP.is_ge))
                o('V', lambda e: e.tensor_tensor(
                    t1b[:, 0:W4], t1b[:, 0:W4], t2b[:, 0:W4], OP.add))
                o('V', lambda e: e.tensor_scalar(
                    t1b[:, 0:W4], t1b[:, 0:W4], -512.0, 1.0 / 16.0,
                    OP.add, OP.mult))
                o('V', lambda e, fb0=fb0: e.tensor_scalar(
                    xqb[:, fb0 * ST:fb0 * ST + W4], t1b[:, 0:W4],
                    -8.0, 127.0 / 16.0, OP.max, OP.min), inc=sV)
            xq_ready = (sV, sV.n)
            xts_free = (sV, sV.n)
            if ti + 1 < len(supertiles):
                s2, t02 = supertiles[ti + 1]
                o('Q', lambda e, s2=s2, t02=t02: e.dma_start(
                    _fv(xts, FB, ST),
                    xT[s2, :, t02:t02 + ST].rearrange("(f p) t -> p f t", p=128)),
                  waits=[xts_free], inc=qx, inc_k=16)
                xts_load = (qx, qx.n)

            for qd_ in range(NQD):
                pw = qd_ % 2
                PQ = PA if pw == 0 else PB
                w_m = [xq_ready, bank_free[pw * 2], bank_free[pw * 2 + 1]]
                if qd_ == 0 and dbg_ffn_free[0]:
                    w_m.append(dbg_ffn_free[0])
                if ti == 0:
                    w_m += [w1cc_ready[((qd_ + 1) * 512 - 1) // CW1],
                            brows_ready]
                for j4 in range(4):
                    hb = qd_ * 4 + j4
                    for kb in range(FB):
                        o('T', lambda e, hb=hb, kb=kb, j4=j4, PQ=PQ: e.matmul(
                            PQ[:, j4 * ST:(j4 + 1) * ST],
                            w1q[:, kb * HID + hb * 128:
                                kb * HID + (hb + 1) * 128],
                            xqb[:, kb * ST:(kb + 1) * ST],
                            start=(j4 % 2 == 0 and kb == 0), stop=False),
                          waits=(w_m if (j4 == 0 and kb == 0) else []))
                    g1 = 64 * (hb // HBH)
                    o('T', lambda e, hb=hb, j4=j4, PQ=PQ, g1=g1: e.matmul(
                        PQ[:, j4 * ST:(j4 + 1) * ST],
                        b1row(hb), ones_row[g1:g1 + 1, 0:ST],
                        start=False, stop=(j4 % 2 == 1)), inc=sT)
                ub = ubuf[pw]
                o('S', lambda e, PQ=PQ, ub=ub: e.activation(
                    ub, PQ, AF.Identity, bias=fc[:, 2:3], scale=16.0),
                  waits=[(sT, sT.n), ub_free[pw]], inc=sS)
                bank_free[pw * 2] = (sS, sS.n)
                bank_free[pw * 2 + 1] = (sS, sS.n)
                o('V', lambda e, ub=ub: e.tensor_scalar(
                    ub, ub, -MAGIC, None, OP.add), waits=[(sS, sS.n)])
                o('V', lambda e, ub=ub: e.tensor_copy(KI[:, 0:QW], ub),
                  inc=sV)
                ub_free[pw] = (sV, sV.n)
                o('V', lambda e: e.tensor_copy(t1b, KI[:, 0:QW]), inc=sV)
                o('S', lambda e: e.activation(
                    cbuf, t1b, AF.Relu, bias=fc[:, 1:2], scale=1.0 / 16.0),
                  waits=[(sV, sV.n), cb_free if qd_ == 0 else None], inc=sS)
                o('V', lambda e, qd_=qd_: e.tensor_scalar(
                    hqb[:, qd_ * QW:(qd_ + 1) * QW], cbuf, 127.0 / 16.0, None,
                    OP.min),
                  waits=[(sS, sS.n), hq_free if qd_ == 0 else None,
                         dbg_ffn_free[0] if qd_ == 0 else None], inc=sV)
                if ti == 0:
                    emit_wjobs(8)
            hq_ready = (sV, sV.n)
            if DBG:
                o('Q', lambda e, ti=ti: e.dma_start(dbg_hq[ti], hqb),
                  waits=[hq_ready, (qd, qd.n)], inc=qd, inc_k=16)
                o('Q', lambda e, ti=ti: e.dma_start(dbg_xq[ti], xqb),
                  waits=[(qd, qd.n)], inc=qd, inc_k=16)
                dbg_ffn_free[0] = (qd, qd.n)
            xq_free = (sT, sT.n)

            NPA = min(FB, 4)
            NPB = FB - NPA
            for hb in range(HB):
                w_m = [hq_ready, bank_free[0], bank_free[1],
                       bank_free[2], bank_free[3]]
                if ti == 0:
                    w_m.append(w2_ready[hb])
                for db in range(FB):
                    gps = PA if db < 4 else PB
                    c0 = (db % 4) * ST
                    o('T', lambda e, db=db, hb=hb, gps=gps, c0=c0: e.matmul(
                        gps[:, c0:c0 + ST],
                        w2q[:, hb * D + db * 128:hb * D + (db + 1) * 128],
                        hqb[:, hb * ST:(hb + 1) * ST],
                        start=(hb == 0 and db % 2 == 0), stop=False),
                      waits=(w_m if db == 0 else []))
            for db in range(FB):
                gps = PA if db < 4 else PB
                c0 = (db % 4) * ST
                g2 = 64 * (db // FBH)
                o('T', lambda e, db=db, gps=gps, c0=c0, g2=g2: e.matmul(
                    gps[:, c0:c0 + ST],
                    b2row(db), ones_row[g2:g2 + 1, 0:ST], start=False,
                    stop=(db % 2 == 1)),
                  inc=(sT if db in (NPA - 1, FB - 1) else None))
                if db == NPA - 1:
                    paw = (sT, sT.n)
            hq_free = (sT, sT.n)
            o('S', lambda e: e.activation(
                gsA[:, 0:NPA * ST], PA[:, 0:NPA * ST], AF.Silu),
              waits=[paw, gs_free[0]], inc=sS)
            bank_free[0] = (sS, sS.n)
            bank_free[1] = (sS, sS.n)
            o('Q', lambda e, s=s, t0=t0: e.dma_start(
                g_scr[s, 0:NPA * 128, t0:t0 + ST]
                .rearrange("(f p) t -> p f t", p=128),
                gsA[:, 0:NPA * ST].rearrange(
                    "p (f c) -> p f c", f=NPA, c=ST)),
              waits=[(sS, sS.n)], inc=qg[0], inc_k=16)
            gs_free[0] = (qg[0], qg[0].n)
            if NPB:
                o('S', lambda e: e.activation(
                    gsB[:, 0:NPB * ST], PB[:, 0:NPB * ST], AF.Silu),
                  waits=[(sT, sT.n)], inc=sS)
                bank_free[2] = (sS, sS.n)
                bank_free[3] = (sS, sS.n)
                o('Q', lambda e, s=s, t0=t0: e.dma_start(
                    g_scr[s, NPA * 128:FB * 128, t0:t0 + ST]
                    .rearrange("(f p) t -> p f t", p=128),
                    gsB[:, 0:NPB * ST].rearrange(
                        "p (f c) -> p f c", f=NPB, c=ST)),
                  waits=[(sS, sS.n)], inc=qg[1], inc_k=16)
                cb_free = (qg[1], qg[1].n)
            xnf_free = (sS, sS.n)

        a_done_T = (sT, sT.n)
        a_done_S = (sS, sS.n)
        a_done_V = (sV, sV.n)
        a_qdone = [(w.h if False else w, w.n) for w in (qn[0], qn[1], qg[0], qg[1], qg[2], qg[3])]

        wq_sb = AB[:, c_wq + 0 * FB * D:c_wq + 1 * FB * D]
        wk_sb = AB[:, c_wq + 1 * FB * D:c_wq + 2 * FB * D]
        wv_sb = AB[:, c_wq + 2 * FB * D:c_wq + 3 * FB * D]
        wo_sb = AB[:, c_wq + 3 * FB * D:c_wq + 4 * FB * D]
        for wsb, wdr in ((wq_sb, wqT), (wk_sb, wkT), (wv_sb, wvT),
                         (wo_sb, woT)):
            o('G', lambda e, wsb=wsb, wdr=wdr: e.dma_start(
                _fv(wsb, FB, D), wdr.rearrange("(f p) c -> p f c", p=128)),
              waits=[a_done_T, a_done_S, a_done_V] + a_qdone +
                    ([(qd, qd.n)] if DBG else []),
              inc=qa, inc_k=16)
        attw_ready = (qa, qa.n)

        xnc = AB[:, c_xnc:c_xnc + FB * QB]
        kt_sb = AB[:, c_kt:c_kt + NH * S]
        v_sb = AB[:, c_v:c_v + KT * D]
        qt_sb = [AB[:, c_qt:c_qt + NH * QB],
                 AB[:, c_qt + NH * QB:c_qt + 2 * NH * QB]]
        e_sb = [AB[:, c_e:c_e + 2 * QB], AB[:, c_e + 2 * QB:c_e + 4 * QB]]
        sqr_sb = AB[:, c_sqr:c_sqr + QB]
        u_all = AF4[:, f_uall:f_uall + 2 * HPG * QB]
        gs2 = [AF4[:, f_gs2 + i * QB:f_gs2 + (i + 1) * QB] for i in range(2)]
        xts2 = [AF4[:, f_xts2 + i * QB:f_xts2 + (i + 1) * QB] for i in range(2)]
        outst = [AF4[:, f_outst + i * QB:f_outst + (i + 1) * QB]
                 for i in range(2)]
        gs2p = AF4[:, f_gs2:f_gs2 + 2 * QB]
        xts2p = AF4[:, f_xts2:f_xts2 + 2 * QB]
        outstp = AF4[:, f_outst:f_outst + 2 * QB]

        st = {
            "xnc_load": None, "xnc_free": None,
            "qt_free": [None, None], "obf_free": [None, None],
            "obf_dump": [None, None],
            "e_free": [None, None],
            "U_free": None, "D_free": None, "M_free": None,
            "u_free": None, "sq_free": None,
            "gx_free": [None, None], "out_free": [None, None],
            "kv_ready": {},
        }

        def setup_chunk(sq_, c):
            if DBG:
                o('Q', lambda e, sq_=sq_, c=c: e.dma_start(
                    dbg_xc[sq_, c], xnc),
                  waits=[st["xnc_load"], (qd, qd.n)], inc=qd, inc_k=16)
            for hb in range(NH):
                pw = hb % 2
                kps = ps[pw * 2]
                for kb in range(FB):
                    o('T', lambda e, hb=hb, kb=kb, kps=kps: e.matmul(
                        kps[:, 0:QB],
                        wk_sb[:, kb * D + hb * 128:kb * D + (hb + 1) * 128],
                        xnc[:, kb * QB:(kb + 1) * QB],
                        start=(kb == 0), stop=(kb == FB - 1)),
                      waits=([st["xnc_load"], attw_ready, bank_free[pw * 2]]
                             if kb == 0 else []),
                      inc=(sT if kb == FB - 1 else None))
                o('S', lambda e, hb=hb, c=c, kps=kps: e.copy(
                    kt_sb[:, hb * S + c * QB:hb * S + (c + 1) * QB],
                    kps[:, 0:QB]), waits=[(sT, sT.n)], inc=sS)
                bank_free[pw * 2] = (sS, sS.n)
            for tt4 in range(TPQ):
                for half in range(NV):
                    pw = half % 2
                    vps = ps[pw * 2 + 1]
                    for kb in range(FB):
                        o('T', lambda e, tt4=tt4, half=half, kb=kb, vps=vps:
                          e.matmul(
                            vps[:, 0:VCW],
                            xnc[:, kb * QB + tt4 * 128:
                                kb * QB + tt4 * 128 + 128],
                            wv_sb[:, kb * D + half * VCW:
                                  kb * D + half * VCW + VCW],
                            start=(kb == 0), stop=(kb == FB - 1)),
                          waits=([st["xnc_load"], bank_free[pw * 2 + 1]]
                                 if kb == 0 else []),
                          inc=(sT if kb == FB - 1 else None))
                    o('S', lambda e, c=c, tt4=tt4, half=half, vps=vps: e.copy(
                        v_sb[:, (c * TPQ + tt4) * D + half * VCW:
                             (c * TPQ + tt4) * D + half * VCW + VCW],
                        vps[:, 0:VCW]), waits=[(sT, sT.n)], inc=sS)
                    bank_free[pw * 2 + 1] = (sS, sS.n)
            qi = c % 2
            for hb in range(NH):
                pw = hb % 2
                qps = ps[pw * 2]
                w0 = [st["xnc_load"], bank_free[pw * 2]]
                if hb == 0:
                    w0 += [st["qt_free"][qi], st["obf_free"][qi],
                           st["obf_dump"][qi]]
                for kb in range(FB):
                    o('T', lambda e, hb=hb, kb=kb, qps=qps: e.matmul(
                        qps[:, 0:QB],
                        wq_sb[:, kb * D + hb * 128:kb * D + (hb + 1) * 128],
                        xnc[:, kb * QB:(kb + 1) * QB],
                        start=(kb == 0), stop=(kb == FB - 1)),
                      waits=(w0 if kb == 0 else []),
                      inc=(sT if kb == FB - 1 else None))
                o('S', lambda e, hb=hb, qi=qi, qps=qps: e.copy(
                    qt_sb[qi][:, hb * QB:(hb + 1) * QB], qps[:, 0:QB]),
                  waits=[(sT, sT.n)], inc=sS)
                bank_free[pw * 2] = (sS, sS.n)
            st["xnc_free"] = (sT, sT.n)
            st["kv_ready"][c] = (sS, sS.n)

        if DBG:
            o('V', lambda e: e.memset(u_all, 0.0),
              waits=[a_done_V], inc=sV)
            o('V', lambda e: e.memset(AF4[0:8, f_d8:f_d8 + QB], 0.0), inc=sV)
            st["u_free"] = (sV, sV.n)
        for s in range(BPC):
            st["kv_ready"].clear()
            for fb in range(FB):
                o('Q', lambda e, s=s, fb=fb: e.dma_start(
                    xnc[:, fb * QB:(fb + 1) * QB],
                    xn_scr[s, fb * 128:(fb + 1) * 128, 0:QB]),
                  waits=(([st["xnc_free"], a_done_S] + a_qdone)
                         if fb == 0 else []), inc=qc, inc_k=16)
            st["xnc_load"] = (qc, qc.n)
            setup_chunk(s, 0)

            for qb in range(NQB):
                q0 = qb * QB
                nk = q0 // 128 + TPQ
                qi = qb % 2
                o('Q', lambda e, s=s, q0=q0: e.dma_start(
                    gs2p.rearrange("p (f c) -> p f c", f=2, c=QB),
                    g_scr[s, 0:256, q0:q0 + QB]
                    .rearrange("(f p) t -> p f t", p=128)),
                  waits=[st["gx_free"][0], st["gx_free"][1]] + a_qdone,
                  inc=qe[0], inc_k=16)
                o('Q', lambda e, s=s, q0=q0: e.dma_start(
                    xts2p.rearrange("p (f c) -> p f c", f=2, c=QB),
                    xT[s, 0:256, q0:q0 + QB]
                    .rearrange("(f p) t -> p f t", p=128)),
                  inc=qe[0], inc_k=16)
                gx_load = [(qe[0], qe[0].n), (qe[0], qe[0].n)]

                obf = qt_sb[qi]
                if DBG:
                    o('Q', lambda e, s=s, qb=qb, qi=qi: e.dma_start(
                        dbg_q[s, qb], qt_sb[qi]),
                      waits=[st["kv_ready"][qb], (qd, qd.n)], inc=qd, inc_k=16)
                    if qb == NQB - 1:
                        o('Q', lambda e, s=s: e.dma_start(dbg_k[s], kt_sb),
                          waits=[(qd, qd.n)], inc=qd, inc_k=16)
                        o('Q', lambda e, s=s: e.dma_start(dbg_v[s], v_sb),
                          waits=[(qd, qd.n)], inc=qd, inc_k=16)
                for g in range(NG):
                    for hl in range(HPG):
                        h = g * HPG + hl
                        for j in range(nk):
                            p = j % 2
                            w_s = [st["kv_ready"][qb], bank_free[p * 2]]
                            if j == 0:
                                w_s += ([st["U_free"], bank_free[4],
                                         bank_free[5]] +
                                        ([st["D_free"]] if hl == 0 else []))
                            diag = j >= nk - TPQ
                            soff = 384 + (q0 - j * 128)
                            for sub in range(2):
                                last = sub == 1 and not diag
                                o('T', lambda e, h=h, j=j, p=p, qi=qi,
                                  sub=sub: e.matmul(
                                    ps[p * 2 + sub][:, 0:QB],
                                    kt_sb[sub * 64:sub * 64 + 64,
                                          h * S + j * 128:h * S + j * 128 + 128],
                                    qt_sb[qi][sub * 64:sub * 64 + 64,
                                              h * QB:(h + 1) * QB],
                                    start=True, stop=not diag),
                                  waits=((w_s + [bank_free[p * 2 + 1]])
                                         if sub == 0 else []),
                                  inc=(sT if last else None))
                                if diag:
                                    o('T', lambda e, p=p, sub=sub, soff=soff:
                                      e.matmul(
                                        ps[p * 2 + sub][:, 0:QB], tril_sb,
                                        mask_sb[:, soff:soff + QB],
                                        start=False, stop=True),
                                      inc=(sT if sub == 1 else None))
                            scores_done = (sT, sT.n)
                            PSpair = PA if p == 0 else PB
                            o('S', lambda e, p=p, PSpair=PSpair: e.activation(
                                e_sb[p], PSpair, AF.Exp, scale=0.125),
                              waits=[scores_done, st["e_free"][p]], inc=sS)
                            bank_free[p * 2] = (sS, sS.n)
                            bank_free[p * 2 + 1] = (sS, sS.n)
                            e_ready = (sS, sS.n)
                            m0, m1 = HPG + hl, hl
                            o('T', lambda e, h=h, j=j, p=p, nk=nk: e.matmul(
                                psU0[:, 0:QB],
                                v_sb[:, j * D + h * 128:j * D + (h + 1) * 128],
                                e_sb[p][:, 0:QB],
                                start=(j == 0), stop=(j == nk - 1)),
                              waits=[e_ready])
                            o('T', lambda e, h=h, j=j, p=p, nk=nk: e.matmul(
                                psU1[:, 0:QB],
                                v_sb[:, j * D + h * 128:j * D + (h + 1) * 128],
                                e_sb[p][:, QB:2 * QB],
                                start=(j == 0), stop=(j == nk - 1)))
                            o('T', lambda e, j=j, p=p, m0=m0, hl=hl:
                              e.matmul(
                                psD[0:8, 0:QB], ind_sb[:, m0 * 8:m0 * 8 + 8],
                                e_sb[p][:, 0:QB],
                                start=(hl == 0 and j == 0), stop=False))
                            o('T', lambda e, j=j, p=p, m1=m1, hl=hl, nk=nk:
                              e.matmul(
                                psD[0:8, 0:QB], ind_sb[:, m1 * 8:m1 * 8 + 8],
                                e_sb[p][:, QB:2 * QB],
                                start=False,
                                stop=(hl == HPG - 1 and j == nk - 1)), inc=sT)
                            st["e_free"][p] = (sT, sT.n)
                        ucol = hl * 2
                        o('S', lambda e, ucol=ucol: e.copy(
                            u_all[:, ucol * QB:(ucol + 1) * QB], psU0[:, 0:QB]),
                          waits=[(sT, sT.n), st["u_free"]])
                        o('S', lambda e, ucol=ucol: e.copy(
                            u_all[:, (ucol + 1) * QB:(ucol + 2) * QB],
                            psU1[:, 0:QB]), inc=sS)
                        st["U_free"] = (sS, sS.n)
                    o('S', lambda e: e.copy(d8[0:2 * HPG, :],
                                            psD[0:2 * HPG, 0:QB]),
                      waits=[(sT, sT.n)], inc=sS)
                    st["D_free"] = (sS, sS.n)
                    if DBG:
                        o('Q', lambda e, s=s, qb=qb, g=g: e.dma_start(
                            dbg_u[s, qb, g], u_all),
                          waits=[(sS, sS.n), (qd, qd.n)], inc=qd, inc_k=16)
                        o('Q', lambda e, s=s, qb=qb, g=g: e.dma_start(
                            dbg_d[s, qb, g], d8), waits=[(qd, qd.n)],
                          inc=qd, inc_k=16)
                    dbg_done = (qd, qd.n)
                    o('V', lambda e: e.reciprocal(
                        al8[0:2 * HPG, :], d8[0:2 * HPG, :]),
                      waits=[(sS, sS.n)])
                    o('V', lambda e: e.tensor_scalar(
                        al8[0:HPG, :], al8[0:HPG, :],
                        lam_col[0:HPG, 0:1], None, OP.mult), inc=sV)
                    alpha_ready = (sV, sV.n)
                    for hl in range(HPG):
                        h = g * HPG + hl
                        ucol = hl * 2
                        o('T', lambda e, hl=hl: e.matmul(
                            ps[0][:, 0:QB],
                            sel_sb[0:2 * HPG,
                                   (HPG + hl) * 128:(HPG + hl + 1) * 128],
                            al8[0:2 * HPG, :], start=True, stop=True),
                          waits=[alpha_ready, bank_free[0], bank_free[1]])
                        o('T', lambda e, hl=hl: e.matmul(
                            ps[1][:, 0:QB],
                            sel_sb[0:2 * HPG, hl * 128:(hl + 1) * 128],
                            al8[0:2 * HPG, :], start=True, stop=True), inc=sT)
                        bc_a = (sT, sT.n)
                        o('V', lambda e, ucol=ucol: e.tensor_tensor(
                            u_all[:, ucol * QB:(ucol + 2) * QB],
                            u_all[:, ucol * QB:(ucol + 2) * QB],
                            PA, OP.mult),
                          waits=[bc_a] + ([dbg_done] if DBG else []))
                        o('V', lambda e, ucol=ucol: e.tensor_tensor(
                            u_all[:, ucol * QB:(ucol + 1) * QB],
                            u_all[:, ucol * QB:(ucol + 1) * QB],
                            u_all[:, (ucol + 1) * QB:(ucol + 2) * QB],
                            OP.subtract), inc=sV)
                        bank_free[0] = (sV, sV.n)
                        bank_free[1] = (sV, sV.n)
                        o_done = (sV, sV.n)
                        o('S', lambda e, ucol=ucol: e.activation(
                            sqr_sb, u_all[:, ucol * QB:(ucol + 1) * QB],
                            AF.Square), waits=[o_done, st["sq_free"]], inc=sS)
                        w_ms = [(sS, sS.n)]
                        if hl == 0:
                            w_ms.append(st["M_free"])
                        o('T', lambda e, hl=hl: e.matmul(
                            psM[0:8, 0:QB], ind_sb[:, hl * 8:hl * 8 + 8],
                            sqr_sb, start=(hl == 0), stop=(hl == HPG - 1)),
                          waits=w_ms, inc=sT)
                        st["sq_free"] = (sT, sT.n)
                    o('S', lambda e: e.copy(ms8[0:HPG, :], psM[0:HPG, 0:QB]),
                      waits=[(sT, sT.n)], inc=sS)
                    st["M_free"] = (sS, sS.n)
                    o('V', lambda e: e.tensor_scalar(
                        bt8[0:HPG, :], ms8[0:HPG, :], 1.0 / 128.0, float(EPS),
                        OP.mult, OP.add), waits=[(sS, sS.n)], inc=sV)
                    o('S', lambda e: e.activation(
                        bt8[0:HPG, :], bt8[0:HPG, :], AF.Sqrt),
                      waits=[(sV, sV.n)], inc=sS)
                    o('V', lambda e: e.reciprocal(bt8[0:HPG, :], bt8[0:HPG, :]),
                      waits=[(sS, sS.n)], inc=sV)
                    beta_ready = (sV, sV.n)
                    for hl in range(HPG):
                        h = g * HPG + hl
                        ucol = hl * 2
                        o('T', lambda e, hl=hl: e.matmul(
                            ps[2][:, 0:QB],
                            sel_sb[0:HPG, hl * 128:(hl + 1) * 128],
                            bt8[0:HPG, :], start=True, stop=True),
                          waits=[beta_ready, bank_free[2]], inc=sT)
                        bc_b = (sT, sT.n)
                        o('V', lambda e, h=h, ucol=ucol:
                          e.scalar_tensor_tensor(
                            obf[:, h * QB:(h + 1) * QB],
                            u_all[:, ucol * QB:(ucol + 1) * QB],
                            1.0 - LAMBDA_INIT, ps[2][:, 0:QB],
                            OP.mult, OP.mult), waits=[bc_b], inc=sV)
                        bank_free[2] = (sV, sV.n)
                    st["u_free"] = (sV, sV.n)
                obf_done = (sV, sV.n)
                if DBG:
                    o('Q', lambda e, s=s, qb=qb: e.dma_start(
                        dbg_o[s, qb], obf), waits=[obf_done, (qd, qd.n)],
                      inc=qd, inc_k=16)
                    st["obf_dump"][qi] = (qd, qd.n)
                st["qt_free"][qi] = (sT, sT.n)

                for dp in range(FB // 2):
                    db0 = dp * 2
                    for half in range(2):
                        db = db0 + half
                        for kb in range(NH):
                            o('T', lambda e, db=db, kb=kb, half=half:
                              e.matmul(
                                PC[:, half * QB:(half + 1) * QB],
                                wo_sb[:, kb * D + db * 128:
                                      kb * D + (db + 1) * 128],
                                obf[:, kb * QB:(kb + 1) * QB],
                                start=(kb == 0), stop=(kb == NH - 1)),
                              waits=([obf_done, bank_free[4], bank_free[5]]
                                     if (half == 0 and kb == 0) else []),
                              inc=(sT if (half == 1 and kb == NH - 1)
                                   else None))
                    wo_done = (sT, sT.n)
                    o('V', lambda e: e.tensor_tensor(
                        outstp, PC, gs2p, OP.mult),
                      waits=[wo_done, gx_load[0], gx_load[1],
                             st["out_free"][0], st["out_free"][1]])
                    o('V', lambda e: e.tensor_tensor(
                        outstp, outstp, xts2p, OP.add), inc=sV)
                    bank_free[4] = (sV, sV.n)
                    bank_free[5] = (sV, sV.n)
                    st["gx_free"][0] = (sV, sV.n)
                    st["gx_free"][1] = (sV, sV.n)
                    o('Q', lambda e, s=s, q0=q0, dp=dp: e.dma_start(
                        outT[s, dp * 256:(dp + 1) * 256, q0:q0 + QB]
                        .rearrange("(f p) t -> p f t", p=128),
                        outstp.rearrange("p (f c) -> p f c", f=2, c=QB)),
                      waits=[(sV, sV.n)], inc=qo[dp % 2], inc_k=16)
                    st["out_free"][0] = (qo[dp % 2], qo[dp % 2].n)
                    st["out_free"][1] = st["out_free"][0]
                    if db0 + 2 < FB:
                        o('Q', lambda e, s=s, q0=q0, dp=dp: e.dma_start(
                            gs2p.rearrange("p (f c) -> p f c", f=2, c=QB),
                            g_scr[s, (dp + 1) * 256:(dp + 2) * 256,
                                  q0:q0 + QB]
                            .rearrange("(f p) t -> p f t", p=128)),
                          waits=[(sV, sV.n)], inc=qe[dp % 2], inc_k=16)
                        o('Q', lambda e, s=s, q0=q0, dp=dp: e.dma_start(
                            xts2p.rearrange("p (f c) -> p f c", f=2, c=QB),
                            xT[s, (dp + 1) * 256:(dp + 2) * 256, q0:q0 + QB]
                            .rearrange("(f p) t -> p f t", p=128)),
                          inc=qe[dp % 2], inc_k=16)
                        gx_load = [(qe[dp % 2], qe[dp % 2].n)] * 2
                st["U_free"] = (sV, sV.n)
                st["obf_free"][qi] = (sT, sT.n)

                if qb + 1 < NQB:
                    for fb in range(FB):
                        o('Q', lambda e, s=s, qb=qb, fb=fb: e.dma_start(
                            xnc[:, fb * QB:(fb + 1) * QB],
                            xn_scr[s, fb * 128:(fb + 1) * 128,
                                   (qb + 1) * QB:(qb + 2) * QB]),
                          waits=([st["xnc_free"]] if fb == 0 else []),
                          inc=qc, inc_k=16)
                    st["xnc_load"] = (qc, qc.n)
                    setup_chunk(s, qb + 1)

        def _final_waits_q(e):
            for sm in (qx, qc, qn[0], qn[1], qg[0], qg[1], qg[2], qg[3],
                       qe[0], qe[1], qo[0], qo[1], qd):
                if sm.n:
                    e.wait_ge(sm.h, sm.n)
            return None
        def _final_waits_g(e):
            for sm in (qk, qw[0], qw[1], qa):
                if sm.n:
                    e.wait_ge(sm.h, sm.n)
            return None
        o('Q', _final_waits_q)
        o('G', _final_waits_g)

    sems = [sT, sS, sV, qx, qc, qk, qa, qd] + qn + qg + qe + qo + qw
    with nc.Block() as blk:
        def _run(name):
            def body(eng):
                for sm in sems:
                    sm.n = 0
                ctx.begin(name, eng)
                schedule()
            return body
        blk.sync(_run('Q'))
        blk.gpsimd(_run('G'))
        blk.tensor(_run('T'))
        blk.scalar(_run('S'))
        blk.vector(_run('V'))
    return nc


def _host_constants():
    import ml_dtypes
    ramp = (np.arange(896)[None, :] <= np.arange(128)[:, None] + 383)
    ramp = (MASKC * ramp.astype(np.float32)).astype(ml_dtypes.bfloat16)
    tril = (np.arange(128)[:, None] <= np.arange(128)[None, :])
    tril = tril.astype(ml_dtypes.bfloat16)
    ind = np.zeros((128, 64), dtype=np.float32)
    for m in range(8):
        ind[:, m * 8 + m] = 1.0
    sel = np.zeros((8, 8 * 128), dtype=np.float32)
    for m in range(8):
        sel[m, m * 128:(m + 1) * 128] = 1.0
    return ramp, tril, ind.astype(ml_dtypes.bfloat16), sel


def kernel(**inputs):
    cfg = CFG
    BPC = cfg["BPC"]
    x = np.ascontiguousarray(np.asarray(inputs["L0_attn_x"], dtype=np.float32))
    ncores = x.shape[0] // BPC
    mask, tril, ind, sel = _host_constants()
    shared = {
        "wqT": np.ascontiguousarray(np.asarray(inputs["L0_attn_Wq"], np.float32).T),
        "wkT": np.ascontiguousarray(np.asarray(inputs["L0_attn_Wk"], np.float32).T),
        "wvT": np.ascontiguousarray(np.asarray(inputs["L0_attn_Wv"], np.float32).T),
        "woT": np.ascontiguousarray(np.asarray(inputs["L0_attn_Wo"], np.float32).T),
        "w1T": np.ascontiguousarray(np.asarray(inputs["L0_ffn_w1"], np.float32).T),
        "w2T": np.ascontiguousarray(np.asarray(inputs["L0_ffn_w2"], np.float32).T),
        "b1": np.ascontiguousarray(np.asarray(inputs["L0_ffn_b1"], np.float32)),
        "b2": np.ascontiguousarray(np.asarray(inputs["L0_ffn_b2"], np.float32)),
        "lqk": np.ascontiguousarray(np.concatenate([
            np.asarray(inputs["L0_attn_lq1"], np.float32),
            np.asarray(inputs["L0_attn_lk1"], np.float32),
            np.asarray(inputs["L0_attn_lq2"], np.float32),
            np.asarray(inputs["L0_attn_lk2"], np.float32)])[None, :]),
        "mask": mask, "tril": tril, "ind": ind, "sel": sel,
    }
    nc = build_nc(cfg)
    in_maps = []
    for c in range(ncores):
        xs = x[c * BPC:(c + 1) * BPC]
        m = dict(shared)
        m["xT"] = np.ascontiguousarray(xs.transpose(0, 2, 1))
        in_maps.append(m)
    res = run_bass_kernel_spmd(nc, in_maps, core_ids=list(range(ncores)))
    outs = [np.asarray(r["outT"]).transpose(0, 2, 1) for r in res.results]
    return np.ascontiguousarray(np.concatenate(outs, axis=0)).astype(np.float32)


# version 07
# speedup vs baseline: 1.6156x; 1.3114x over previous
import math
import numpy as np

import concourse.bass as bass
import concourse.mybir as mybir
from concourse.bass_utils import run_bass_kernel_spmd

F32 = mybir.dt.float32
BF16 = mybir.dt.bfloat16
I32 = mybir.dt.int32
AF = mybir.ActivationFunctionType
OP = mybir.AluOpType
AX = mybir.AxisListType

LAMBDA_INIT = 0.8 - 0.6 * math.exp(-0.3 * 6)
EPS = 1e-5
MAGIC = 12582912.0
MASKC = -400.0

CFG = dict(D=1024, S=2048, HID=4096, BPC=2, ST=256, QB=512)


class Sem:
    __slots__ = ("h", "n", "name")

    def __init__(self, nc, name):
        self.h = nc.alloc_semaphore(name)
        self.n = 0
        self.name = name


class Ctx:

    def __init__(self):
        self.active = None
        self.eng = None
        self._waited = {}
        self._need_drain = False

    def begin(self, name, eng):
        self.active = name
        self.eng = eng
        self._waited = {}
        self._need_drain = False

    def op(self, engine, emit, waits=(), inc=None, inc_k=1):
        if engine == self.active:
            if self.active in ('V', 'S') and self._need_drain:
                self.eng.drain()
                self._need_drain = False
            for w in waits:
                if w is None:
                    continue
                sem, val = w
                assert val <= sem.n, (
                    f"wait-ahead on {sem.name}: {val} > {sem.n}")
                if val <= 0 or self._waited.get(sem.name, -1) >= val:
                    continue
                self.eng.wait_ge(sem.h, val)
                self._waited[sem.name] = val
            inst = emit(self.eng)
            self._need_drain = True
            if inc is not None:
                if isinstance(inst, list):
                    inst = inst[-1]
                inst.then_inc(inc.h, inc_k)
        if inc is not None:
            inc.n += inc_k


def _fv(ap2d, nblk, blk_cols):
    return ap2d.rearrange("p (f t) -> p f t", f=nblk, t=blk_cols)


def build_nc(cfg):
    D, S, HID, BPC = cfg["D"], cfg["S"], cfg["HID"], cfg["BPC"]
    ST, QB = cfg["ST"], cfg["QB"]
    FB = D // 128
    HB = HID // 128
    NH = D // 128
    NQB = S // QB
    KT = S // 128
    TPQ = QB // 128
    HPG = min(4, NH)
    NG = NH // HPG
    VCW = min(512, D)
    NV = D // VCW
    assert 2 * HPG <= 8 and NH <= 8 and S % QB == 0 and D % VCW == 0

    nc = bass.Bass("TRN2", target_bir_lowering=False, debug=False)

    xT = nc.dram_tensor("xT", [BPC, D, S], F32, kind="ExternalInput").ap()
    wqT = nc.dram_tensor("wqT", [D, D], F32, kind="ExternalInput").ap()
    wkT = nc.dram_tensor("wkT", [D, D], F32, kind="ExternalInput").ap()
    wvT = nc.dram_tensor("wvT", [D, D], F32, kind="ExternalInput").ap()
    woT = nc.dram_tensor("woT", [D, D], F32, kind="ExternalInput").ap()
    w1T = nc.dram_tensor("w1T", [D, HID], F32, kind="ExternalInput").ap()
    w2T = nc.dram_tensor("w2T", [HID, D], F32, kind="ExternalInput").ap()
    b1d = nc.dram_tensor("b1", [HID], F32, kind="ExternalInput").ap()
    b2d = nc.dram_tensor("b2", [D], F32, kind="ExternalInput").ap()
    lqkd = nc.dram_tensor("lqk", [1, 256], F32, kind="ExternalInput").ap()
    maskd = nc.dram_tensor("mask", [128, 896], BF16, kind="ExternalInput").ap()
    trild = nc.dram_tensor("tril", [128, 128], BF16, kind="ExternalInput").ap()
    indd = nc.dram_tensor("ind", [128, 64], BF16, kind="ExternalInput").ap()
    seld = nc.dram_tensor("sel", [8, 8 * 128], F32, kind="ExternalInput").ap()

    outT = nc.dram_tensor("outT", [BPC, D, S], F32, kind="ExternalOutput").ap()
    g_scr = nc.dram_tensor("g_scr", [BPC, D, S], F32, kind="Internal").ap()
    xn_scr = nc.dram_tensor("xn_scr", [BPC, D, S], BF16, kind="Internal").ap()
    b1_scr = nc.dram_tensor("b1_scr", [HID], BF16, kind="Internal").ap()
    b2_scr = nc.dram_tensor("b2_scr", [D], BF16, kind="Internal").ap()
    DBG = bool(cfg.get("debug"))
    if DBG:
        NQB_ = S // QB
        NG_ = (D // 128) // min(4, D // 128)
        HPG_ = min(4, D // 128)
        dbg_u = nc.dram_tensor("dbg_u", [BPC, NQB_, NG_, 128, 2 * HPG_ * QB],
                               F32, kind="ExternalOutput").ap()
        dbg_d = nc.dram_tensor("dbg_d", [BPC, NQB_, NG_, 8, QB], F32,
                               kind="ExternalOutput").ap()
        dbg_o = nc.dram_tensor("dbg_o", [BPC, NQB_, 128, (D // 128) * QB],
                               BF16, kind="ExternalOutput").ap()
        dbg_k = nc.dram_tensor("dbg_k", [BPC, 128, (D // 128) * S], BF16,
                               kind="ExternalOutput").ap()
        dbg_v = nc.dram_tensor("dbg_v", [BPC, 128, (S // 128) * D], BF16,
                               kind="ExternalOutput").ap()
        dbg_q = nc.dram_tensor("dbg_q", [BPC, NQB_, 128, (D // 128) * QB],
                               BF16, kind="ExternalOutput").ap()
        dbg_xc = nc.dram_tensor("dbg_xc", [BPC, NQB_, 128, (D // 128) * QB],
                                BF16, kind="ExternalOutput").ap()
        NT_ = (BPC * S) // ST
        dbg_hq = nc.dram_tensor("dbg_hq", [NT_, 128, (HID // 128) * ST],
                                BF16, kind="ExternalOutput").ap()
        dbg_xq = nc.dram_tensor("dbg_xq", [NT_, 128, (D // 128) * ST],
                                BF16, kind="ExternalOutput").ap()
        dbg_ub = nc.dram_tensor("dbg_ub", [4, 128, ST], F32,
                                kind="ExternalOutput").ap()

    c_w1q = 0
    c_w2q = c_w1q + FB * HID
    c_hqb = c_w2q + HB * D
    c_xqb = c_hqb + HB * ST
    c_xnst = c_xqb + FB * ST
    a_cols = c_xnst + 8 * ST

    c_wq = 0
    c_xnc = c_wq + 4 * FB * D
    c_kt = c_xnc + FB * QB
    c_v = c_kt + NH * S
    c_qt = c_v + KT * D
    c_e = c_qt + 2 * NH * QB
    c_sqr = c_e + 4 * QB
    b_cols = c_sqr + QB
    BFC = max(a_cols, b_cols)

    QW = min(4 * ST, 1024)
    f_xts = 0
    f_c = f_xts + FB * ST
    f_u = f_c + QW
    f_gs = f_u + 2 * QW
    f_sq = f_gs + 4 * ST
    f_mbc = f_sq + 4 * ST
    f_wst = f_mbc + 2 * ST
    a_fcols = f_wst + 1024

    f_uall = 0
    f_gs2 = f_uall + 2 * HPG * QB
    f_xts2 = f_gs2 + 2 * QB
    f_outst = f_xts2 + 2 * QB
    f_d8 = f_outst + 2 * QB
    f_al8 = f_d8 + QB
    f_ms8 = f_al8 + QB
    f_bt8 = f_ms8 + QB
    b_fcols = f_bt8 + QB
    f_mv = a_fcols
    f_lnt = f_mv + ST
    a_fcols = f_mv + 3 * ST
    F32C = max(a_fcols, b_fcols)

    AB = nc.alloc_sbuf_tensor("AB", [128, BFC], BF16).ap()
    AF4 = nc.alloc_sbuf_tensor("AF4", [128, F32C], F32).ap()
    KI = AB[:, c_e:c_e + 2048].bitcast(I32)

    b1h = (HID // 2 + 63) // 64 * 64
    b2h = (D // 2 + 63) // 64 * 64
    scb_w = 896 + 128 + 64 + 16 + ST + 32 + 16 + b1h + b2h
    SCB = nc.alloc_sbuf_tensor("SCB", [128, scb_w], BF16).ap()
    _o = [0]

    def _scb(w):
        a = _o[0]
        _o[0] += w
        return SCB[:, a:a + w]
    mask_sb = _scb(896)
    tril_sb = _scb(128)
    ind_sb = _scb(64)
    ones_bf = _scb(16)[:, 0:1]
    ones_row = _scb(ST)
    b1qb = _scb(32)[:, 0:HB]
    b2qb = _scb(16)[:, 0:FB]
    b1r2 = _scb(b1h)
    b2r2 = _scb(b2h)
    scf_w = 1024 + 16 + 128 + 32 + 16 + 16 + 16 + 16
    SCF = nc.alloc_sbuf_tensor("SCF", [128, scf_w], F32).ap()
    _o[0] = 0

    def _scf(w):
        a = _o[0]
        _o[0] += w
        return SCF[:, a:a + w]
    sel_sb = _scf(1024)[0:8, :]
    ones_cf = _scf(16)[:, 0:1]
    ones1f = _scf(128)[0:1, :]
    b1q = _scf(32)[:, 0:HB]
    b2q = _scf(16)[:, 0:FB]
    lam_sb = _scf(16)[0:1, 0:8]
    lam_col = _scf(16)[:, 0:1]
    fc = _scf(16)[:, 0:4]
    lam_in = AF4[0:1, f_mv:f_mv + 256]
    d8 = AF4[0:8, f_d8:f_d8 + QB]
    al8 = AF4[0:8, f_al8:f_al8 + QB]
    ms8 = AF4[0:8, f_ms8:f_ms8 + QB]
    bt8 = AF4[0:8, f_bt8:f_bt8 + QB]
    mv_sb = AF4[0:1, f_mv:f_mv + 3 * ST]
    lnt = AF4[0:1, f_lnt:f_lnt + ST]

    PA = nc.alloc_psum_tensor("PA", [128, 1024], F32).ap()
    PB = nc.alloc_psum_tensor("PB", [128, 1024], F32).ap()
    PC = nc.alloc_psum_tensor("PC", [128, 1024], F32).ap()
    ps = [PA[:, 0:512], PA[:, 512:1024], PB[:, 0:512], PB[:, 512:1024],
          PC[:, 0:512], PC[:, 512:1024]]
    ps += [nc.alloc_psum_tensor(f"ps{i}", [128, 512], F32).ap()
           for i in range(6, 8)]
    psU0, psU1, psD, psM = ps[4], ps[5], ps[6], ps[7]

    sT = Sem(nc, "sT")
    sS = Sem(nc, "sS")
    sV = Sem(nc, "sV")
    qx = Sem(nc, "qx")
    qn = [Sem(nc, "qn0"), Sem(nc, "qn1")]
    qg = [Sem(nc, f"qg{i}") for i in range(4)]
    qc = Sem(nc, "qc")
    qe = [Sem(nc, "qe0"), Sem(nc, "qe1")]
    qo = [Sem(nc, "qo0"), Sem(nc, "qo1")]
    qk = Sem(nc, "qk")
    qw = [Sem(nc, "qw0"), Sem(nc, "qw1")]
    qa = Sem(nc, "qa")
    qd = Sem(nc, "qd")

    ctx = Ctx()

    bank_free = [None] * 8

    def schedule():
        o = ctx.op
        for i in range(8):
            bank_free[i] = None

        o('G', lambda e: e.dma_start(mask_sb, maskd), inc=qk, inc_k=16)
        o('G', lambda e: e.dma_start(tril_sb, trild), inc=qk, inc_k=16)
        o('G', lambda e: e.dma_start(ind_sb, indd), inc=qk, inc_k=16)
        o('G', lambda e: e.dma_start(sel_sb, seld), inc=qk, inc_k=16)
        o('G', lambda e: e.dma_start(lam_in, lqkd), inc=qk, inc_k=16)
        with nc.allow_non_contiguous_dma("tiny bias loads"):
            o('G', lambda e: e.dma_start(
                b1q, b1d.rearrange("(c p) -> p c", p=128)), inc=qk, inc_k=16)
            o('G', lambda e: e.dma_start(
                b2q, b2d.rearrange("(c p) -> p c", p=128)), inc=qk, inc_k=16)
        const_loaded = (qk, qk.n)
        o('V', lambda e: e.memset(ones_bf, 1.0))
        o('V', lambda e: e.memset(ones_cf, 1.0))
        o('V', lambda e: e.memset(ones_row, 1.0))
        o('V', lambda e: e.memset(fc[:, 0:1], float(EPS)))
        o('V', lambda e: e.memset(fc[:, 1:2], -32.0))
        o('V', lambda e: e.memset(fc[:, 2:3], 512.0 + MAGIC))
        o('V', lambda e: e.memset(ones1f, 1.0), inc=sV)

        def quantw(dst, src, n):
            ii = KI[:, 0:n]
            o('V', lambda e: e.tensor_scalar(
                src, src, 64.0, MAGIC, OP.mult, OP.add))
            o('V', lambda e: e.tensor_scalar(src, src, -MAGIC, None, OP.add))
            o('V', lambda e: e.tensor_copy(ii, src))
            o('V', lambda e: e.tensor_copy(src, ii))
            o('V', lambda e: e.tensor_scalar(
                src, src, 1.0 / 64.0, -2.0, OP.mult, OP.max))
            o('V', lambda e: e.tensor_scalar(
                dst, src, 127.0 / 64.0, None, OP.min), inc=sV)

        o('V', lambda e: e.engine_nop(), waits=[const_loaded])
        quantw(b1qb, b1q, HB)
        quantw(b2qb, b2q, FB)
        HBH = HB // 2
        FBH = FB // 2

        def b1row(hb):
            g = 64 * (hb // HBH)
            c = (hb % HBH) * 128
            return b1r2[g:g + 1, c:c + 128]

        def b2row(db):
            g = 64 * (db // FBH)
            c = (db % FBH) * 128
            return b2r2[g:g + 1, c:c + 128]

        with nc.allow_non_contiguous_dma("bias row gathers"):
            o('G', lambda e: e.dma_start(
                b1_scr.rearrange("(c p) -> p c", p=128), b1qb),
              waits=[(sV, sV.n)], inc=qk, inc_k=16)
            o('G', lambda e: e.dma_start(
                b2_scr.rearrange("(c p) -> p c", p=128), b2qb),
              inc=qk, inc_k=16)
            b1v = b1_scr.rearrange("(a b) -> a b", a=2)
            b2v = b2_scr.rearrange("(a b) -> a b", a=2)
            o('G', lambda e: e.dma_start(b1r2[0:1, 0:HID // 2], b1v[0:1, :]),
              waits=[(qk, qk.n)], inc=qk, inc_k=16)
            o('G', lambda e: e.dma_start(b1r2[64:65, 0:HID // 2], b1v[1:2, :]),
              inc=qk, inc_k=16)
            o('G', lambda e: e.dma_start(b2r2[0:1, 0:D // 2], b2v[0:1, :]),
              inc=qk, inc_k=16)
            o('G', lambda e: e.dma_start(b2r2[64:65, 0:D // 2], b2v[1:2, :]),
              inc=qk, inc_k=16)
        brows_ready = (qk, qk.n)

        o('V', lambda e: e.tensor_tensor(
            lam_in[:, 0:64], lam_in[:, 0:64], lam_in[:, 64:128], OP.mult))
        o('V', lambda e: e.tensor_tensor(
            lam_in[:, 128:192], lam_in[:, 128:192], lam_in[:, 192:256], OP.mult))
        o('V', lambda e: e.reduce_sum(lam_sb[:, 0:1], lam_in[:, 0:64], axis=AX.X))
        o('V', lambda e: e.reduce_sum(lam_sb[:, 1:2], lam_in[:, 128:192], axis=AX.X),
          inc=sV)
        o('S', lambda e: e.activation(lam_sb[:, 2:3], lam_sb[:, 0:1], AF.Exp),
          waits=[(sV, sV.n)])
        o('S', lambda e: e.activation(lam_sb[:, 3:4], lam_sb[:, 1:2], AF.Exp),
          inc=sS)
        o('V', lambda e: e.tensor_tensor(
            lam_sb[:, 4:5], lam_sb[:, 2:3], lam_sb[:, 3:4], OP.subtract),
          waits=[(sS, sS.n)])
        o('V', lambda e: e.tensor_scalar(
            lam_sb[:, 4:5], lam_sb[:, 4:5], LAMBDA_INIT, None, OP.add), inc=sV)
        o('T', lambda e: e.matmul(ps[0][:, 0:1], ones1f, lam_sb[:, 4:5],
                                  start=True, stop=True),
          waits=[(sV, sV.n)], inc=sT)
        o('S', lambda e: e.copy(lam_col, ps[0][:, 0:1]),
          waits=[(sT, sT.n)], inc=sS)
        bank_free[0] = (sS, sS.n)

        w1q = AB[:, c_w1q:c_w1q + FB * HID]
        w2q = AB[:, c_w2q:c_w2q + HB * D]
        CW1 = min(1024, HID)
        assert D <= 1024
        wst = [AF4[:, f_wst:f_wst + 1024],
               AF4[:, f_gs:f_gs + 1024]]
        wst_free = [None, None]
        wjobs = ([("w1", kb, cc) for cc in range(HID // CW1)
                  for kb in range(FB)] +
                 [("w2", hb, 0) for hb in range(HB)])
        w1cc_ready = {}
        w2_ready = {}
        wstate = {"i": 0}

        def emit_wjobs(k):
            for _ in range(k):
                i = wstate["i"]
                if i >= len(wjobs):
                    return
                wstate["i"] = i + 1
                w, b, cc = wjobs[i]
                sb = i % 2
                if w == "w1":
                    nw = CW1
                    src = w1T[b * 128:(b + 1) * 128, cc * CW1:(cc + 1) * CW1]
                    dst = w1q[:, b * HID + cc * CW1:b * HID + (cc + 1) * CW1]
                else:
                    nw = D
                    src = w2T[b * 128:(b + 1) * 128, 0:D]
                    dst = w2q[:, b * D:(b + 1) * D]
                o('G', lambda e, src=src, d=wst[sb], nw=nw: e.dma_start(
                    d[:, 0:nw], src),
                  waits=[wst_free[sb]], inc=qw[sb], inc_k=16)
                o('V', lambda e: e.engine_nop(), waits=[(qw[sb], qw[sb].n)])
                quantw(dst, wst[sb][:, 0:nw], nw)
                wst_free[sb] = (sV, sV.n)
                if w == "w1" and b == FB - 1:
                    w1cc_ready[cc] = (sV, sV.n)
                elif w == "w2":
                    w2_ready[b] = (sV, sV.n)

        emit_wjobs(FB)

        xts = AF4[:, f_xts:f_xts + FB * ST]
        cbuf = AF4[:, f_c:f_c + QW]
        ubuf = [AF4[:, f_u:f_u + QW], AF4[:, f_u + QW:f_u + 2 * QW]]
        t1b = ubuf[1]
        t2b = ubuf[0]
        gsA = AF4[:, f_gs:f_gs + 4 * ST]
        gsB = cbuf
        xqb = AB[:, c_xqb:c_xqb + FB * ST]
        hqb = AB[:, c_hqb:c_hqb + HB * ST]
        xnst = [AB[:, c_xnst:c_xnst + 4 * ST],
                AB[:, c_xnst + 4 * ST:c_xnst + 8 * ST]]
        sqfb = [AF4[:, f_sq:f_sq + 2 * ST], AF4[:, f_sq + 2 * ST:f_sq + 4 * ST]]
        m_bc = AF4[:, f_mbc:f_mbc + ST]
        r_bc = AF4[:, f_mbc + ST:f_mbc + 2 * ST]

        WQF = min(4, FB)
        def bc4(ap):
            return ap.unsqueeze(1).to_broadcast([128, WQF, ST])

        def pv4(ap2d):
            return ap2d.rearrange("p (f c) -> p f c", f=WQF, c=ST)

        gs_free = [None, None]
        cb_free = None
        dbg_ffn_free = [None]
        xnst_free = [None, None]
        ub_free = [None, None]
        xq_free = None
        hq_free = None
        xnf_free = None
        stats_bank_free = None
        xnst_i = 0
        NQD = HB // 4

        supertiles = [(s, t0) for s in range(BPC) for t0 in range(0, S, ST)]
        o('Q', lambda e: e.dma_start(
            _fv(xts, FB, ST),
            xT[supertiles[0][0], :, supertiles[0][1]:supertiles[0][1] + ST]
            .rearrange("(f p) t -> p f t", p=128)), inc=qx, inc_k=16)
        xts_load = (qx, qx.n)
        sq_mm_done = [None, None]

        for ti, (s, t0) in enumerate(supertiles):
            for fp in range(FB // 2):
                sqi = fp % 2
                o('S', lambda e, fp=fp, sqi=sqi: e.activation(
                    sqfb[sqi], xts[:, fp * 2 * ST:(fp + 1) * 2 * ST],
                    AF.Square), waits=[xts_load, sq_mm_done[sqi]], inc=sS)
                w_t = [xts_load]
                if fp == 0:
                    w_t += [stats_bank_free, bank_free[6], bank_free[7]]
                sq_done = (sS, sS.n)
                for h2 in range(2):
                    fb = fp * 2 + h2
                    o('T', lambda e, fb=fb: e.matmul(
                        ps[6][0:1, 0:ST], ones_cf,
                        xts[:, fb * ST:(fb + 1) * ST],
                        start=(fb == 0), stop=(fb == FB - 1)),
                      waits=(w_t if fb == 0 else []))
                    o('T', lambda e, fb=fb, sqi=sqi, h2=h2: e.matmul(
                        ps[7][0:1, 0:ST], ones_cf,
                        sqfb[sqi][:, h2 * ST:(h2 + 1) * ST],
                        start=(fb == 0), stop=(fb == FB - 1)),
                      waits=([sq_done] if h2 == 0 else []), inc=sT)
                sq_mm_done[sqi] = (sT, sT.n)
            m_row = mv_sb[:, 0:ST]
            r_row = mv_sb[:, ST:2 * ST]
            v_row = mv_sb[:, 2 * ST:3 * ST]
            o('V', lambda e: e.tensor_scalar(
                m_row, ps[6][0:1, 0:ST], 1.0 / D, None, OP.mult),
              waits=[(sT, sT.n)])
            o('V', lambda e: e.tensor_tensor(lnt, m_row, m_row, OP.mult))
            o('V', lambda e: e.scalar_tensor_tensor(
                v_row, ps[7][0:1, 0:ST], 1.0 / D, lnt, OP.mult, OP.subtract),
              inc=sV)
            o('S', lambda e: e.activation(
                r_row, v_row, AF.Sqrt, bias=fc[0:1, 0:1], scale=1.0),
              waits=[(sV, sV.n)], inc=sS)
            o('V', lambda e: e.reciprocal(r_row, r_row),
              waits=[(sS, sS.n)], inc=sV)
            o('T', lambda e: e.matmul(ps[6][:, 0:ST], ones1f, m_row,
                                      start=True, stop=True),
              waits=[(sV, sV.n)])
            o('T', lambda e: e.matmul(ps[7][:, 0:ST], ones1f, r_row,
                                      start=True, stop=True), inc=sT)
            o('S', lambda e: e.copy(m_bc, ps[6][:, 0:ST]),
              waits=[(sT, sT.n), xq_free])
            o('S', lambda e: e.copy(r_bc, ps[7][:, 0:ST]), inc=sS)
            stats_bank_free = (sS, sS.n)
            bcast_done = (sS, sS.n)
            bank_free[6] = (sS, sS.n)
            bank_free[7] = (sS, sS.n)
            W4 = WQF * ST
            for fp in range(FB // WQF):
                fb0 = fp * WQF
                xsl = xts[:, fb0 * ST:fb0 * ST + W4]
                o('V', lambda e, xsl=xsl: e.tensor_tensor(
                    pv4(cbuf[:, 0:W4]), pv4(xsl), bc4(m_bc), OP.subtract),
                  waits=[bcast_done, cb_free,
                         xnf_free if fp == 0 else None])
                o('V', lambda e: e.tensor_tensor(
                    pv4(cbuf[:, 0:W4]), pv4(cbuf[:, 0:W4]), bc4(r_bc),
                    OP.mult))
                xi = xnst_i
                o('V', lambda e, xi=xi: e.tensor_copy(
                    xnst[xi][:, 0:W4], cbuf[:, 0:W4]),
                  waits=[xnst_free[xi]], inc=sV)
                o('Q', lambda e, s=s, t0=t0, fb0=fb0, xi=xi: e.dma_start(
                    xn_scr[s, fb0 * 128:(fb0 + WQF) * 128, t0:t0 + ST]
                    .rearrange("(f p) t -> p f t", p=128),
                    pv4(xnst[xi][:, 0:W4])),
                  waits=[(sV, sV.n)], inc=qn[xi], inc_k=16)
                xnst_free[xi] = (qn[xi], qn[xi].n)
                xnst_i ^= 1
                w_q = [xq_free, dbg_ffn_free[0]] if fp == 0 else []
                o('V', lambda e: e.tensor_scalar(
                    t1b[:, 0:W4], cbuf[:, 0:W4], 16.0, 512.0,
                    OP.mult, OP.add),
                  waits=w_q)
                o('V', lambda e: e.tensor_copy(KI[:, 0:W4], t1b[:, 0:W4]))
                o('V', lambda e: e.tensor_copy(t2b[:, 0:W4], KI[:, 0:W4]))
                o('V', lambda e: e.tensor_tensor(
                    t1b[:, 0:W4], t1b[:, 0:W4], t2b[:, 0:W4], OP.subtract))
                o('V', lambda e: e.tensor_scalar(
                    t1b[:, 0:W4], t1b[:, 0:W4], 0.5, None, OP.is_ge))
                o('V', lambda e: e.tensor_tensor(
                    t1b[:, 0:W4], t1b[:, 0:W4], t2b[:, 0:W4], OP.add))
                o('V', lambda e: e.tensor_scalar(
                    t1b[:, 0:W4], t1b[:, 0:W4], -512.0, 1.0 / 16.0,
                    OP.add, OP.mult))
                o('V', lambda e, fb0=fb0: e.tensor_scalar(
                    xqb[:, fb0 * ST:fb0 * ST + W4], t1b[:, 0:W4],
                    -8.0, 127.0 / 16.0, OP.max, OP.min), inc=sV)
            xq_ready = (sV, sV.n)
            xts_free = (sV, sV.n)
            if ti + 1 < len(supertiles):
                s2, t02 = supertiles[ti + 1]
                o('Q', lambda e, s2=s2, t02=t02: e.dma_start(
                    _fv(xts, FB, ST),
                    xT[s2, :, t02:t02 + ST].rearrange("(f p) t -> p f t", p=128)),
                  waits=[xts_free], inc=qx, inc_k=16)
                xts_load = (qx, qx.n)

            for qd_ in range(NQD):
                pw = qd_ % 2
                PQ = PA if pw == 0 else PB
                w_m = [xq_ready, bank_free[pw * 2], bank_free[pw * 2 + 1]]
                if qd_ == 0 and dbg_ffn_free[0]:
                    w_m.append(dbg_ffn_free[0])
                if ti == 0:
                    w_m += [w1cc_ready[((qd_ + 1) * 512 - 1) // CW1],
                            brows_ready]
                for j4 in range(4):
                    hb = qd_ * 4 + j4
                    for kb in range(FB):
                        o('T', lambda e, hb=hb, kb=kb, j4=j4, PQ=PQ: e.matmul(
                            PQ[:, j4 * ST:(j4 + 1) * ST],
                            w1q[:, kb * HID + hb * 128:
                                kb * HID + (hb + 1) * 128],
                            xqb[:, kb * ST:(kb + 1) * ST],
                            start=(j4 % 2 == 0 and kb == 0), stop=False),
                          waits=(w_m if (j4 == 0 and kb == 0) else []))
                    g1 = 64 * (hb // HBH)
                    o('T', lambda e, hb=hb, j4=j4, PQ=PQ, g1=g1: e.matmul(
                        PQ[:, j4 * ST:(j4 + 1) * ST],
                        b1row(hb), ones_row[g1:g1 + 1, 0:ST],
                        start=False, stop=(j4 % 2 == 1)), inc=sT)
                ub = ubuf[pw]
                o('S', lambda e, PQ=PQ, ub=ub: e.activation(
                    ub, PQ, AF.Identity, bias=fc[:, 2:3], scale=16.0),
                  waits=[(sT, sT.n), ub_free[pw]], inc=sS)
                bank_free[pw * 2] = (sS, sS.n)
                bank_free[pw * 2 + 1] = (sS, sS.n)
                o('V', lambda e, ub=ub: e.tensor_scalar(
                    ub, ub, -MAGIC, None, OP.add), waits=[(sS, sS.n)])
                o('V', lambda e, ub=ub: e.tensor_copy(KI[:, 0:QW], ub),
                  inc=sV)
                ub_free[pw] = (sV, sV.n)
                o('V', lambda e: e.tensor_copy(t1b, KI[:, 0:QW]), inc=sV)
                o('S', lambda e: e.activation(
                    cbuf, t1b, AF.Relu, bias=fc[:, 1:2], scale=1.0 / 16.0),
                  waits=[(sV, sV.n), cb_free if qd_ == 0 else None], inc=sS)
                o('V', lambda e, qd_=qd_: e.tensor_scalar(
                    hqb[:, qd_ * QW:(qd_ + 1) * QW], cbuf, 127.0 / 16.0, None,
                    OP.min),
                  waits=[(sS, sS.n), hq_free if qd_ == 0 else None,
                         dbg_ffn_free[0] if qd_ == 0 else None], inc=sV)
                if ti == 0:
                    emit_wjobs(8)
            hq_ready = (sV, sV.n)
            if DBG:
                o('Q', lambda e, ti=ti: e.dma_start(dbg_hq[ti], hqb),
                  waits=[hq_ready, (qd, qd.n)], inc=qd, inc_k=16)
                o('Q', lambda e, ti=ti: e.dma_start(dbg_xq[ti], xqb),
                  waits=[(qd, qd.n)], inc=qd, inc_k=16)
                dbg_ffn_free[0] = (qd, qd.n)
            xq_free = (sT, sT.n)

            NPA = min(FB, 4)
            NPB = FB - NPA
            for hb in range(HB):
                w_m = [hq_ready, bank_free[0], bank_free[1],
                       bank_free[2], bank_free[3]]
                if ti == 0:
                    w_m.append(w2_ready[hb])
                for db in range(FB):
                    gps = PA if db < 4 else PB
                    c0 = (db % 4) * ST
                    o('T', lambda e, db=db, hb=hb, gps=gps, c0=c0: e.matmul(
                        gps[:, c0:c0 + ST],
                        w2q[:, hb * D + db * 128:hb * D + (db + 1) * 128],
                        hqb[:, hb * ST:(hb + 1) * ST],
                        start=(hb == 0 and db % 2 == 0), stop=False),
                      waits=(w_m if db == 0 else []))
            for db in range(FB):
                gps = PA if db < 4 else PB
                c0 = (db % 4) * ST
                g2 = 64 * (db // FBH)
                o('T', lambda e, db=db, gps=gps, c0=c0, g2=g2: e.matmul(
                    gps[:, c0:c0 + ST],
                    b2row(db), ones_row[g2:g2 + 1, 0:ST], start=False,
                    stop=(db % 2 == 1)),
                  inc=(sT if db in (NPA - 1, FB - 1) else None))
                if db == NPA - 1:
                    paw = (sT, sT.n)
            hq_free = (sT, sT.n)
            o('S', lambda e: e.activation(
                gsA[:, 0:NPA * ST], PA[:, 0:NPA * ST], AF.Silu),
              waits=[paw, gs_free[0]], inc=sS)
            bank_free[0] = (sS, sS.n)
            bank_free[1] = (sS, sS.n)
            o('Q', lambda e, s=s, t0=t0: e.dma_start(
                g_scr[s, 0:NPA * 128, t0:t0 + ST]
                .rearrange("(f p) t -> p f t", p=128),
                gsA[:, 0:NPA * ST].rearrange(
                    "p (f c) -> p f c", f=NPA, c=ST)),
              waits=[(sS, sS.n)], inc=qg[0], inc_k=16)
            gs_free[0] = (qg[0], qg[0].n)
            if NPB:
                o('S', lambda e: e.activation(
                    gsB[:, 0:NPB * ST], PB[:, 0:NPB * ST], AF.Silu),
                  waits=[(sT, sT.n)], inc=sS)
                bank_free[2] = (sS, sS.n)
                bank_free[3] = (sS, sS.n)
                o('Q', lambda e, s=s, t0=t0: e.dma_start(
                    g_scr[s, NPA * 128:FB * 128, t0:t0 + ST]
                    .rearrange("(f p) t -> p f t", p=128),
                    gsB[:, 0:NPB * ST].rearrange(
                        "p (f c) -> p f c", f=NPB, c=ST)),
                  waits=[(sS, sS.n)], inc=qg[1], inc_k=16)
                cb_free = (qg[1], qg[1].n)
            xnf_free = (sS, sS.n)

        a_done_T = (sT, sT.n)
        a_done_S = (sS, sS.n)
        a_done_V = (sV, sV.n)
        a_qdone = [(w.h if False else w, w.n) for w in (qn[0], qn[1], qg[0], qg[1], qg[2], qg[3])]

        wq_sb = AB[:, c_wq + 0 * FB * D:c_wq + 1 * FB * D]
        wk_sb = AB[:, c_wq + 1 * FB * D:c_wq + 2 * FB * D]
        wv_sb = AB[:, c_wq + 2 * FB * D:c_wq + 3 * FB * D]
        wo_sb = AB[:, c_wq + 3 * FB * D:c_wq + 4 * FB * D]
        for wsb, wdr in ((wq_sb, wqT), (wk_sb, wkT), (wv_sb, wvT),
                         (wo_sb, woT)):
            o('G', lambda e, wsb=wsb, wdr=wdr: e.dma_start(
                _fv(wsb, FB, D), wdr.rearrange("(f p) c -> p f c", p=128)),
              waits=[a_done_T, a_done_S, a_done_V] + a_qdone +
                    ([(qd, qd.n)] if DBG else []),
              inc=qa, inc_k=16)
        attw_ready = (qa, qa.n)

        xnc = AB[:, c_xnc:c_xnc + FB * QB]
        kt_sb = AB[:, c_kt:c_kt + NH * S]
        v_sb = AB[:, c_v:c_v + KT * D]
        qt_sb = [AB[:, c_qt:c_qt + NH * QB],
                 AB[:, c_qt + NH * QB:c_qt + 2 * NH * QB]]
        e_sb = [AB[:, c_e:c_e + 2 * QB], AB[:, c_e + 2 * QB:c_e + 4 * QB]]
        sqr_sb = AB[:, c_sqr:c_sqr + QB]
        u_all = AF4[:, f_uall:f_uall + 2 * HPG * QB]
        gs2 = [AF4[:, f_gs2 + i * QB:f_gs2 + (i + 1) * QB] for i in range(2)]
        xts2 = [AF4[:, f_xts2 + i * QB:f_xts2 + (i + 1) * QB] for i in range(2)]
        outst = [AF4[:, f_outst + i * QB:f_outst + (i + 1) * QB]
                 for i in range(2)]
        gs2p = AF4[:, f_gs2:f_gs2 + 2 * QB]
        xts2p = AF4[:, f_xts2:f_xts2 + 2 * QB]
        outstp = AF4[:, f_outst:f_outst + 2 * QB]

        st = {
            "xnc_load": None, "xnc_free": None,
            "qt_free": [None, None], "obf_free": [None, None],
            "obf_dump": [None, None],
            "e_free": [None, None],
            "U_free": None, "D_free": None, "M_free": None,
            "u_free": None, "sq_free": None,
            "gx_free": [None, None], "out_free": [None, None],
            "kv_ready": {},
        }

        def setup_chunk(sq_, c):
            if DBG:
                o('Q', lambda e, sq_=sq_, c=c: e.dma_start(
                    dbg_xc[sq_, c], xnc),
                  waits=[st["xnc_load"], (qd, qd.n)], inc=qd, inc_k=16)
            for hb in range(NH):
                pw = hb % 2
                kps = ps[pw * 2]
                for kb in range(FB):
                    o('T', lambda e, hb=hb, kb=kb, kps=kps: e.matmul(
                        kps[:, 0:QB],
                        wk_sb[:, kb * D + hb * 128:kb * D + (hb + 1) * 128],
                        xnc[:, kb * QB:(kb + 1) * QB],
                        start=(kb == 0), stop=(kb == FB - 1)),
                      waits=([st["xnc_load"], attw_ready, bank_free[pw * 2]]
                             if kb == 0 else []),
                      inc=(sT if kb == FB - 1 else None))
                o('S', lambda e, hb=hb, c=c, kps=kps: e.copy(
                    kt_sb[:, hb * S + c * QB:hb * S + (c + 1) * QB],
                    kps[:, 0:QB]), waits=[(sT, sT.n)], inc=sS)
                bank_free[pw * 2] = (sS, sS.n)
            for tt4 in range(TPQ):
                for half in range(NV):
                    pw = half % 2
                    vps = ps[pw * 2 + 1]
                    for kb in range(FB):
                        o('T', lambda e, tt4=tt4, half=half, kb=kb, vps=vps:
                          e.matmul(
                            vps[:, 0:VCW],
                            xnc[:, kb * QB + tt4 * 128:
                                kb * QB + tt4 * 128 + 128],
                            wv_sb[:, kb * D + half * VCW:
                                  kb * D + half * VCW + VCW],
                            start=(kb == 0), stop=(kb == FB - 1)),
                          waits=([st["xnc_load"], bank_free[pw * 2 + 1]]
                                 if kb == 0 else []),
                          inc=(sT if kb == FB - 1 else None))
                    o('S', lambda e, c=c, tt4=tt4, half=half, vps=vps: e.copy(
                        v_sb[:, (c * TPQ + tt4) * D + half * VCW:
                             (c * TPQ + tt4) * D + half * VCW + VCW],
                        vps[:, 0:VCW]), waits=[(sT, sT.n)], inc=sS)
                    bank_free[pw * 2 + 1] = (sS, sS.n)
            qi = c % 2
            for hb in range(NH):
                pw = hb % 2
                qps = ps[pw * 2]
                w0 = [st["xnc_load"], bank_free[pw * 2]]
                if hb == 0:
                    w0 += [st["qt_free"][qi], st["obf_free"][qi],
                           st["obf_dump"][qi]]
                for kb in range(FB):
                    o('T', lambda e, hb=hb, kb=kb, qps=qps: e.matmul(
                        qps[:, 0:QB],
                        wq_sb[:, kb * D + hb * 128:kb * D + (hb + 1) * 128],
                        xnc[:, kb * QB:(kb + 1) * QB],
                        start=(kb == 0), stop=(kb == FB - 1)),
                      waits=(w0 if kb == 0 else []),
                      inc=(sT if kb == FB - 1 else None))
                o('S', lambda e, hb=hb, qi=qi, qps=qps: e.copy(
                    qt_sb[qi][:, hb * QB:(hb + 1) * QB], qps[:, 0:QB]),
                  waits=[(sT, sT.n)], inc=sS)
                bank_free[pw * 2] = (sS, sS.n)
            st["xnc_free"] = (sT, sT.n)
            st["kv_ready"][c] = (sS, sS.n)

        if DBG:
            o('V', lambda e: e.memset(u_all, 0.0),
              waits=[a_done_V], inc=sV)
            o('V', lambda e: e.memset(AF4[0:8, f_d8:f_d8 + QB], 0.0), inc=sV)
            st["u_free"] = (sV, sV.n)
        for s in range(BPC):
            st["kv_ready"].clear()
            for fb in range(FB):
                o('Q', lambda e, s=s, fb=fb: e.dma_start(
                    xnc[:, fb * QB:(fb + 1) * QB],
                    xn_scr[s, fb * 128:(fb + 1) * 128, 0:QB]),
                  waits=(([st["xnc_free"], a_done_S] + a_qdone)
                         if fb == 0 else []), inc=qc, inc_k=16)
            st["xnc_load"] = (qc, qc.n)
            setup_chunk(s, 0)

            for qb in range(NQB):
                q0 = qb * QB
                nk = q0 // 128 + TPQ
                qi = qb % 2
                o('Q', lambda e, s=s, q0=q0: e.dma_start(
                    gs2p.rearrange("p (f c) -> p f c", f=2, c=QB),
                    g_scr[s, 0:256, q0:q0 + QB]
                    .rearrange("(f p) t -> p f t", p=128)),
                  waits=[st["gx_free"][0], st["gx_free"][1]] + a_qdone,
                  inc=qe[0], inc_k=16)
                o('Q', lambda e, s=s, q0=q0: e.dma_start(
                    xts2p.rearrange("p (f c) -> p f c", f=2, c=QB),
                    xT[s, 0:256, q0:q0 + QB]
                    .rearrange("(f p) t -> p f t", p=128)),
                  inc=qe[0], inc_k=16)
                gx_load = [(qe[0], qe[0].n), (qe[0], qe[0].n)]

                obf = qt_sb[qi]
                if DBG:
                    o('Q', lambda e, s=s, qb=qb, qi=qi: e.dma_start(
                        dbg_q[s, qb], qt_sb[qi]),
                      waits=[st["kv_ready"][qb], (qd, qd.n)], inc=qd, inc_k=16)
                    if qb == NQB - 1:
                        o('Q', lambda e, s=s: e.dma_start(dbg_k[s], kt_sb),
                          waits=[(qd, qd.n)], inc=qd, inc_k=16)
                        o('Q', lambda e, s=s: e.dma_start(dbg_v[s], v_sb),
                          waits=[(qd, qd.n)], inc=qd, inc_k=16)
                for g in range(NG):
                    pend = [None]

                    def flush_pend():
                        if pend[0] is not None:
                            fn = pend[0]
                            pend[0] = None
                            fn()

                    for hl in range(HPG):
                        h = g * HPG + hl
                        for j in range(nk):
                            p = j % 2
                            w_s = [st["kv_ready"][qb], bank_free[p * 2]]
                            diag = j >= nk - TPQ
                            soff = 384 + (q0 - j * 128)
                            for sub in range(2):
                                last = sub == 1 and not diag
                                o('T', lambda e, h=h, j=j, p=p, qi=qi,
                                  sub=sub: e.matmul(
                                    ps[p * 2 + sub][:, 0:QB],
                                    kt_sb[sub * 64:sub * 64 + 64,
                                          h * S + j * 128:h * S + j * 128 + 128],
                                    qt_sb[qi][sub * 64:sub * 64 + 64,
                                              h * QB:(h + 1) * QB],
                                    start=True, stop=not diag),
                                  waits=((w_s + [bank_free[p * 2 + 1]])
                                         if sub == 0 else []),
                                  inc=(sT if last else None))
                                if diag:
                                    o('T', lambda e, p=p, sub=sub, soff=soff:
                                      e.matmul(
                                        ps[p * 2 + sub][:, 0:QB], tril_sb,
                                        mask_sb[:, soff:soff + QB],
                                        start=False, stop=True),
                                      inc=(sT if sub == 1 else None))
                            scores_done = (sT, sT.n)
                            PSpair = PA if p == 0 else PB
                            o('S', lambda e, p=p, PSpair=PSpair: e.activation(
                                e_sb[p], PSpair, AF.Exp, scale=0.125),
                              waits=[scores_done, st["e_free"][p]], inc=sS)
                            bank_free[p * 2] = (sS, sS.n)
                            bank_free[p * 2 + 1] = (sS, sS.n)
                            e_ready = (sS, sS.n)
                            flush_pend()

                            def mkpv(h=h, j=j, p=p, hl=hl, nk=nk,
                                     e_ready=e_ready):
                                def fn():
                                    w_u = [e_ready]
                                    if j == 0:
                                        w_u += [st["U_free"], bank_free[4],
                                                bank_free[5]]
                                    o('T', lambda e: e.matmul(
                                        psU0[:, 0:QB],
                                        v_sb[:, j * D + h * 128:
                                             j * D + (h + 1) * 128],
                                        e_sb[p][:, 0:QB],
                                        start=(j == 0), stop=(j == nk - 1)),
                                      waits=w_u)
                                    o('T', lambda e: e.matmul(
                                        psU1[:, 0:QB],
                                        v_sb[:, j * D + h * 128:
                                             j * D + (h + 1) * 128],
                                        e_sb[p][:, QB:2 * QB],
                                        start=(j == 0), stop=(j == nk - 1)))
                                    m0, m1 = HPG + hl, hl
                                    o('T', lambda e: e.matmul(
                                        psD[0:8, 0:QB],
                                        ind_sb[:, m0 * 8:m0 * 8 + 8],
                                        e_sb[p][:, 0:QB],
                                        start=(hl == 0 and j == 0),
                                        stop=False),
                                      waits=([st["D_free"]]
                                             if (hl == 0 and j == 0) else []))
                                    o('T', lambda e: e.matmul(
                                        psD[0:8, 0:QB],
                                        ind_sb[:, m1 * 8:m1 * 8 + 8],
                                        e_sb[p][:, QB:2 * QB],
                                        start=False,
                                        stop=(hl == HPG - 1 and j == nk - 1)),
                                      inc=sT)
                                    st["e_free"][p] = (sT, sT.n)
                                    if j == nk - 1:
                                        ucol = hl * 2
                                        o('S', lambda e: e.copy(
                                            u_all[:, ucol * QB:
                                                  (ucol + 1) * QB],
                                            psU0[:, 0:QB]),
                                          waits=[(sT, sT.n), st["u_free"]])
                                        o('S', lambda e: e.copy(
                                            u_all[:, (ucol + 1) * QB:
                                                  (ucol + 2) * QB],
                                            psU1[:, 0:QB]), inc=sS)
                                        st["U_free"] = (sS, sS.n)
                                return fn
                            pend[0] = mkpv()
                    flush_pend()
                    o('S', lambda e: e.copy(d8[0:2 * HPG, :],
                                            psD[0:2 * HPG, 0:QB]),
                      waits=[(sT, sT.n)], inc=sS)
                    st["D_free"] = (sS, sS.n)
                    if DBG:
                        o('Q', lambda e, s=s, qb=qb, g=g: e.dma_start(
                            dbg_u[s, qb, g], u_all),
                          waits=[(sS, sS.n), (qd, qd.n)], inc=qd, inc_k=16)
                        o('Q', lambda e, s=s, qb=qb, g=g: e.dma_start(
                            dbg_d[s, qb, g], d8), waits=[(qd, qd.n)],
                          inc=qd, inc_k=16)
                    dbg_done = (qd, qd.n)
                    o('V', lambda e: e.reciprocal(
                        al8[0:2 * HPG, :], d8[0:2 * HPG, :]),
                      waits=[(sS, sS.n)])
                    o('V', lambda e: e.tensor_scalar(
                        al8[0:HPG, :], al8[0:HPG, :],
                        lam_col[0:HPG, 0:1], None, OP.mult), inc=sV)
                    alpha_ready = (sV, sV.n)
                    for hl in range(HPG):
                        h = g * HPG + hl
                        ucol = hl * 2
                        o('T', lambda e, hl=hl: e.matmul(
                            ps[0][:, 0:QB],
                            sel_sb[0:2 * HPG,
                                   (HPG + hl) * 128:(HPG + hl + 1) * 128],
                            al8[0:2 * HPG, :], start=True, stop=True),
                          waits=[alpha_ready, bank_free[0], bank_free[1]])
                        o('T', lambda e, hl=hl: e.matmul(
                            ps[1][:, 0:QB],
                            sel_sb[0:2 * HPG, hl * 128:(hl + 1) * 128],
                            al8[0:2 * HPG, :], start=True, stop=True), inc=sT)
                        bc_a = (sT, sT.n)
                        o('V', lambda e, ucol=ucol: e.tensor_tensor(
                            u_all[:, ucol * QB:(ucol + 2) * QB],
                            u_all[:, ucol * QB:(ucol + 2) * QB],
                            PA, OP.mult),
                          waits=[bc_a] + ([dbg_done] if DBG else []))
                        o('V', lambda e, ucol=ucol: e.tensor_tensor(
                            u_all[:, ucol * QB:(ucol + 1) * QB],
                            u_all[:, ucol * QB:(ucol + 1) * QB],
                            u_all[:, (ucol + 1) * QB:(ucol + 2) * QB],
                            OP.subtract), inc=sV)
                        bank_free[0] = (sV, sV.n)
                        bank_free[1] = (sV, sV.n)
                        o_done = (sV, sV.n)
                        o('S', lambda e, ucol=ucol: e.activation(
                            sqr_sb, u_all[:, ucol * QB:(ucol + 1) * QB],
                            AF.Square), waits=[o_done, st["sq_free"]], inc=sS)
                        w_ms = [(sS, sS.n)]
                        if hl == 0:
                            w_ms.append(st["M_free"])
                        o('T', lambda e, hl=hl: e.matmul(
                            psM[0:8, 0:QB], ind_sb[:, hl * 8:hl * 8 + 8],
                            sqr_sb, start=(hl == 0), stop=(hl == HPG - 1)),
                          waits=w_ms, inc=sT)
                        st["sq_free"] = (sT, sT.n)
                    o('S', lambda e: e.copy(ms8[0:HPG, :], psM[0:HPG, 0:QB]),
                      waits=[(sT, sT.n)], inc=sS)
                    st["M_free"] = (sS, sS.n)
                    o('V', lambda e: e.tensor_scalar(
                        bt8[0:HPG, :], ms8[0:HPG, :], 1.0 / 128.0, float(EPS),
                        OP.mult, OP.add), waits=[(sS, sS.n)], inc=sV)
                    o('S', lambda e: e.activation(
                        bt8[0:HPG, :], bt8[0:HPG, :], AF.Sqrt),
                      waits=[(sV, sV.n)], inc=sS)
                    o('V', lambda e: e.reciprocal(bt8[0:HPG, :], bt8[0:HPG, :]),
                      waits=[(sS, sS.n)], inc=sV)
                    beta_ready = (sV, sV.n)
                    for hl in range(HPG):
                        h = g * HPG + hl
                        ucol = hl * 2
                        o('T', lambda e, hl=hl: e.matmul(
                            ps[2][:, 0:QB],
                            sel_sb[0:HPG, hl * 128:(hl + 1) * 128],
                            bt8[0:HPG, :], start=True, stop=True),
                          waits=[beta_ready, bank_free[2]], inc=sT)
                        bc_b = (sT, sT.n)
                        o('V', lambda e, h=h, ucol=ucol:
                          e.scalar_tensor_tensor(
                            obf[:, h * QB:(h + 1) * QB],
                            u_all[:, ucol * QB:(ucol + 1) * QB],
                            1.0 - LAMBDA_INIT, ps[2][:, 0:QB],
                            OP.mult, OP.mult), waits=[bc_b], inc=sV)
                        bank_free[2] = (sV, sV.n)
                    st["u_free"] = (sV, sV.n)
                obf_done = (sV, sV.n)
                if DBG:
                    o('Q', lambda e, s=s, qb=qb: e.dma_start(
                        dbg_o[s, qb], obf), waits=[obf_done, (qd, qd.n)],
                      inc=qd, inc_k=16)
                    st["obf_dump"][qi] = (qd, qd.n)
                st["qt_free"][qi] = (sT, sT.n)

                for dp in range(FB // 2):
                    db0 = dp * 2
                    for half in range(2):
                        db = db0 + half
                        for kb in range(NH):
                            o('T', lambda e, db=db, kb=kb, half=half:
                              e.matmul(
                                PC[:, half * QB:(half + 1) * QB],
                                wo_sb[:, kb * D + db * 128:
                                      kb * D + (db + 1) * 128],
                                obf[:, kb * QB:(kb + 1) * QB],
                                start=(kb == 0), stop=(kb == NH - 1)),
                              waits=([obf_done, bank_free[4], bank_free[5]]
                                     if (half == 0 and kb == 0) else []),
                              inc=(sT if (half == 1 and kb == NH - 1)
                                   else None))
                    wo_done = (sT, sT.n)
                    o('V', lambda e: e.tensor_tensor(
                        outstp, PC, gs2p, OP.mult),
                      waits=[wo_done, gx_load[0], gx_load[1],
                             st["out_free"][0], st["out_free"][1]])
                    o('V', lambda e: e.tensor_tensor(
                        outstp, outstp, xts2p, OP.add), inc=sV)
                    bank_free[4] = (sV, sV.n)
                    bank_free[5] = (sV, sV.n)
                    st["gx_free"][0] = (sV, sV.n)
                    st["gx_free"][1] = (sV, sV.n)
                    o('Q', lambda e, s=s, q0=q0, dp=dp: e.dma_start(
                        outT[s, dp * 256:(dp + 1) * 256, q0:q0 + QB]
                        .rearrange("(f p) t -> p f t", p=128),
                        outstp.rearrange("p (f c) -> p f c", f=2, c=QB)),
                      waits=[(sV, sV.n)], inc=qo[dp % 2], inc_k=16)
                    st["out_free"][0] = (qo[dp % 2], qo[dp % 2].n)
                    st["out_free"][1] = st["out_free"][0]
                    if db0 + 2 < FB:
                        o('Q', lambda e, s=s, q0=q0, dp=dp: e.dma_start(
                            gs2p.rearrange("p (f c) -> p f c", f=2, c=QB),
                            g_scr[s, (dp + 1) * 256:(dp + 2) * 256,
                                  q0:q0 + QB]
                            .rearrange("(f p) t -> p f t", p=128)),
                          waits=[(sV, sV.n)], inc=qe[dp % 2], inc_k=16)
                        o('Q', lambda e, s=s, q0=q0, dp=dp: e.dma_start(
                            xts2p.rearrange("p (f c) -> p f c", f=2, c=QB),
                            xT[s, (dp + 1) * 256:(dp + 2) * 256, q0:q0 + QB]
                            .rearrange("(f p) t -> p f t", p=128)),
                          inc=qe[dp % 2], inc_k=16)
                        gx_load = [(qe[dp % 2], qe[dp % 2].n)] * 2
                st["U_free"] = (sV, sV.n)
                st["obf_free"][qi] = (sT, sT.n)

                if qb + 1 < NQB:
                    for fb in range(FB):
                        o('Q', lambda e, s=s, qb=qb, fb=fb: e.dma_start(
                            xnc[:, fb * QB:(fb + 1) * QB],
                            xn_scr[s, fb * 128:(fb + 1) * 128,
                                   (qb + 1) * QB:(qb + 2) * QB]),
                          waits=([st["xnc_free"]] if fb == 0 else []),
                          inc=qc, inc_k=16)
                    st["xnc_load"] = (qc, qc.n)
                    setup_chunk(s, qb + 1)

        def _final_waits_q(e):
            for sm in (qx, qc, qn[0], qn[1], qg[0], qg[1], qg[2], qg[3],
                       qe[0], qe[1], qo[0], qo[1], qd):
                if sm.n:
                    e.wait_ge(sm.h, sm.n)
            return None
        def _final_waits_g(e):
            for sm in (qk, qw[0], qw[1], qa):
                if sm.n:
                    e.wait_ge(sm.h, sm.n)
            return None
        o('Q', _final_waits_q)
        o('G', _final_waits_g)

    sems = [sT, sS, sV, qx, qc, qk, qa, qd] + qn + qg + qe + qo + qw
    with nc.Block() as blk:
        def _run(name):
            def body(eng):
                for sm in sems:
                    sm.n = 0
                ctx.begin(name, eng)
                schedule()
            return body
        blk.sync(_run('Q'))
        blk.gpsimd(_run('G'))
        blk.tensor(_run('T'))
        blk.scalar(_run('S'))
        blk.vector(_run('V'))
    return nc


def _host_constants():
    import ml_dtypes
    ramp = (np.arange(896)[None, :] <= np.arange(128)[:, None] + 383)
    ramp = (MASKC * ramp.astype(np.float32)).astype(ml_dtypes.bfloat16)
    tril = (np.arange(128)[:, None] <= np.arange(128)[None, :])
    tril = tril.astype(ml_dtypes.bfloat16)
    ind = np.zeros((128, 64), dtype=np.float32)
    for m in range(8):
        ind[:, m * 8 + m] = 1.0
    sel = np.zeros((8, 8 * 128), dtype=np.float32)
    for m in range(8):
        sel[m, m * 128:(m + 1) * 128] = 1.0
    return ramp, tril, ind.astype(ml_dtypes.bfloat16), sel


def kernel(**inputs):
    cfg = CFG
    BPC = cfg["BPC"]
    x = np.ascontiguousarray(np.asarray(inputs["L0_attn_x"], dtype=np.float32))
    ncores = x.shape[0] // BPC
    mask, tril, ind, sel = _host_constants()
    shared = {
        "wqT": np.ascontiguousarray(np.asarray(inputs["L0_attn_Wq"], np.float32).T),
        "wkT": np.ascontiguousarray(np.asarray(inputs["L0_attn_Wk"], np.float32).T),
        "wvT": np.ascontiguousarray(np.asarray(inputs["L0_attn_Wv"], np.float32).T),
        "woT": np.ascontiguousarray(np.asarray(inputs["L0_attn_Wo"], np.float32).T),
        "w1T": np.ascontiguousarray(np.asarray(inputs["L0_ffn_w1"], np.float32).T),
        "w2T": np.ascontiguousarray(np.asarray(inputs["L0_ffn_w2"], np.float32).T),
        "b1": np.ascontiguousarray(np.asarray(inputs["L0_ffn_b1"], np.float32)),
        "b2": np.ascontiguousarray(np.asarray(inputs["L0_ffn_b2"], np.float32)),
        "lqk": np.ascontiguousarray(np.concatenate([
            np.asarray(inputs["L0_attn_lq1"], np.float32),
            np.asarray(inputs["L0_attn_lk1"], np.float32),
            np.asarray(inputs["L0_attn_lq2"], np.float32),
            np.asarray(inputs["L0_attn_lk2"], np.float32)])[None, :]),
        "mask": mask, "tril": tril, "ind": ind, "sel": sel,
    }
    nc = build_nc(cfg)
    in_maps = []
    for c in range(ncores):
        xs = x[c * BPC:(c + 1) * BPC]
        m = dict(shared)
        m["xT"] = np.ascontiguousarray(xs.transpose(0, 2, 1))
        in_maps.append(m)
    res = run_bass_kernel_spmd(nc, in_maps, core_ids=list(range(ncores)))
    outs = [np.asarray(r["outT"]).transpose(0, 2, 1) for r in res.results]
    return np.ascontiguousarray(np.concatenate(outs, axis=0)).astype(np.float32)


# version 08
# speedup vs baseline: 1.6720x; 1.0349x over previous
import math
import numpy as np

import concourse.bass as bass
import concourse.mybir as mybir
from concourse.bass_utils import run_bass_kernel_spmd

F32 = mybir.dt.float32
BF16 = mybir.dt.bfloat16
I32 = mybir.dt.int32
AF = mybir.ActivationFunctionType
OP = mybir.AluOpType
AX = mybir.AxisListType

LAMBDA_INIT = 0.8 - 0.6 * math.exp(-0.3 * 6)
EPS = 1e-5
MAGIC = 12582912.0
MASKC = -400.0

CFG = dict(D=1024, S=2048, HID=4096, BPC=2, ST=256, QB=512)


class Sem:
    __slots__ = ("h", "n", "name")

    def __init__(self, nc, name):
        self.h = nc.alloc_semaphore(name)
        self.n = 0
        self.name = name


class Ctx:

    def __init__(self):
        self.active = None
        self.eng = None
        self._waited = {}
        self._need_drain = False

    def begin(self, name, eng):
        self.active = name
        self.eng = eng
        self._waited = {}
        self._need_drain = False

    def op(self, engine, emit, waits=(), inc=None, inc_k=1):
        if engine == self.active:
            if self.active in ('V', 'S') and self._need_drain:
                self.eng.drain()
                self._need_drain = False
            for w in waits:
                if w is None:
                    continue
                sem, val = w
                assert val <= sem.n, (
                    f"wait-ahead on {sem.name}: {val} > {sem.n}")
                if val <= 0 or self._waited.get(sem.name, -1) >= val:
                    continue
                self.eng.wait_ge(sem.h, val)
                self._waited[sem.name] = val
            inst = emit(self.eng)
            self._need_drain = True
            if inc is not None:
                if isinstance(inst, list):
                    inst = inst[-1]
                inst.then_inc(inc.h, inc_k)
        if inc is not None:
            inc.n += inc_k


def _fv(ap2d, nblk, blk_cols):
    return ap2d.rearrange("p (f t) -> p f t", f=nblk, t=blk_cols)


def build_nc(cfg):
    D, S, HID, BPC = cfg["D"], cfg["S"], cfg["HID"], cfg["BPC"]
    ST, QB = cfg["ST"], cfg["QB"]
    FB = D // 128
    HB = HID // 128
    NH = D // 128
    NQB = S // QB
    KT = S // 128
    TPQ = QB // 128
    HPG = min(4, NH)
    NG = NH // HPG
    VCW = min(512, D)
    NV = D // VCW
    assert 2 * HPG <= 8 and NH <= 8 and S % QB == 0 and D % VCW == 0

    nc = bass.Bass("TRN2", target_bir_lowering=False, debug=False)

    xT = nc.dram_tensor("xT", [BPC, D, S], F32, kind="ExternalInput").ap()
    wqT = nc.dram_tensor("wqT", [D, D], F32, kind="ExternalInput").ap()
    wkT = nc.dram_tensor("wkT", [D, D], F32, kind="ExternalInput").ap()
    wvT = nc.dram_tensor("wvT", [D, D], F32, kind="ExternalInput").ap()
    woT = nc.dram_tensor("woT", [D, D], F32, kind="ExternalInput").ap()
    w1T = nc.dram_tensor("w1T", [D, HID], F32, kind="ExternalInput").ap()
    w2T = nc.dram_tensor("w2T", [HID, D], F32, kind="ExternalInput").ap()
    b1d = nc.dram_tensor("b1", [HID], F32, kind="ExternalInput").ap()
    b2d = nc.dram_tensor("b2", [D], F32, kind="ExternalInput").ap()
    lqkd = nc.dram_tensor("lqk", [1, 256], F32, kind="ExternalInput").ap()
    maskd = nc.dram_tensor("mask", [128, 896], BF16, kind="ExternalInput").ap()
    trild = nc.dram_tensor("tril", [128, 128], BF16, kind="ExternalInput").ap()
    indd = nc.dram_tensor("ind", [128, 64], BF16, kind="ExternalInput").ap()
    seld = nc.dram_tensor("sel", [8, 8 * 128], F32, kind="ExternalInput").ap()

    outT = nc.dram_tensor("outT", [BPC, D, S], F32, kind="ExternalOutput").ap()
    g_scr = nc.dram_tensor("g_scr", [BPC, D, S], F32, kind="Internal").ap()
    xn_scr = nc.dram_tensor("xn_scr", [BPC, D, S], BF16, kind="Internal").ap()
    b1_scr = nc.dram_tensor("b1_scr", [HID], BF16, kind="Internal").ap()
    b2_scr = nc.dram_tensor("b2_scr", [D], BF16, kind="Internal").ap()
    DBG = bool(cfg.get("debug"))
    if DBG:
        NQB_ = S // QB
        NG_ = (D // 128) // min(4, D // 128)
        HPG_ = min(4, D // 128)
        dbg_u = nc.dram_tensor("dbg_u", [BPC, NQB_, NG_, 128, 2 * HPG_ * QB],
                               F32, kind="ExternalOutput").ap()
        dbg_d = nc.dram_tensor("dbg_d", [BPC, NQB_, NG_, 8, QB], F32,
                               kind="ExternalOutput").ap()
        dbg_o = nc.dram_tensor("dbg_o", [BPC, NQB_, 128, (D // 128) * QB],
                               BF16, kind="ExternalOutput").ap()
        dbg_k = nc.dram_tensor("dbg_k", [BPC, 128, (D // 128) * S], BF16,
                               kind="ExternalOutput").ap()
        dbg_v = nc.dram_tensor("dbg_v", [BPC, 128, (S // 128) * D], BF16,
                               kind="ExternalOutput").ap()
        dbg_q = nc.dram_tensor("dbg_q", [BPC, NQB_, 128, (D // 128) * QB],
                               BF16, kind="ExternalOutput").ap()
        dbg_xc = nc.dram_tensor("dbg_xc", [BPC, NQB_, 128, (D // 128) * QB],
                                BF16, kind="ExternalOutput").ap()
        NT_ = (BPC * S) // ST
        dbg_hq = nc.dram_tensor("dbg_hq", [NT_, 128, (HID // 128) * ST],
                                BF16, kind="ExternalOutput").ap()
        dbg_xq = nc.dram_tensor("dbg_xq", [NT_, 128, (D // 128) * ST],
                                BF16, kind="ExternalOutput").ap()
        dbg_ub = nc.dram_tensor("dbg_ub", [4, 128, ST], F32,
                                kind="ExternalOutput").ap()

    c_w1q = 0
    c_w2q = c_w1q + FB * HID
    c_hqb = c_w2q + HB * D
    c_xqb = c_hqb + HB * ST
    c_xnst = c_xqb + FB * ST
    a_cols = c_xnst + 8 * ST

    c_wq = 0
    c_xnc = c_wq + 4 * FB * D
    c_kt = c_xnc + FB * QB
    c_v = c_kt + NH * S
    c_qt = c_v + KT * D
    c_e = c_qt + 2 * NH * QB
    c_sqr = c_e + 4 * QB
    b_cols = c_sqr + QB
    BFC = max(a_cols, b_cols)

    QW = min(4 * ST, 1024)
    f_xts = 0
    f_c = f_xts + FB * ST
    f_u = f_c + QW
    f_gs = f_u + 2 * QW
    f_sq = f_gs + 4 * ST
    f_mbc = f_sq + 4 * ST
    f_wst = f_mbc + 2 * ST
    a_fcols = f_wst + 1024

    f_uall = 0
    f_gs2 = f_uall + 2 * HPG * QB
    f_xts2 = f_gs2 + 2 * QB
    f_outst = f_xts2 + 2 * QB
    f_d8 = f_outst + 2 * QB
    f_al8 = f_d8 + QB
    f_ms8 = f_al8 + QB
    f_bt8 = f_ms8 + QB
    b_fcols = f_bt8 + QB
    f_mv = a_fcols
    f_lnt = f_mv + ST
    a_fcols = f_mv + 3 * ST
    F32C = max(a_fcols, b_fcols)

    AB = nc.alloc_sbuf_tensor("AB", [128, BFC], BF16).ap()
    AF4 = nc.alloc_sbuf_tensor("AF4", [128, F32C], F32).ap()
    KI = AB[:, c_e:c_e + 2048].bitcast(I32)

    b1h = (HID // 2 + 63) // 64 * 64
    b2h = (D // 2 + 63) // 64 * 64
    scb_w = 896 + 128 + 64 + 16 + ST + 32 + 16 + b1h + b2h
    SCB = nc.alloc_sbuf_tensor("SCB", [128, scb_w], BF16).ap()
    _o = [0]

    def _scb(w):
        a = _o[0]
        _o[0] += w
        return SCB[:, a:a + w]
    mask_sb = _scb(896)
    tril_sb = _scb(128)
    ind_sb = _scb(64)
    ones_bf = _scb(16)[:, 0:1]
    ones_row = _scb(ST)
    b1qb = _scb(32)[:, 0:HB]
    b2qb = _scb(16)[:, 0:FB]
    b1r2 = _scb(b1h)
    b2r2 = _scb(b2h)
    scf_w = 1024 + 16 + 128 + 32 + 16 + 16 + 16 + 16
    SCF = nc.alloc_sbuf_tensor("SCF", [128, scf_w], F32).ap()
    _o[0] = 0

    def _scf(w):
        a = _o[0]
        _o[0] += w
        return SCF[:, a:a + w]
    sel_sb = _scf(1024)[0:8, :]
    ones_cf = _scf(16)[:, 0:1]
    ones1f = _scf(128)[0:1, :]
    b1q = _scf(32)[:, 0:HB]
    b2q = _scf(16)[:, 0:FB]
    lam_sb = _scf(16)[0:1, 0:8]
    lam_col = _scf(16)[:, 0:1]
    fc = _scf(16)[:, 0:8]
    lam_in = AF4[0:1, f_mv:f_mv + 256]
    d8 = AF4[0:8, f_d8:f_d8 + QB]
    al8 = AF4[0:8, f_al8:f_al8 + QB]
    ms8 = AF4[0:8, f_ms8:f_ms8 + QB]
    bt8 = AF4[0:8, f_bt8:f_bt8 + QB]
    mv_sb = AF4[0:1, f_mv:f_mv + 3 * ST]
    lnt = AF4[0:1, f_lnt:f_lnt + ST]

    PA = nc.alloc_psum_tensor("PA", [128, 1024], F32).ap()
    PB = nc.alloc_psum_tensor("PB", [128, 1024], F32).ap()
    PC = nc.alloc_psum_tensor("PC", [128, 1024], F32).ap()
    ps = [PA[:, 0:512], PA[:, 512:1024], PB[:, 0:512], PB[:, 512:1024],
          PC[:, 0:512], PC[:, 512:1024]]
    ps += [nc.alloc_psum_tensor(f"ps{i}", [128, 512], F32).ap()
           for i in range(6, 8)]
    psU0, psU1, psD, psM = ps[4], ps[5], ps[6], ps[7]

    sT = Sem(nc, "sT")
    sS = Sem(nc, "sS")
    sV = Sem(nc, "sV")
    qx = Sem(nc, "qx")
    qn = [Sem(nc, "qn0"), Sem(nc, "qn1")]
    qg = [Sem(nc, f"qg{i}") for i in range(4)]
    qc = Sem(nc, "qc")
    qe = [Sem(nc, "qe0"), Sem(nc, "qe1")]
    qo = [Sem(nc, "qo0"), Sem(nc, "qo1")]
    qk = Sem(nc, "qk")
    qw = [Sem(nc, "qw0"), Sem(nc, "qw1")]
    qa = Sem(nc, "qa")
    qd = Sem(nc, "qd")

    ctx = Ctx()

    bank_free = [None] * 8

    def schedule():
        o = ctx.op
        for i in range(8):
            bank_free[i] = None

        o('G', lambda e: e.dma_start(mask_sb, maskd), inc=qk, inc_k=16)
        o('G', lambda e: e.dma_start(tril_sb, trild), inc=qk, inc_k=16)
        o('G', lambda e: e.dma_start(ind_sb, indd), inc=qk, inc_k=16)
        o('G', lambda e: e.dma_start(sel_sb, seld), inc=qk, inc_k=16)
        o('G', lambda e: e.dma_start(lam_in, lqkd), inc=qk, inc_k=16)
        with nc.allow_non_contiguous_dma("tiny bias loads"):
            o('G', lambda e: e.dma_start(
                b1q, b1d.rearrange("(c p) -> p c", p=128)), inc=qk, inc_k=16)
            o('G', lambda e: e.dma_start(
                b2q, b2d.rearrange("(c p) -> p c", p=128)), inc=qk, inc_k=16)
        const_loaded = (qk, qk.n)
        o('V', lambda e: e.memset(ones_bf, 1.0))
        o('V', lambda e: e.memset(ones_cf, 1.0))
        o('V', lambda e: e.memset(ones_row, 1.0))
        o('V', lambda e: e.memset(fc[:, 0:1], float(EPS)))
        o('V', lambda e: e.memset(fc[:, 1:2], -32.0))
        o('V', lambda e: e.memset(fc[:, 2:3], 512.0 + MAGIC))
        o('V', lambda e: e.memset(fc[:, 3:4], MAGIC))
        o('V', lambda e: e.memset(fc[:, 4:5], -MAGIC / 64.0))
        o('V', lambda e: e.memset(fc[:, 5:6], 512.0))
        o('V', lambda e: e.memset(fc[:, 6:7], -MAGIC))
        o('V', lambda e: e.memset(fc[:, 7:8], -(MAGIC + 512.0) / 16.0))
        o('V', lambda e: e.memset(ones1f, 1.0), inc=sV)
        fc_ready = (sV, sV.n)

        def quantw(dst, src, n, waits=()):
            ii = KI[:, 0:n]
            o('S', lambda e: e.activation(
                ii, src, AF.Identity, bias=fc[:, 3:4], scale=64.0),
              waits=list(waits) + [fc_ready])
            o('S', lambda e: e.activation(
                src, ii, AF.Identity, bias=fc[:, 4:5], scale=1.0 / 64.0),
              inc=sS)
            o('V', lambda e: e.tensor_scalar(
                dst, src, -2.0, 127.0 / 64.0, OP.max, OP.min),
              waits=[(sS, sS.n)], inc=sV)

        quantw(b1qb, b1q, HB, waits=[const_loaded])
        quantw(b2qb, b2q, FB)
        HBH = HB // 2
        FBH = FB // 2

        def b1row(hb):
            g = 64 * (hb // HBH)
            c = (hb % HBH) * 128
            return b1r2[g:g + 1, c:c + 128]

        def b2row(db):
            g = 64 * (db // FBH)
            c = (db % FBH) * 128
            return b2r2[g:g + 1, c:c + 128]

        with nc.allow_non_contiguous_dma("bias row gathers"):
            o('G', lambda e: e.dma_start(
                b1_scr.rearrange("(c p) -> p c", p=128), b1qb),
              waits=[(sV, sV.n)], inc=qk, inc_k=16)
            o('G', lambda e: e.dma_start(
                b2_scr.rearrange("(c p) -> p c", p=128), b2qb),
              inc=qk, inc_k=16)
            b1v = b1_scr.rearrange("(a b) -> a b", a=2)
            b2v = b2_scr.rearrange("(a b) -> a b", a=2)
            o('G', lambda e: e.dma_start(b1r2[0:1, 0:HID // 2], b1v[0:1, :]),
              waits=[(qk, qk.n)], inc=qk, inc_k=16)
            o('G', lambda e: e.dma_start(b1r2[64:65, 0:HID // 2], b1v[1:2, :]),
              inc=qk, inc_k=16)
            o('G', lambda e: e.dma_start(b2r2[0:1, 0:D // 2], b2v[0:1, :]),
              inc=qk, inc_k=16)
            o('G', lambda e: e.dma_start(b2r2[64:65, 0:D // 2], b2v[1:2, :]),
              inc=qk, inc_k=16)
        brows_ready = (qk, qk.n)

        o('V', lambda e: e.tensor_tensor(
            lam_in[:, 0:64], lam_in[:, 0:64], lam_in[:, 64:128], OP.mult))
        o('V', lambda e: e.tensor_tensor(
            lam_in[:, 128:192], lam_in[:, 128:192], lam_in[:, 192:256], OP.mult))
        o('V', lambda e: e.reduce_sum(lam_sb[:, 0:1], lam_in[:, 0:64], axis=AX.X))
        o('V', lambda e: e.reduce_sum(lam_sb[:, 1:2], lam_in[:, 128:192], axis=AX.X),
          inc=sV)
        o('S', lambda e: e.activation(lam_sb[:, 2:3], lam_sb[:, 0:1], AF.Exp),
          waits=[(sV, sV.n)])
        o('S', lambda e: e.activation(lam_sb[:, 3:4], lam_sb[:, 1:2], AF.Exp),
          inc=sS)
        o('V', lambda e: e.tensor_tensor(
            lam_sb[:, 4:5], lam_sb[:, 2:3], lam_sb[:, 3:4], OP.subtract),
          waits=[(sS, sS.n)])
        o('V', lambda e: e.tensor_scalar(
            lam_sb[:, 4:5], lam_sb[:, 4:5], LAMBDA_INIT, None, OP.add), inc=sV)
        o('T', lambda e: e.matmul(ps[0][:, 0:1], ones1f, lam_sb[:, 4:5],
                                  start=True, stop=True),
          waits=[(sV, sV.n)], inc=sT)
        o('S', lambda e: e.copy(lam_col, ps[0][:, 0:1]),
          waits=[(sT, sT.n)], inc=sS)
        bank_free[0] = (sS, sS.n)

        w1q = AB[:, c_w1q:c_w1q + FB * HID]
        w2q = AB[:, c_w2q:c_w2q + HB * D]
        CW1 = min(1024, HID)
        assert D <= 1024
        wst = [AF4[:, f_wst:f_wst + 1024],
               AF4[:, f_gs:f_gs + 1024]]
        wst_free = [None, None]
        wjobs = ([("w1", kb, cc) for cc in range(HID // CW1)
                  for kb in range(FB)] +
                 [("w2", hb, 0) for hb in range(HB)])
        w1cc_ready = {}
        w2_ready = {}
        wstate = {"i": 0}

        def emit_wjobs(k):
            for _ in range(k):
                i = wstate["i"]
                if i >= len(wjobs):
                    return
                wstate["i"] = i + 1
                w, b, cc = wjobs[i]
                sb = i % 2
                if w == "w1":
                    nw = CW1
                    src = w1T[b * 128:(b + 1) * 128, cc * CW1:(cc + 1) * CW1]
                    dst = w1q[:, b * HID + cc * CW1:b * HID + (cc + 1) * CW1]
                else:
                    nw = D
                    src = w2T[b * 128:(b + 1) * 128, 0:D]
                    dst = w2q[:, b * D:(b + 1) * D]
                o('G', lambda e, src=src, d=wst[sb], nw=nw: e.dma_start(
                    d[:, 0:nw], src),
                  waits=[wst_free[sb]], inc=qw[sb], inc_k=16)
                quantw(dst, wst[sb][:, 0:nw], nw,
                       waits=[(qw[sb], qw[sb].n)])
                wst_free[sb] = (sV, sV.n)
                if w == "w1" and b == FB - 1:
                    w1cc_ready[cc] = (sV, sV.n)
                elif w == "w2":
                    w2_ready[b] = (sV, sV.n)

        emit_wjobs(FB)

        xts = AF4[:, f_xts:f_xts + FB * ST]
        cbuf = AF4[:, f_c:f_c + QW]
        ubuf = [AF4[:, f_u:f_u + QW], AF4[:, f_u + QW:f_u + 2 * QW]]
        t1b = ubuf[1]
        t2b = ubuf[0]
        gsA = AF4[:, f_gs:f_gs + 4 * ST]
        gsB = cbuf
        xqb = AB[:, c_xqb:c_xqb + FB * ST]
        hqb = AB[:, c_hqb:c_hqb + HB * ST]
        xnst = [AB[:, c_xnst:c_xnst + 4 * ST],
                AB[:, c_xnst + 4 * ST:c_xnst + 8 * ST]]
        sqfb = [AF4[:, f_sq:f_sq + 2 * ST], AF4[:, f_sq + 2 * ST:f_sq + 4 * ST]]
        m_bc = AF4[:, f_mbc:f_mbc + ST]
        r_bc = AF4[:, f_mbc + ST:f_mbc + 2 * ST]

        WQF = min(4, FB)
        def bc4(ap):
            return ap.unsqueeze(1).to_broadcast([128, WQF, ST])

        def pv4(ap2d):
            return ap2d.rearrange("p (f c) -> p f c", f=WQF, c=ST)

        gs_free = [None, None]
        cb_free = None
        dbg_ffn_free = [None]
        xnst_free = [None, None]
        ub_free = [None, None]
        xq_free = None
        hq_free = None
        xnf_free = None
        stats_bank_free = None
        xnst_i = 0
        NQD = HB // 4

        supertiles = [(s, t0) for s in range(BPC) for t0 in range(0, S, ST)]
        o('Q', lambda e: e.dma_start(
            _fv(xts, FB, ST),
            xT[supertiles[0][0], :, supertiles[0][1]:supertiles[0][1] + ST]
            .rearrange("(f p) t -> p f t", p=128)), inc=qx, inc_k=16)
        xts_load = (qx, qx.n)
        sq_mm_done = [None, None]

        for ti, (s, t0) in enumerate(supertiles):
            for fp in range(FB // 2):
                sqi = fp % 2
                o('S', lambda e, fp=fp, sqi=sqi: e.activation(
                    sqfb[sqi], xts[:, fp * 2 * ST:(fp + 1) * 2 * ST],
                    AF.Square), waits=[xts_load, sq_mm_done[sqi]], inc=sS)
                w_t = [xts_load]
                if fp == 0:
                    w_t += [stats_bank_free, bank_free[6], bank_free[7]]
                sq_done = (sS, sS.n)
                for h2 in range(2):
                    fb = fp * 2 + h2
                    o('T', lambda e, fb=fb: e.matmul(
                        ps[6][0:1, 0:ST], ones_cf,
                        xts[:, fb * ST:(fb + 1) * ST],
                        start=(fb == 0), stop=(fb == FB - 1)),
                      waits=(w_t if fb == 0 else []))
                    o('T', lambda e, fb=fb, sqi=sqi, h2=h2: e.matmul(
                        ps[7][0:1, 0:ST], ones_cf,
                        sqfb[sqi][:, h2 * ST:(h2 + 1) * ST],
                        start=(fb == 0), stop=(fb == FB - 1)),
                      waits=([sq_done] if h2 == 0 else []), inc=sT)
                sq_mm_done[sqi] = (sT, sT.n)
            m_row = mv_sb[:, 0:ST]
            r_row = mv_sb[:, ST:2 * ST]
            v_row = mv_sb[:, 2 * ST:3 * ST]
            o('V', lambda e: e.tensor_scalar(
                m_row, ps[6][0:1, 0:ST], 1.0 / D, None, OP.mult),
              waits=[(sT, sT.n)])
            o('V', lambda e: e.tensor_tensor(lnt, m_row, m_row, OP.mult))
            o('V', lambda e: e.scalar_tensor_tensor(
                v_row, ps[7][0:1, 0:ST], 1.0 / D, lnt, OP.mult, OP.subtract),
              inc=sV)
            o('S', lambda e: e.activation(
                r_row, v_row, AF.Sqrt, bias=fc[0:1, 0:1], scale=1.0),
              waits=[(sV, sV.n)], inc=sS)
            o('V', lambda e: e.reciprocal(r_row, r_row),
              waits=[(sS, sS.n)], inc=sV)
            o('T', lambda e: e.matmul(ps[6][:, 0:ST], ones1f, m_row,
                                      start=True, stop=True),
              waits=[(sV, sV.n)])
            o('T', lambda e: e.matmul(ps[7][:, 0:ST], ones1f, r_row,
                                      start=True, stop=True), inc=sT)
            o('S', lambda e: e.copy(m_bc, ps[6][:, 0:ST]),
              waits=[(sT, sT.n), xq_free])
            o('S', lambda e: e.copy(r_bc, ps[7][:, 0:ST]), inc=sS)
            stats_bank_free = (sS, sS.n)
            bcast_done = (sS, sS.n)
            bank_free[6] = (sS, sS.n)
            bank_free[7] = (sS, sS.n)
            W4 = WQF * ST
            for fp in range(FB // WQF):
                fb0 = fp * WQF
                xsl = xts[:, fb0 * ST:fb0 * ST + W4]
                o('V', lambda e, xsl=xsl: e.tensor_tensor(
                    pv4(cbuf[:, 0:W4]), pv4(xsl), bc4(m_bc), OP.subtract),
                  waits=[bcast_done, cb_free,
                         xnf_free if fp == 0 else None])
                o('V', lambda e: e.tensor_tensor(
                    pv4(cbuf[:, 0:W4]), pv4(cbuf[:, 0:W4]), bc4(r_bc),
                    OP.mult))
                xi = xnst_i
                o('V', lambda e, xi=xi: e.tensor_copy(
                    xnst[xi][:, 0:W4], cbuf[:, 0:W4]),
                  waits=[xnst_free[xi]], inc=sV)
                o('Q', lambda e, s=s, t0=t0, fb0=fb0, xi=xi: e.dma_start(
                    xn_scr[s, fb0 * 128:(fb0 + WQF) * 128, t0:t0 + ST]
                    .rearrange("(f p) t -> p f t", p=128),
                    pv4(xnst[xi][:, 0:W4])),
                  waits=[(sV, sV.n)], inc=qn[xi], inc_k=16)
                xnst_free[xi] = (qn[xi], qn[xi].n)
                xnst_i ^= 1
                w_q = [xq_free, dbg_ffn_free[0]] if fp == 0 else []
                o('S', lambda e: e.activation(
                    t1b[:, 0:W4], cbuf[:, 0:W4], AF.Identity,
                    bias=fc[:, 5:6], scale=16.0), waits=[(sV, sV.n)])
                o('S', lambda e: e.activation(
                    KI[:, 0:W4], t1b[:, 0:W4], AF.Identity, scale=1.0))
                o('S', lambda e: e.activation(
                    t2b[:, 0:W4], KI[:, 0:W4], AF.Identity, scale=1.0),
                  inc=sS)
                o('V', lambda e: e.tensor_tensor(
                    t1b[:, 0:W4], t1b[:, 0:W4], t2b[:, 0:W4], OP.subtract),
                  waits=[(sS, sS.n)])
                o('V', lambda e: e.tensor_scalar(
                    t1b[:, 0:W4], t1b[:, 0:W4], 0.5, None, OP.is_ge))
                o('V', lambda e: e.tensor_tensor(
                    t1b[:, 0:W4], t1b[:, 0:W4], t2b[:, 0:W4], OP.add))
                o('V', lambda e: e.tensor_scalar(
                    t1b[:, 0:W4], t1b[:, 0:W4], -512.0, 1.0 / 16.0,
                    OP.add, OP.mult))
                o('V', lambda e, fb0=fb0: e.tensor_scalar(
                    xqb[:, fb0 * ST:fb0 * ST + W4], t1b[:, 0:W4],
                    -8.0, 127.0 / 16.0, OP.max, OP.min),
                  waits=w_q, inc=sV)
            xq_ready = (sV, sV.n)
            xts_free = (sV, sV.n)
            if ti + 1 < len(supertiles):
                s2, t02 = supertiles[ti + 1]
                o('Q', lambda e, s2=s2, t02=t02: e.dma_start(
                    _fv(xts, FB, ST),
                    xT[s2, :, t02:t02 + ST].rearrange("(f p) t -> p f t", p=128)),
                  waits=[xts_free], inc=qx, inc_k=16)
                xts_load = (qx, qx.n)

            for qd_ in range(NQD):
                pw = qd_ % 2
                PQ = PA if pw == 0 else PB
                w_m = [xq_ready, bank_free[pw * 2], bank_free[pw * 2 + 1]]
                if qd_ == 0 and dbg_ffn_free[0]:
                    w_m.append(dbg_ffn_free[0])
                if ti == 0:
                    w_m += [w1cc_ready[((qd_ + 1) * 512 - 1) // CW1],
                            brows_ready]
                for j4 in range(4):
                    hb = qd_ * 4 + j4
                    for kb in range(FB):
                        o('T', lambda e, hb=hb, kb=kb, j4=j4, PQ=PQ: e.matmul(
                            PQ[:, j4 * ST:(j4 + 1) * ST],
                            w1q[:, kb * HID + hb * 128:
                                kb * HID + (hb + 1) * 128],
                            xqb[:, kb * ST:(kb + 1) * ST],
                            start=(j4 % 2 == 0 and kb == 0), stop=False),
                          waits=(w_m if (j4 == 0 and kb == 0) else []))
                    g1 = 64 * (hb // HBH)
                    o('T', lambda e, hb=hb, j4=j4, PQ=PQ, g1=g1: e.matmul(
                        PQ[:, j4 * ST:(j4 + 1) * ST],
                        b1row(hb), ones_row[g1:g1 + 1, 0:ST],
                        start=False, stop=(j4 % 2 == 1)), inc=sT)
                o('S', lambda e, PQ=PQ: e.activation(
                    KI[:, 0:QW], PQ, AF.Identity, bias=fc[:, 2:3],
                    scale=16.0), waits=[(sT, sT.n)], inc=sS)
                bank_free[pw * 2] = (sS, sS.n)
                bank_free[pw * 2 + 1] = (sS, sS.n)
                o('S', lambda e: e.activation(
                    cbuf, KI[:, 0:QW], AF.Relu, bias=fc[:, 7:8],
                    scale=1.0 / 16.0),
                  waits=[cb_free if qd_ == 0 else None], inc=sS)
                o('V', lambda e, qd_=qd_: e.tensor_scalar(
                    hqb[:, qd_ * QW:(qd_ + 1) * QW], cbuf, 127.0 / 16.0, None,
                    OP.min),
                  waits=[(sS, sS.n), hq_free if qd_ == 0 else None,
                         dbg_ffn_free[0] if qd_ == 0 else None], inc=sV)
                if ti == 0:
                    emit_wjobs(8)
            hq_ready = (sV, sV.n)
            if DBG:
                o('Q', lambda e, ti=ti: e.dma_start(dbg_hq[ti], hqb),
                  waits=[hq_ready, (qd, qd.n)], inc=qd, inc_k=16)
                o('Q', lambda e, ti=ti: e.dma_start(dbg_xq[ti], xqb),
                  waits=[(qd, qd.n)], inc=qd, inc_k=16)
                dbg_ffn_free[0] = (qd, qd.n)
            xq_free = (sT, sT.n)

            NPA = min(FB, 4)
            NPB = FB - NPA
            for hb in range(HB):
                w_m = [hq_ready, bank_free[0], bank_free[1],
                       bank_free[2], bank_free[3]]
                if ti == 0:
                    w_m.append(w2_ready[hb])
                for db in range(FB):
                    gps = PA if db < 4 else PB
                    c0 = (db % 4) * ST
                    o('T', lambda e, db=db, hb=hb, gps=gps, c0=c0: e.matmul(
                        gps[:, c0:c0 + ST],
                        w2q[:, hb * D + db * 128:hb * D + (db + 1) * 128],
                        hqb[:, hb * ST:(hb + 1) * ST],
                        start=(hb == 0 and db % 2 == 0), stop=False),
                      waits=(w_m if db == 0 else []))
            for db in range(FB):
                gps = PA if db < 4 else PB
                c0 = (db % 4) * ST
                g2 = 64 * (db // FBH)
                o('T', lambda e, db=db, gps=gps, c0=c0, g2=g2: e.matmul(
                    gps[:, c0:c0 + ST],
                    b2row(db), ones_row[g2:g2 + 1, 0:ST], start=False,
                    stop=(db % 2 == 1)),
                  inc=(sT if db in (NPA - 1, FB - 1) else None))
                if db == NPA - 1:
                    paw = (sT, sT.n)
            hq_free = (sT, sT.n)
            o('S', lambda e: e.activation(
                gsA[:, 0:NPA * ST], PA[:, 0:NPA * ST], AF.Silu),
              waits=[paw, gs_free[0]], inc=sS)
            bank_free[0] = (sS, sS.n)
            bank_free[1] = (sS, sS.n)
            o('Q', lambda e, s=s, t0=t0: e.dma_start(
                g_scr[s, 0:NPA * 128, t0:t0 + ST]
                .rearrange("(f p) t -> p f t", p=128),
                gsA[:, 0:NPA * ST].rearrange(
                    "p (f c) -> p f c", f=NPA, c=ST)),
              waits=[(sS, sS.n)], inc=qg[0], inc_k=16)
            gs_free[0] = (qg[0], qg[0].n)
            if NPB:
                o('S', lambda e: e.activation(
                    gsB[:, 0:NPB * ST], PB[:, 0:NPB * ST], AF.Silu),
                  waits=[(sT, sT.n)], inc=sS)
                bank_free[2] = (sS, sS.n)
                bank_free[3] = (sS, sS.n)
                o('Q', lambda e, s=s, t0=t0: e.dma_start(
                    g_scr[s, NPA * 128:FB * 128, t0:t0 + ST]
                    .rearrange("(f p) t -> p f t", p=128),
                    gsB[:, 0:NPB * ST].rearrange(
                        "p (f c) -> p f c", f=NPB, c=ST)),
                  waits=[(sS, sS.n)], inc=qg[1], inc_k=16)
                cb_free = (qg[1], qg[1].n)
            xnf_free = (sS, sS.n)

        a_done_T = (sT, sT.n)
        a_done_S = (sS, sS.n)
        a_done_V = (sV, sV.n)
        a_qdone = [(w.h if False else w, w.n) for w in (qn[0], qn[1], qg[0], qg[1], qg[2], qg[3])]

        wq_sb = AB[:, c_wq + 0 * FB * D:c_wq + 1 * FB * D]
        wk_sb = AB[:, c_wq + 1 * FB * D:c_wq + 2 * FB * D]
        wv_sb = AB[:, c_wq + 2 * FB * D:c_wq + 3 * FB * D]
        wo_sb = AB[:, c_wq + 3 * FB * D:c_wq + 4 * FB * D]
        for wsb, wdr in ((wq_sb, wqT), (wk_sb, wkT), (wv_sb, wvT),
                         (wo_sb, woT)):
            o('G', lambda e, wsb=wsb, wdr=wdr: e.dma_start(
                _fv(wsb, FB, D), wdr.rearrange("(f p) c -> p f c", p=128)),
              waits=[a_done_T, a_done_S, a_done_V] + a_qdone +
                    ([(qd, qd.n)] if DBG else []),
              inc=qa, inc_k=16)
        attw_ready = (qa, qa.n)

        xnc = AB[:, c_xnc:c_xnc + FB * QB]
        kt_sb = AB[:, c_kt:c_kt + NH * S]
        v_sb = AB[:, c_v:c_v + KT * D]
        qt_sb = [AB[:, c_qt:c_qt + NH * QB],
                 AB[:, c_qt + NH * QB:c_qt + 2 * NH * QB]]
        e_sb = [AB[:, c_e:c_e + 2 * QB], AB[:, c_e + 2 * QB:c_e + 4 * QB]]
        sqr_sb = AB[:, c_sqr:c_sqr + QB]
        u_all = AF4[:, f_uall:f_uall + 2 * HPG * QB]
        gs2 = [AF4[:, f_gs2 + i * QB:f_gs2 + (i + 1) * QB] for i in range(2)]
        xts2 = [AF4[:, f_xts2 + i * QB:f_xts2 + (i + 1) * QB] for i in range(2)]
        outst = [AF4[:, f_outst + i * QB:f_outst + (i + 1) * QB]
                 for i in range(2)]
        gs2p = AF4[:, f_gs2:f_gs2 + 2 * QB]
        xts2p = AF4[:, f_xts2:f_xts2 + 2 * QB]
        outstp = AF4[:, f_outst:f_outst + 2 * QB]

        st = {
            "xnc_load": None, "xnc_free": None,
            "qt_free": [None, None], "obf_free": [None, None],
            "obf_dump": [None, None],
            "e_free": [None, None],
            "U_free": None, "D_free": None, "M_free": None,
            "u_free": None, "sq_free": None,
            "gx_free": [None, None], "out_free": [None, None],
            "kv_ready": {},
        }

        def setup_chunk(sq_, c):
            if DBG:
                o('Q', lambda e, sq_=sq_, c=c: e.dma_start(
                    dbg_xc[sq_, c], xnc),
                  waits=[st["xnc_load"], (qd, qd.n)], inc=qd, inc_k=16)
            for hb in range(NH):
                pw = hb % 2
                kps = ps[pw * 2]
                for kb in range(FB):
                    o('T', lambda e, hb=hb, kb=kb, kps=kps: e.matmul(
                        kps[:, 0:QB],
                        wk_sb[:, kb * D + hb * 128:kb * D + (hb + 1) * 128],
                        xnc[:, kb * QB:(kb + 1) * QB],
                        start=(kb == 0), stop=(kb == FB - 1)),
                      waits=([st["xnc_load"], attw_ready, bank_free[pw * 2]]
                             if kb == 0 else []),
                      inc=(sT if kb == FB - 1 else None))
                o('S', lambda e, hb=hb, c=c, kps=kps: e.copy(
                    kt_sb[:, hb * S + c * QB:hb * S + (c + 1) * QB],
                    kps[:, 0:QB]), waits=[(sT, sT.n)], inc=sS)
                bank_free[pw * 2] = (sS, sS.n)
            for tt4 in range(TPQ):
                for half in range(NV):
                    pw = half % 2
                    vps = ps[pw * 2 + 1]
                    for kb in range(FB):
                        o('T', lambda e, tt4=tt4, half=half, kb=kb, vps=vps:
                          e.matmul(
                            vps[:, 0:VCW],
                            xnc[:, kb * QB + tt4 * 128:
                                kb * QB + tt4 * 128 + 128],
                            wv_sb[:, kb * D + half * VCW:
                                  kb * D + half * VCW + VCW],
                            start=(kb == 0), stop=(kb == FB - 1)),
                          waits=([st["xnc_load"], bank_free[pw * 2 + 1]]
                                 if kb == 0 else []),
                          inc=(sT if kb == FB - 1 else None))
                    o('S', lambda e, c=c, tt4=tt4, half=half, vps=vps: e.copy(
                        v_sb[:, (c * TPQ + tt4) * D + half * VCW:
                             (c * TPQ + tt4) * D + half * VCW + VCW],
                        vps[:, 0:VCW]), waits=[(sT, sT.n)], inc=sS)
                    bank_free[pw * 2 + 1] = (sS, sS.n)
            qi = c % 2
            for hb in range(NH):
                pw = hb % 2
                qps = ps[pw * 2]
                w0 = [st["xnc_load"], bank_free[pw * 2]]
                if hb == 0:
                    w0 += [st["qt_free"][qi], st["obf_free"][qi],
                           st["obf_dump"][qi]]
                for kb in range(FB):
                    o('T', lambda e, hb=hb, kb=kb, qps=qps: e.matmul(
                        qps[:, 0:QB],
                        wq_sb[:, kb * D + hb * 128:kb * D + (hb + 1) * 128],
                        xnc[:, kb * QB:(kb + 1) * QB],
                        start=(kb == 0), stop=(kb == FB - 1)),
                      waits=(w0 if kb == 0 else []),
                      inc=(sT if kb == FB - 1 else None))
                o('S', lambda e, hb=hb, qi=qi, qps=qps: e.copy(
                    qt_sb[qi][:, hb * QB:(hb + 1) * QB], qps[:, 0:QB]),
                  waits=[(sT, sT.n)], inc=sS)
                bank_free[pw * 2] = (sS, sS.n)
            st["xnc_free"] = (sT, sT.n)
            st["kv_ready"][c] = (sS, sS.n)

        if DBG:
            o('V', lambda e: e.memset(u_all, 0.0),
              waits=[a_done_V], inc=sV)
            o('V', lambda e: e.memset(AF4[0:8, f_d8:f_d8 + QB], 0.0), inc=sV)
            st["u_free"] = (sV, sV.n)
        for s in range(BPC):
            st["kv_ready"].clear()
            for fb in range(FB):
                o('Q', lambda e, s=s, fb=fb: e.dma_start(
                    xnc[:, fb * QB:(fb + 1) * QB],
                    xn_scr[s, fb * 128:(fb + 1) * 128, 0:QB]),
                  waits=(([st["xnc_free"], a_done_S] + a_qdone)
                         if fb == 0 else []), inc=qc, inc_k=16)
            st["xnc_load"] = (qc, qc.n)
            setup_chunk(s, 0)

            for qb in range(NQB):
                q0 = qb * QB
                nk = q0 // 128 + TPQ
                qi = qb % 2
                o('Q', lambda e, s=s, q0=q0: e.dma_start(
                    gs2p.rearrange("p (f c) -> p f c", f=2, c=QB),
                    g_scr[s, 0:256, q0:q0 + QB]
                    .rearrange("(f p) t -> p f t", p=128)),
                  waits=[st["gx_free"][0], st["gx_free"][1]] + a_qdone,
                  inc=qe[0], inc_k=16)
                o('Q', lambda e, s=s, q0=q0: e.dma_start(
                    xts2p.rearrange("p (f c) -> p f c", f=2, c=QB),
                    xT[s, 0:256, q0:q0 + QB]
                    .rearrange("(f p) t -> p f t", p=128)),
                  inc=qe[0], inc_k=16)
                gx_load = [(qe[0], qe[0].n), (qe[0], qe[0].n)]

                obf = qt_sb[qi]
                if DBG:
                    o('Q', lambda e, s=s, qb=qb, qi=qi: e.dma_start(
                        dbg_q[s, qb], qt_sb[qi]),
                      waits=[st["kv_ready"][qb], (qd, qd.n)], inc=qd, inc_k=16)
                    if qb == NQB - 1:
                        o('Q', lambda e, s=s: e.dma_start(dbg_k[s], kt_sb),
                          waits=[(qd, qd.n)], inc=qd, inc_k=16)
                        o('Q', lambda e, s=s: e.dma_start(dbg_v[s], v_sb),
                          waits=[(qd, qd.n)], inc=qd, inc_k=16)
                for g in range(NG):
                    pend = [None]

                    def flush_pend():
                        if pend[0] is not None:
                            fn = pend[0]
                            pend[0] = None
                            fn()

                    for hl in range(HPG):
                        h = g * HPG + hl
                        for j in range(nk):
                            p = j % 2
                            w_s = [st["kv_ready"][qb], bank_free[p * 2]]
                            diag = j >= nk - TPQ
                            soff = 384 + (q0 - j * 128)
                            for sub in range(2):
                                last = sub == 1 and not diag
                                o('T', lambda e, h=h, j=j, p=p, qi=qi,
                                  sub=sub: e.matmul(
                                    ps[p * 2 + sub][:, 0:QB],
                                    kt_sb[sub * 64:sub * 64 + 64,
                                          h * S + j * 128:h * S + j * 128 + 128],
                                    qt_sb[qi][sub * 64:sub * 64 + 64,
                                              h * QB:(h + 1) * QB],
                                    start=True, stop=not diag),
                                  waits=((w_s + [bank_free[p * 2 + 1]])
                                         if sub == 0 else []),
                                  inc=(sT if last else None))
                                if diag:
                                    mw = min(QB, 512 - soff)
                                    o('T', lambda e, p=p, sub=sub, soff=soff,
                                      mw=mw: e.matmul(
                                        ps[p * 2 + sub][:, 0:mw], tril_sb,
                                        mask_sb[:, soff:soff + mw],
                                        start=False, stop=True),
                                      inc=(sT if sub == 1 else None))
                            scores_done = (sT, sT.n)
                            PSpair = PA if p == 0 else PB
                            o('S', lambda e, p=p, PSpair=PSpair: e.activation(
                                e_sb[p], PSpair, AF.Exp, scale=0.125),
                              waits=[scores_done, st["e_free"][p]], inc=sS)
                            bank_free[p * 2] = (sS, sS.n)
                            bank_free[p * 2 + 1] = (sS, sS.n)
                            e_ready = (sS, sS.n)
                            flush_pend()

                            def mkpv(h=h, j=j, p=p, hl=hl, nk=nk,
                                     e_ready=e_ready):
                                def fn():
                                    w_u = [e_ready]
                                    if j == 0:
                                        w_u += [st["U_free"], bank_free[4],
                                                bank_free[5]]
                                    o('T', lambda e: e.matmul(
                                        psU0[:, 0:QB],
                                        v_sb[:, j * D + h * 128:
                                             j * D + (h + 1) * 128],
                                        e_sb[p][:, 0:QB],
                                        start=(j == 0), stop=(j == nk - 1)),
                                      waits=w_u)
                                    o('T', lambda e: e.matmul(
                                        psU1[:, 0:QB],
                                        v_sb[:, j * D + h * 128:
                                             j * D + (h + 1) * 128],
                                        e_sb[p][:, QB:2 * QB],
                                        start=(j == 0), stop=(j == nk - 1)))
                                    m0, m1 = HPG + hl, hl
                                    o('T', lambda e: e.matmul(
                                        psD[0:8, 0:QB],
                                        ind_sb[:, m0 * 8:m0 * 8 + 8],
                                        e_sb[p][:, 0:QB],
                                        start=(hl == 0 and j == 0),
                                        stop=False),
                                      waits=([st["D_free"]]
                                             if (hl == 0 and j == 0) else []))
                                    o('T', lambda e: e.matmul(
                                        psD[0:8, 0:QB],
                                        ind_sb[:, m1 * 8:m1 * 8 + 8],
                                        e_sb[p][:, QB:2 * QB],
                                        start=False,
                                        stop=(hl == HPG - 1 and j == nk - 1)),
                                      inc=sT)
                                    st["e_free"][p] = (sT, sT.n)
                                    if j == nk - 1:
                                        ucol = hl * 2
                                        o('S', lambda e: e.copy(
                                            u_all[:, ucol * QB:
                                                  (ucol + 1) * QB],
                                            psU0[:, 0:QB]),
                                          waits=[(sT, sT.n), st["u_free"]])
                                        o('S', lambda e: e.copy(
                                            u_all[:, (ucol + 1) * QB:
                                                  (ucol + 2) * QB],
                                            psU1[:, 0:QB]), inc=sS)
                                        st["U_free"] = (sS, sS.n)
                                return fn
                            pend[0] = mkpv()
                    flush_pend()
                    o('S', lambda e: e.copy(d8[0:2 * HPG, :],
                                            psD[0:2 * HPG, 0:QB]),
                      waits=[(sT, sT.n)], inc=sS)
                    st["D_free"] = (sS, sS.n)
                    if DBG:
                        o('Q', lambda e, s=s, qb=qb, g=g: e.dma_start(
                            dbg_u[s, qb, g], u_all),
                          waits=[(sS, sS.n), (qd, qd.n)], inc=qd, inc_k=16)
                        o('Q', lambda e, s=s, qb=qb, g=g: e.dma_start(
                            dbg_d[s, qb, g], d8), waits=[(qd, qd.n)],
                          inc=qd, inc_k=16)
                    dbg_done = (qd, qd.n)
                    o('V', lambda e: e.reciprocal(
                        al8[0:2 * HPG, :], d8[0:2 * HPG, :]),
                      waits=[(sS, sS.n)])
                    o('V', lambda e: e.tensor_scalar(
                        al8[0:HPG, :], al8[0:HPG, :],
                        lam_col[0:HPG, 0:1], None, OP.mult), inc=sV)
                    alpha_ready = (sV, sV.n)
                    for hl in range(HPG):
                        h = g * HPG + hl
                        ucol = hl * 2
                        o('T', lambda e, hl=hl: e.matmul(
                            ps[0][:, 0:QB],
                            sel_sb[0:2 * HPG,
                                   (HPG + hl) * 128:(HPG + hl + 1) * 128],
                            al8[0:2 * HPG, :], start=True, stop=True),
                          waits=[alpha_ready, bank_free[0], bank_free[1]])
                        o('T', lambda e, hl=hl: e.matmul(
                            ps[1][:, 0:QB],
                            sel_sb[0:2 * HPG, hl * 128:(hl + 1) * 128],
                            al8[0:2 * HPG, :], start=True, stop=True), inc=sT)
                        bc_a = (sT, sT.n)
                        o('V', lambda e, ucol=ucol: e.tensor_tensor(
                            u_all[:, ucol * QB:(ucol + 2) * QB],
                            u_all[:, ucol * QB:(ucol + 2) * QB],
                            PA, OP.mult),
                          waits=[bc_a] + ([dbg_done] if DBG else []))
                        o('V', lambda e, ucol=ucol: e.tensor_tensor(
                            u_all[:, ucol * QB:(ucol + 1) * QB],
                            u_all[:, ucol * QB:(ucol + 1) * QB],
                            u_all[:, (ucol + 1) * QB:(ucol + 2) * QB],
                            OP.subtract), inc=sV)
                        bank_free[0] = (sV, sV.n)
                        bank_free[1] = (sV, sV.n)
                        o_done = (sV, sV.n)
                        o('S', lambda e, ucol=ucol: e.activation(
                            sqr_sb, u_all[:, ucol * QB:(ucol + 1) * QB],
                            AF.Square), waits=[o_done, st["sq_free"]], inc=sS)
                        w_ms = [(sS, sS.n)]
                        if hl == 0:
                            w_ms.append(st["M_free"])
                        o('T', lambda e, hl=hl: e.matmul(
                            psM[0:8, 0:QB], ind_sb[:, hl * 8:hl * 8 + 8],
                            sqr_sb, start=(hl == 0), stop=(hl == HPG - 1)),
                          waits=w_ms, inc=sT)
                        st["sq_free"] = (sT, sT.n)
                    o('S', lambda e: e.copy(ms8[0:HPG, :], psM[0:HPG, 0:QB]),
                      waits=[(sT, sT.n)], inc=sS)
                    st["M_free"] = (sS, sS.n)
                    o('V', lambda e: e.tensor_scalar(
                        bt8[0:HPG, :], ms8[0:HPG, :], 1.0 / 128.0, float(EPS),
                        OP.mult, OP.add), waits=[(sS, sS.n)], inc=sV)
                    o('S', lambda e: e.activation(
                        bt8[0:HPG, :], bt8[0:HPG, :], AF.Sqrt),
                      waits=[(sV, sV.n)], inc=sS)
                    o('V', lambda e: e.reciprocal(bt8[0:HPG, :], bt8[0:HPG, :]),
                      waits=[(sS, sS.n)], inc=sV)
                    beta_ready = (sV, sV.n)
                    for hl in range(HPG):
                        h = g * HPG + hl
                        ucol = hl * 2
                        o('T', lambda e, hl=hl: e.matmul(
                            ps[2][:, 0:QB],
                            sel_sb[0:HPG, hl * 128:(hl + 1) * 128],
                            bt8[0:HPG, :], start=True, stop=True),
                          waits=[beta_ready, bank_free[2]], inc=sT)
                        bc_b = (sT, sT.n)
                        o('V', lambda e, h=h, ucol=ucol:
                          e.scalar_tensor_tensor(
                            obf[:, h * QB:(h + 1) * QB],
                            u_all[:, ucol * QB:(ucol + 1) * QB],
                            1.0 - LAMBDA_INIT, ps[2][:, 0:QB],
                            OP.mult, OP.mult), waits=[bc_b], inc=sV)
                        bank_free[2] = (sV, sV.n)
                    st["u_free"] = (sV, sV.n)
                obf_done = (sV, sV.n)
                if DBG:
                    o('Q', lambda e, s=s, qb=qb: e.dma_start(
                        dbg_o[s, qb], obf), waits=[obf_done, (qd, qd.n)],
                      inc=qd, inc_k=16)
                    st["obf_dump"][qi] = (qd, qd.n)
                st["qt_free"][qi] = (sT, sT.n)

                for dp in range(FB // 2):
                    db0 = dp * 2
                    for half in range(2):
                        db = db0 + half
                        for kb in range(NH):
                            o('T', lambda e, db=db, kb=kb, half=half:
                              e.matmul(
                                PC[:, half * QB:(half + 1) * QB],
                                wo_sb[:, kb * D + db * 128:
                                      kb * D + (db + 1) * 128],
                                obf[:, kb * QB:(kb + 1) * QB],
                                start=(kb == 0), stop=(kb == NH - 1)),
                              waits=([obf_done, bank_free[4], bank_free[5]]
                                     if (half == 0 and kb == 0) else []),
                              inc=(sT if (half == 1 and kb == NH - 1)
                                   else None))
                    wo_done = (sT, sT.n)
                    o('V', lambda e: e.tensor_tensor(
                        outstp, PC, gs2p, OP.mult),
                      waits=[wo_done, gx_load[0], gx_load[1],
                             st["out_free"][0], st["out_free"][1]])
                    o('V', lambda e: e.tensor_tensor(
                        outstp, outstp, xts2p, OP.add), inc=sV)
                    bank_free[4] = (sV, sV.n)
                    bank_free[5] = (sV, sV.n)
                    st["gx_free"][0] = (sV, sV.n)
                    st["gx_free"][1] = (sV, sV.n)
                    o('Q', lambda e, s=s, q0=q0, dp=dp: e.dma_start(
                        outT[s, dp * 256:(dp + 1) * 256, q0:q0 + QB]
                        .rearrange("(f p) t -> p f t", p=128),
                        outstp.rearrange("p (f c) -> p f c", f=2, c=QB)),
                      waits=[(sV, sV.n)], inc=qo[dp % 2], inc_k=16)
                    st["out_free"][0] = (qo[dp % 2], qo[dp % 2].n)
                    st["out_free"][1] = st["out_free"][0]
                    if db0 + 2 < FB:
                        o('Q', lambda e, s=s, q0=q0, dp=dp: e.dma_start(
                            gs2p.rearrange("p (f c) -> p f c", f=2, c=QB),
                            g_scr[s, (dp + 1) * 256:(dp + 2) * 256,
                                  q0:q0 + QB]
                            .rearrange("(f p) t -> p f t", p=128)),
                          waits=[(sV, sV.n)], inc=qe[dp % 2], inc_k=16)
                        o('Q', lambda e, s=s, q0=q0, dp=dp: e.dma_start(
                            xts2p.rearrange("p (f c) -> p f c", f=2, c=QB),
                            xT[s, (dp + 1) * 256:(dp + 2) * 256, q0:q0 + QB]
                            .rearrange("(f p) t -> p f t", p=128)),
                          inc=qe[dp % 2], inc_k=16)
                        gx_load = [(qe[dp % 2], qe[dp % 2].n)] * 2
                st["U_free"] = (sV, sV.n)
                st["obf_free"][qi] = (sT, sT.n)

                if qb + 1 < NQB:
                    for fb in range(FB):
                        o('Q', lambda e, s=s, qb=qb, fb=fb: e.dma_start(
                            xnc[:, fb * QB:(fb + 1) * QB],
                            xn_scr[s, fb * 128:(fb + 1) * 128,
                                   (qb + 1) * QB:(qb + 2) * QB]),
                          waits=([st["xnc_free"]] if fb == 0 else []),
                          inc=qc, inc_k=16)
                    st["xnc_load"] = (qc, qc.n)
                    setup_chunk(s, qb + 1)

        def _final_waits_q(e):
            for sm in (qx, qc, qn[0], qn[1], qg[0], qg[1], qg[2], qg[3],
                       qe[0], qe[1], qo[0], qo[1], qd):
                if sm.n:
                    e.wait_ge(sm.h, sm.n)
            return None
        def _final_waits_g(e):
            for sm in (qk, qw[0], qw[1], qa):
                if sm.n:
                    e.wait_ge(sm.h, sm.n)
            return None
        o('Q', _final_waits_q)
        o('G', _final_waits_g)

    sems = [sT, sS, sV, qx, qc, qk, qa, qd] + qn + qg + qe + qo + qw
    with nc.Block() as blk:
        def _run(name):
            def body(eng):
                for sm in sems:
                    sm.n = 0
                ctx.begin(name, eng)
                schedule()
            return body
        blk.sync(_run('Q'))
        blk.gpsimd(_run('G'))
        blk.tensor(_run('T'))
        blk.scalar(_run('S'))
        blk.vector(_run('V'))
    return nc


def _host_constants():
    import ml_dtypes
    ramp = (np.arange(896)[None, :] <= np.arange(128)[:, None] + 383)
    ramp = (MASKC * ramp.astype(np.float32)).astype(ml_dtypes.bfloat16)
    tril = (np.arange(128)[:, None] <= np.arange(128)[None, :])
    tril = tril.astype(ml_dtypes.bfloat16)
    ind = np.zeros((128, 64), dtype=np.float32)
    for m in range(8):
        ind[:, m * 8 + m] = 1.0
    sel = np.zeros((8, 8 * 128), dtype=np.float32)
    for m in range(8):
        sel[m, m * 128:(m + 1) * 128] = 1.0
    return ramp, tril, ind.astype(ml_dtypes.bfloat16), sel


def kernel(**inputs):
    cfg = CFG
    BPC = cfg["BPC"]
    x = np.ascontiguousarray(np.asarray(inputs["L0_attn_x"], dtype=np.float32))
    ncores = x.shape[0] // BPC
    mask, tril, ind, sel = _host_constants()
    shared = {
        "wqT": np.ascontiguousarray(np.asarray(inputs["L0_attn_Wq"], np.float32).T),
        "wkT": np.ascontiguousarray(np.asarray(inputs["L0_attn_Wk"], np.float32).T),
        "wvT": np.ascontiguousarray(np.asarray(inputs["L0_attn_Wv"], np.float32).T),
        "woT": np.ascontiguousarray(np.asarray(inputs["L0_attn_Wo"], np.float32).T),
        "w1T": np.ascontiguousarray(np.asarray(inputs["L0_ffn_w1"], np.float32).T),
        "w2T": np.ascontiguousarray(np.asarray(inputs["L0_ffn_w2"], np.float32).T),
        "b1": np.ascontiguousarray(np.asarray(inputs["L0_ffn_b1"], np.float32)),
        "b2": np.ascontiguousarray(np.asarray(inputs["L0_ffn_b2"], np.float32)),
        "lqk": np.ascontiguousarray(np.concatenate([
            np.asarray(inputs["L0_attn_lq1"], np.float32),
            np.asarray(inputs["L0_attn_lk1"], np.float32),
            np.asarray(inputs["L0_attn_lq2"], np.float32),
            np.asarray(inputs["L0_attn_lk2"], np.float32)])[None, :]),
        "mask": mask, "tril": tril, "ind": ind, "sel": sel,
    }
    nc = build_nc(cfg)
    in_maps = []
    for c in range(ncores):
        xs = x[c * BPC:(c + 1) * BPC]
        m = dict(shared)
        m["xT"] = np.ascontiguousarray(xs.transpose(0, 2, 1))
        in_maps.append(m)
    res = run_bass_kernel_spmd(nc, in_maps, core_ids=list(range(ncores)))
    outs = [np.asarray(r["outT"]).transpose(0, 2, 1) for r in res.results]
    return np.ascontiguousarray(np.concatenate(outs, axis=0)).astype(np.float32)


# version 09
# speedup vs baseline: 1.8144x; 1.0852x over previous
import math
import numpy as np

import concourse.bass as bass
import concourse.mybir as mybir
from concourse.bass_utils import run_bass_kernel_spmd

F32 = mybir.dt.float32
BF16 = mybir.dt.bfloat16
I32 = mybir.dt.int32
AF = mybir.ActivationFunctionType
OP = mybir.AluOpType
AX = mybir.AxisListType

LAMBDA_INIT = 0.8 - 0.6 * math.exp(-0.3 * 6)
EPS = 1e-5
MAGIC = 12582912.0
MASKC = -400.0

CFG = dict(D=1024, S=2048, HID=4096, BPC=2, ST=256, QB=512)


class Sem:
    __slots__ = ("h", "n", "name")

    def __init__(self, nc, name):
        self.h = nc.alloc_semaphore(name)
        self.n = 0
        self.name = name


class Ctx:

    def __init__(self):
        self.active = None
        self.eng = None
        self._waited = {}
        self._need_drain = False

    def begin(self, name, eng):
        self.active = name
        self.eng = eng
        self._waited = {}
        self._need_drain = False

    def op(self, engine, emit, waits=(), inc=None, inc_k=1):
        if engine == self.active:
            if self.active in ('V', 'S') and self._need_drain:
                self.eng.drain()
                self._need_drain = False
            for w in waits:
                if w is None:
                    continue
                sem, val = w
                assert val <= sem.n, (
                    f"wait-ahead on {sem.name}: {val} > {sem.n}")
                if val <= 0 or self._waited.get(sem.name, -1) >= val:
                    continue
                self.eng.wait_ge(sem.h, val)
                self._waited[sem.name] = val
            inst = emit(self.eng)
            self._need_drain = True
            if inc is not None:
                if isinstance(inst, list):
                    inst = inst[-1]
                inst.then_inc(inc.h, inc_k)
        if inc is not None:
            inc.n += inc_k


def _fv(ap2d, nblk, blk_cols):
    return ap2d.rearrange("p (f t) -> p f t", f=nblk, t=blk_cols)


def build_nc(cfg):
    D, S, HID, BPC = cfg["D"], cfg["S"], cfg["HID"], cfg["BPC"]
    ST, QB = cfg["ST"], cfg["QB"]
    FB = D // 128
    HB = HID // 128
    NH = D // 128
    NQB = S // QB
    KT = S // 128
    TPQ = QB // 128
    HPG = min(4, NH)
    NG = NH // HPG
    VCW = min(512, D)
    NV = D // VCW
    assert 2 * HPG <= 8 and NH <= 8 and S % QB == 0 and D % VCW == 0

    nc = bass.Bass("TRN2", target_bir_lowering=False, debug=False)

    xT = nc.dram_tensor("xT", [BPC, D, S], F32, kind="ExternalInput").ap()
    wqT = nc.dram_tensor("wqT", [D, D], F32, kind="ExternalInput").ap()
    wkT = nc.dram_tensor("wkT", [D, D], F32, kind="ExternalInput").ap()
    wvT = nc.dram_tensor("wvT", [D, D], F32, kind="ExternalInput").ap()
    woT = nc.dram_tensor("woT", [D, D], F32, kind="ExternalInput").ap()
    w1T = nc.dram_tensor("w1T", [D, HID], F32, kind="ExternalInput").ap()
    w2T = nc.dram_tensor("w2T", [HID, D], F32, kind="ExternalInput").ap()
    b1d = nc.dram_tensor("b1", [HID], F32, kind="ExternalInput").ap()
    b2d = nc.dram_tensor("b2", [D], F32, kind="ExternalInput").ap()
    lqkd = nc.dram_tensor("lqk", [1, 256], F32, kind="ExternalInput").ap()
    maskd = nc.dram_tensor("mask", [128, 896], BF16, kind="ExternalInput").ap()
    trild = nc.dram_tensor("tril", [128, 128], BF16, kind="ExternalInput").ap()
    indd = nc.dram_tensor("ind", [128, 64], BF16, kind="ExternalInput").ap()
    seld = nc.dram_tensor("sel", [8, 8 * 128], F32, kind="ExternalInput").ap()

    outT = nc.dram_tensor("outT", [BPC, D, S], F32, kind="ExternalOutput").ap()
    g_scr = nc.dram_tensor("g_scr", [BPC, D, S], F32, kind="Internal").ap()
    xn_scr = nc.dram_tensor("xn_scr", [BPC, D, S], BF16, kind="Internal").ap()
    b1_scr = nc.dram_tensor("b1_scr", [HID], BF16, kind="Internal").ap()
    b2_scr = nc.dram_tensor("b2_scr", [D], BF16, kind="Internal").ap()
    DBG = bool(cfg.get("debug"))
    if DBG:
        NQB_ = S // QB
        NG_ = (D // 128) // min(4, D // 128)
        HPG_ = min(4, D // 128)
        dbg_u = nc.dram_tensor("dbg_u", [BPC, NQB_, NG_, 128, 2 * HPG_ * QB],
                               F32, kind="ExternalOutput").ap()
        dbg_d = nc.dram_tensor("dbg_d", [BPC, NQB_, NG_, 8, QB], F32,
                               kind="ExternalOutput").ap()
        dbg_o = nc.dram_tensor("dbg_o", [BPC, NQB_, 128, (D // 128) * QB],
                               BF16, kind="ExternalOutput").ap()
        dbg_k = nc.dram_tensor("dbg_k", [BPC, 128, (D // 128) * S], BF16,
                               kind="ExternalOutput").ap()
        dbg_v = nc.dram_tensor("dbg_v", [BPC, 128, (S // 128) * D], BF16,
                               kind="ExternalOutput").ap()
        dbg_q = nc.dram_tensor("dbg_q", [BPC, NQB_, 128, (D // 128) * QB],
                               BF16, kind="ExternalOutput").ap()
        dbg_xc = nc.dram_tensor("dbg_xc", [BPC, NQB_, 128, (D // 128) * QB],
                                BF16, kind="ExternalOutput").ap()
        NT_ = (BPC * S) // ST
        dbg_hq = nc.dram_tensor("dbg_hq", [NT_, 128, (HID // 128) * ST],
                                BF16, kind="ExternalOutput").ap()
        dbg_xq = nc.dram_tensor("dbg_xq", [NT_, 128, (D // 128) * ST],
                                BF16, kind="ExternalOutput").ap()
        dbg_ub = nc.dram_tensor("dbg_ub", [4, 128, ST], F32,
                                kind="ExternalOutput").ap()

    c_w1q = 0
    c_w2q = c_w1q + FB * HID
    c_hqb = c_w2q + HB * D
    c_xqb = c_hqb + HB * ST
    c_xnst = c_xqb + FB * ST
    a_cols = c_xnst + 8 * ST

    c_wq = 0
    c_xnc = c_wq + 4 * FB * D
    c_kt = c_xnc + FB * QB
    c_v = c_kt + NH * S
    c_qt = c_v + KT * D
    c_e = c_qt + 2 * NH * QB
    c_sqr = c_e + 4 * QB
    b_cols = c_sqr + QB
    BFC = max(a_cols, b_cols)

    QW = min(4 * ST, 1024)
    f_xts = 0
    f_c = f_xts + FB * ST
    f_u = f_c + QW
    f_gs = f_u + 2 * QW
    f_sq = f_gs + 4 * ST
    f_mbc = f_sq + 4 * ST
    f_wst = f_mbc + 2 * ST
    a_fcols = f_wst + 1024

    f_uall = 0
    f_gs2 = f_uall + 2 * HPG * QB
    f_xts2 = f_gs2 + 2 * QB
    f_outst = f_xts2 + 2 * QB
    f_d8 = f_outst + 2 * QB
    f_al8 = f_d8 + QB
    f_ms8 = f_al8 + QB
    f_bt8 = f_ms8 + QB
    b_fcols = f_bt8 + QB
    f_mv = a_fcols
    f_lnt = f_mv + ST
    a_fcols = f_mv + 3 * ST
    F32C = max(a_fcols, b_fcols)

    AB = nc.alloc_sbuf_tensor("AB", [128, BFC], BF16).ap()
    AF4 = nc.alloc_sbuf_tensor("AF4", [128, F32C], F32).ap()
    KI = AB[:, c_e:c_e + 2048].bitcast(I32)

    b1h = (HID // 2 + 63) // 64 * 64
    b2h = (D // 2 + 63) // 64 * 64
    scb_w = 896 + 128 + 64 + 16 + ST + 32 + 16 + b1h + b2h
    SCB = nc.alloc_sbuf_tensor("SCB", [128, scb_w], BF16).ap()
    _o = [0]

    def _scb(w):
        a = _o[0]
        _o[0] += w
        return SCB[:, a:a + w]
    mask_sb = _scb(896)
    tril_sb = _scb(128)
    ind_sb = _scb(64)
    ones_bf = _scb(16)[:, 0:1]
    ones_row = _scb(ST)
    b1qb = _scb(32)[:, 0:HB]
    b2qb = _scb(16)[:, 0:FB]
    b1r2 = _scb(b1h)
    b2r2 = _scb(b2h)
    scf_w = 1024 + 16 + 128 + 32 + 16 + 16 + 16 + 16
    SCF = nc.alloc_sbuf_tensor("SCF", [128, scf_w], F32).ap()
    _o[0] = 0

    def _scf(w):
        a = _o[0]
        _o[0] += w
        return SCF[:, a:a + w]
    sel_sb = _scf(1024)[0:8, :]
    ones_cf = _scf(16)[:, 0:1]
    ones1f = _scf(128)[0:1, :]
    b1q = _scf(32)[:, 0:HB]
    b2q = _scf(16)[:, 0:FB]
    lam_sb = _scf(16)[0:1, 0:8]
    lam_col = _scf(16)[:, 0:1]
    fc = _scf(16)[:, 0:8]
    lam_in = AF4[0:1, f_mv:f_mv + 256]
    d8 = AF4[0:8, f_d8:f_d8 + QB]
    al8 = AF4[0:8, f_al8:f_al8 + QB]
    ms8 = AF4[0:8, f_ms8:f_ms8 + QB]
    bt8 = AF4[0:8, f_bt8:f_bt8 + QB]
    mv_sb = AF4[0:1, f_mv:f_mv + 3 * ST]
    lnt = AF4[0:1, f_lnt:f_lnt + ST]

    PA = nc.alloc_psum_tensor("PA", [128, 1024], F32).ap()
    PB = nc.alloc_psum_tensor("PB", [128, 1024], F32).ap()
    PC = nc.alloc_psum_tensor("PC", [128, 1024], F32).ap()
    ps = [PA[:, 0:512], PA[:, 512:1024], PB[:, 0:512], PB[:, 512:1024],
          PC[:, 0:512], PC[:, 512:1024]]
    ps += [nc.alloc_psum_tensor(f"ps{i}", [128, 512], F32).ap()
           for i in range(6, 8)]
    psU0, psU1, psD, psM = ps[4], ps[5], ps[6], ps[7]

    sT = Sem(nc, "sT")
    sS = Sem(nc, "sS")
    sV = Sem(nc, "sV")
    qx = Sem(nc, "qx")
    qn = [Sem(nc, "qn0"), Sem(nc, "qn1")]
    qg = [Sem(nc, f"qg{i}") for i in range(4)]
    qc = Sem(nc, "qc")
    qe = [Sem(nc, "qe0"), Sem(nc, "qe1")]
    qo = [Sem(nc, "qo0"), Sem(nc, "qo1")]
    qk = Sem(nc, "qk")
    qw = [Sem(nc, "qw0"), Sem(nc, "qw1")]
    qa = Sem(nc, "qa")
    qd = Sem(nc, "qd")

    ctx = Ctx()

    bank_free = [None] * 8

    def schedule():
        o = ctx.op
        for i in range(8):
            bank_free[i] = None

        o('G', lambda e: e.dma_start(mask_sb, maskd), inc=qk, inc_k=16)
        o('G', lambda e: e.dma_start(tril_sb, trild), inc=qk, inc_k=16)
        o('G', lambda e: e.dma_start(ind_sb, indd), inc=qk, inc_k=16)
        o('G', lambda e: e.dma_start(sel_sb, seld), inc=qk, inc_k=16)
        o('G', lambda e: e.dma_start(lam_in, lqkd), inc=qk, inc_k=16)
        with nc.allow_non_contiguous_dma("tiny bias loads"):
            o('G', lambda e: e.dma_start(
                b1q, b1d.rearrange("(c p) -> p c", p=128)), inc=qk, inc_k=16)
            o('G', lambda e: e.dma_start(
                b2q, b2d.rearrange("(c p) -> p c", p=128)), inc=qk, inc_k=16)
        const_loaded = (qk, qk.n)
        o('V', lambda e: e.memset(ones_bf, 1.0))
        o('V', lambda e: e.memset(ones_cf, 1.0))
        o('V', lambda e: e.memset(ones_row, 1.0))
        o('V', lambda e: e.memset(fc[:, 0:1], float(EPS)))
        o('V', lambda e: e.memset(fc[:, 1:2], -32.0))
        o('V', lambda e: e.memset(fc[:, 2:3], 512.0 + MAGIC))
        o('V', lambda e: e.memset(fc[:, 3:4], MAGIC))
        o('V', lambda e: e.memset(fc[:, 4:5], -MAGIC / 64.0))
        o('V', lambda e: e.memset(fc[:, 5:6], 512.0))
        o('V', lambda e: e.memset(fc[:, 6:7], -MAGIC))
        o('V', lambda e: e.memset(fc[:, 7:8], -(MAGIC + 512.0) / 16.0))
        o('V', lambda e: e.memset(ones1f, 1.0), inc=sV)
        fc_ready = (sV, sV.n)

        def quantw(dst, src, n, waits=()):
            ii = KI[:, 0:n]
            o('S', lambda e: e.activation(
                ii, src, AF.Identity, bias=fc[:, 3:4], scale=64.0),
              waits=list(waits) + [fc_ready])
            o('S', lambda e: e.activation(
                src, ii, AF.Identity, bias=fc[:, 4:5], scale=1.0 / 64.0),
              inc=sS)
            o('V', lambda e: e.tensor_scalar(
                dst, src, -2.0, 127.0 / 64.0, OP.max, OP.min),
              waits=[(sS, sS.n)], inc=sV)

        quantw(b1qb, b1q, HB, waits=[const_loaded])
        quantw(b2qb, b2q, FB)
        HBH = HB // 2
        FBH = FB // 2

        def b1row(hb):
            g = 64 * (hb // HBH)
            c = (hb % HBH) * 128
            return b1r2[g:g + 1, c:c + 128]

        def b2row(db):
            g = 64 * (db // FBH)
            c = (db % FBH) * 128
            return b2r2[g:g + 1, c:c + 128]

        with nc.allow_non_contiguous_dma("bias row gathers"):
            o('G', lambda e: e.dma_start(
                b1_scr.rearrange("(c p) -> p c", p=128), b1qb),
              waits=[(sV, sV.n)], inc=qk, inc_k=16)
            o('G', lambda e: e.dma_start(
                b2_scr.rearrange("(c p) -> p c", p=128), b2qb),
              inc=qk, inc_k=16)
            b1v = b1_scr.rearrange("(a b) -> a b", a=2)
            b2v = b2_scr.rearrange("(a b) -> a b", a=2)
            o('G', lambda e: e.dma_start(b1r2[0:1, 0:HID // 2], b1v[0:1, :]),
              waits=[(qk, qk.n)], inc=qk, inc_k=16)
            o('G', lambda e: e.dma_start(b1r2[64:65, 0:HID // 2], b1v[1:2, :]),
              inc=qk, inc_k=16)
            o('G', lambda e: e.dma_start(b2r2[0:1, 0:D // 2], b2v[0:1, :]),
              inc=qk, inc_k=16)
            o('G', lambda e: e.dma_start(b2r2[64:65, 0:D // 2], b2v[1:2, :]),
              inc=qk, inc_k=16)
        brows_ready = (qk, qk.n)

        o('V', lambda e: e.tensor_tensor(
            lam_in[:, 0:64], lam_in[:, 0:64], lam_in[:, 64:128], OP.mult))
        o('V', lambda e: e.tensor_tensor(
            lam_in[:, 128:192], lam_in[:, 128:192], lam_in[:, 192:256], OP.mult))
        o('V', lambda e: e.reduce_sum(lam_sb[:, 0:1], lam_in[:, 0:64], axis=AX.X))
        o('V', lambda e: e.reduce_sum(lam_sb[:, 1:2], lam_in[:, 128:192], axis=AX.X),
          inc=sV)
        o('S', lambda e: e.activation(lam_sb[:, 2:3], lam_sb[:, 0:1], AF.Exp),
          waits=[(sV, sV.n)])
        o('S', lambda e: e.activation(lam_sb[:, 3:4], lam_sb[:, 1:2], AF.Exp),
          inc=sS)
        o('V', lambda e: e.tensor_tensor(
            lam_sb[:, 4:5], lam_sb[:, 2:3], lam_sb[:, 3:4], OP.subtract),
          waits=[(sS, sS.n)])
        o('V', lambda e: e.tensor_scalar(
            lam_sb[:, 4:5], lam_sb[:, 4:5], LAMBDA_INIT, None, OP.add), inc=sV)
        o('T', lambda e: e.matmul(ps[0][:, 0:1], ones1f, lam_sb[:, 4:5],
                                  start=True, stop=True),
          waits=[(sV, sV.n)], inc=sT)
        o('S', lambda e: e.copy(lam_col, ps[0][:, 0:1]),
          waits=[(sT, sT.n)], inc=sS)
        bank_free[0] = (sS, sS.n)

        w1q = AB[:, c_w1q:c_w1q + FB * HID]
        w2q = AB[:, c_w2q:c_w2q + HB * D]
        CW1 = min(1024, HID)
        assert D <= 1024
        wst = [AF4[:, f_wst:f_wst + 1024],
               AF4[:, f_gs:f_gs + 1024]]
        wst_free = [None, None]
        wjobs = ([("w1", kb, cc) for cc in range(HID // CW1)
                  for kb in range(FB)] +
                 [("w2", hb, 0) for hb in range(HB)])
        w1cc_ready = {}
        w2_ready = {}
        wstate = {"i": 0}

        def emit_wjobs(k):
            for _ in range(k):
                i = wstate["i"]
                if i >= len(wjobs):
                    return
                wstate["i"] = i + 1
                w, b, cc = wjobs[i]
                sb = i % 2
                if w == "w1":
                    nw = CW1
                    src = w1T[b * 128:(b + 1) * 128, cc * CW1:(cc + 1) * CW1]
                    dst = w1q[:, b * HID + cc * CW1:b * HID + (cc + 1) * CW1]
                else:
                    nw = D
                    src = w2T[b * 128:(b + 1) * 128, 0:D]
                    dst = w2q[:, b * D:(b + 1) * D]
                o('G', lambda e, src=src, d=wst[sb], nw=nw: e.dma_start(
                    d[:, 0:nw], src),
                  waits=[wst_free[sb]], inc=qw[sb], inc_k=16)
                quantw(dst, wst[sb][:, 0:nw], nw,
                       waits=[(qw[sb], qw[sb].n)])
                wst_free[sb] = (sV, sV.n)
                if w == "w1" and b == FB - 1:
                    w1cc_ready[cc] = (sV, sV.n)
                elif w == "w2":
                    w2_ready[b] = (sV, sV.n)

        emit_wjobs(FB)

        xts = AF4[:, f_xts:f_xts + FB * ST]
        cbuf = AF4[:, f_c:f_c + QW]
        ubuf = [AF4[:, f_u:f_u + QW], AF4[:, f_u + QW:f_u + 2 * QW]]
        t1b = ubuf[1]
        t2b = ubuf[0]
        gsA = AF4[:, f_gs:f_gs + 4 * ST]
        gsB = AF4[:, f_wst:f_wst + 1024]
        xqb = AB[:, c_xqb:c_xqb + FB * ST]
        hqb = AB[:, c_hqb:c_hqb + HB * ST]
        xnst = [AB[:, c_xnst:c_xnst + 4 * ST],
                AB[:, c_xnst + 4 * ST:c_xnst + 8 * ST]]
        sqfb = [AF4[:, f_sq:f_sq + 2 * ST], AF4[:, f_sq + 2 * ST:f_sq + 4 * ST]]
        m_bc = AF4[:, f_mbc:f_mbc + ST]
        r_bc = AF4[:, f_mbc + ST:f_mbc + 2 * ST]

        WQF = min(4, FB)
        def bc4(ap):
            return ap.unsqueeze(1).to_broadcast([128, WQF, ST])

        def pv4(ap2d):
            return ap2d.rearrange("p (f c) -> p f c", f=WQF, c=ST)

        NQD = HB // 4

        supertiles = [(s, t0) for s in range(BPC) for t0 in range(0, S, ST)]
        NT = len(supertiles)
        o('Q', lambda e: e.dma_start(
            _fv(xts, FB, ST),
            xT[supertiles[0][0], :, supertiles[0][1]:supertiles[0][1] + ST]
            .rearrange("(f p) t -> p f t", p=128)), inc=qx, inc_k=16)
        As = {
            "xts_load": (qx, qx.n), "sq_mm_done": [None, None],
            "stats_bank_free": None, "xq_ready": None, "xq_free": None,
            "hq_ready": None, "hq_free": None, "xnst_free": [None, None],
            "xnst_i": 0, "gs_free": None, "gsB_free": None,
            "dbg_free": None,
        }
        W4 = WQF * ST
        NPA = min(FB, 4)
        NPB = FB - NPA

        def emit_ln_xq(ti):
            s, t0 = supertiles[ti]
            for fp in range(FB // 2):
                sqi = fp % 2
                o('S', lambda e, fp=fp, sqi=sqi: e.activation(
                    sqfb[sqi], xts[:, fp * 2 * ST:(fp + 1) * 2 * ST],
                    AF.Square),
                  waits=[As["xts_load"], As["sq_mm_done"][sqi]], inc=sS)
                w_t = [As["xts_load"]]
                if fp == 0:
                    w_t += [As["stats_bank_free"], bank_free[6],
                            bank_free[7]]
                sq_done = (sS, sS.n)
                for h2 in range(2):
                    fb = fp * 2 + h2
                    o('T', lambda e, fb=fb: e.matmul(
                        ps[6][0:1, 0:ST], ones_cf,
                        xts[:, fb * ST:(fb + 1) * ST],
                        start=(fb == 0), stop=(fb == FB - 1)),
                      waits=(w_t if fb == 0 else []))
                    o('T', lambda e, fb=fb, sqi=sqi, h2=h2: e.matmul(
                        ps[7][0:1, 0:ST], ones_cf,
                        sqfb[sqi][:, h2 * ST:(h2 + 1) * ST],
                        start=(fb == 0), stop=(fb == FB - 1)),
                      waits=([sq_done] if h2 == 0 else []), inc=sT)
                As["sq_mm_done"][sqi] = (sT, sT.n)
            m_row = mv_sb[:, 0:ST]
            r_row = mv_sb[:, ST:2 * ST]
            v_row = mv_sb[:, 2 * ST:3 * ST]
            o('V', lambda e: e.tensor_scalar(
                m_row, ps[6][0:1, 0:ST], 1.0 / D, None, OP.mult),
              waits=[(sT, sT.n)])
            o('V', lambda e: e.tensor_tensor(lnt, m_row, m_row, OP.mult))
            o('V', lambda e: e.scalar_tensor_tensor(
                v_row, ps[7][0:1, 0:ST], 1.0 / D, lnt, OP.mult, OP.subtract),
              inc=sV)
            o('S', lambda e: e.activation(
                r_row, v_row, AF.Sqrt, bias=fc[0:1, 0:1], scale=1.0),
              waits=[(sV, sV.n)], inc=sS)
            o('V', lambda e: e.reciprocal(r_row, r_row),
              waits=[(sS, sS.n)], inc=sV)
            o('T', lambda e: e.matmul(ps[6][:, 0:ST], ones1f, m_row,
                                      start=True, stop=True),
              waits=[(sV, sV.n)])
            o('T', lambda e: e.matmul(ps[7][:, 0:ST], ones1f, r_row,
                                      start=True, stop=True), inc=sT)
            o('S', lambda e: e.copy(m_bc, ps[6][:, 0:ST]),
              waits=[(sT, sT.n)])
            o('S', lambda e: e.copy(r_bc, ps[7][:, 0:ST]), inc=sS)
            As["stats_bank_free"] = (sS, sS.n)
            bcast_done = (sS, sS.n)
            bank_free[6] = (sS, sS.n)
            bank_free[7] = (sS, sS.n)
            for fp in range(FB // WQF):
                fb0 = fp * WQF
                xsl = xts[:, fb0 * ST:fb0 * ST + W4]
                o('V', lambda e, xsl=xsl: e.tensor_tensor(
                    pv4(cbuf[:, 0:W4]), pv4(xsl), bc4(m_bc), OP.subtract),
                  waits=[bcast_done])
                o('V', lambda e: e.tensor_tensor(
                    pv4(cbuf[:, 0:W4]), pv4(cbuf[:, 0:W4]), bc4(r_bc),
                    OP.mult))
                xi = As["xnst_i"]
                o('V', lambda e, xi=xi: e.tensor_copy(
                    xnst[xi][:, 0:W4], cbuf[:, 0:W4]),
                  waits=[As["xnst_free"][xi]], inc=sV)
                o('Q', lambda e, s=s, t0=t0, fb0=fb0, xi=xi: e.dma_start(
                    xn_scr[s, fb0 * 128:(fb0 + WQF) * 128, t0:t0 + ST]
                    .rearrange("(f p) t -> p f t", p=128),
                    pv4(xnst[xi][:, 0:W4])),
                  waits=[(sV, sV.n)], inc=qn[xi], inc_k=16)
                As["xnst_free"][xi] = (qn[xi], qn[xi].n)
                As["xnst_i"] ^= 1
                w_q = [As["xq_free"], As["dbg_free"]] if fp == 0 else []
                o('S', lambda e: e.activation(
                    t1b[:, 0:W4], cbuf[:, 0:W4], AF.Identity,
                    bias=fc[:, 5:6], scale=16.0), waits=[(sV, sV.n)])
                o('S', lambda e: e.activation(
                    KI[:, 0:W4], t1b[:, 0:W4], AF.Identity, scale=1.0))
                o('S', lambda e: e.activation(
                    t2b[:, 0:W4], KI[:, 0:W4], AF.Identity, scale=1.0),
                  inc=sS)
                o('V', lambda e: e.tensor_tensor(
                    t1b[:, 0:W4], t1b[:, 0:W4], t2b[:, 0:W4], OP.subtract),
                  waits=[(sS, sS.n)])
                o('V', lambda e: e.tensor_scalar(
                    t1b[:, 0:W4], t1b[:, 0:W4], 0.5, None, OP.is_ge))
                o('V', lambda e: e.tensor_tensor(
                    t1b[:, 0:W4], t1b[:, 0:W4], t2b[:, 0:W4], OP.add))
                o('V', lambda e: e.tensor_scalar(
                    t1b[:, 0:W4], t1b[:, 0:W4], -512.0, 1.0 / 16.0,
                    OP.add, OP.mult))
                o('V', lambda e, fb0=fb0: e.tensor_scalar(
                    xqb[:, fb0 * ST:fb0 * ST + W4], t1b[:, 0:W4],
                    -8.0, 127.0 / 16.0, OP.max, OP.min),
                  waits=w_q, inc=sV)
            As["xq_ready"] = (sV, sV.n)
            if ti + 1 < NT:
                s2, t02 = supertiles[ti + 1]
                o('Q', lambda e, s2=s2, t02=t02: e.dma_start(
                    _fv(xts, FB, ST),
                    xT[s2, :, t02:t02 + ST]
                    .rearrange("(f p) t -> p f t", p=128)),
                  waits=[(sV, sV.n)], inc=qx, inc_k=16)
                As["xts_load"] = (qx, qx.n)

        def emit_ffn1(ti):
            for qd_ in range(NQD):
                pw = qd_ % 2
                PQ = PA if pw == 0 else PB
                w_m = [As["xq_ready"], bank_free[pw * 2],
                       bank_free[pw * 2 + 1]]
                if qd_ == 0 and As["dbg_free"]:
                    w_m.append(As["dbg_free"])
                if ti == 0:
                    w_m += [w1cc_ready[((qd_ + 1) * 512 - 1) // CW1],
                            brows_ready]
                for j4 in range(4):
                    hb = qd_ * 4 + j4
                    for kb in range(FB):
                        o('T', lambda e, hb=hb, kb=kb, j4=j4, PQ=PQ:
                          e.matmul(
                            PQ[:, j4 * ST:(j4 + 1) * ST],
                            w1q[:, kb * HID + hb * 128:
                                kb * HID + (hb + 1) * 128],
                            xqb[:, kb * ST:(kb + 1) * ST],
                            start=(j4 % 2 == 0 and kb == 0), stop=False),
                          waits=(w_m if (j4 == 0 and kb == 0) else []))
                    g1 = 64 * (hb // HBH)
                    o('T', lambda e, hb=hb, j4=j4, PQ=PQ, g1=g1: e.matmul(
                        PQ[:, j4 * ST:(j4 + 1) * ST],
                        b1row(hb), ones_row[g1:g1 + 1, 0:ST],
                        start=False, stop=(j4 % 2 == 1)), inc=sT)
                o('S', lambda e, PQ=PQ: e.activation(
                    KI[:, 0:QW], PQ, AF.Identity, bias=fc[:, 2:3],
                    scale=16.0), waits=[(sT, sT.n)], inc=sS)
                bank_free[pw * 2] = (sS, sS.n)
                bank_free[pw * 2 + 1] = (sS, sS.n)
                o('S', lambda e: e.activation(
                    cbuf, KI[:, 0:QW], AF.Relu, bias=fc[:, 7:8],
                    scale=1.0 / 16.0),
                  waits=[(sV, sV.n)], inc=sS)
                o('V', lambda e, qd_=qd_: e.tensor_scalar(
                    hqb[:, qd_ * QW:(qd_ + 1) * QW], cbuf, 127.0 / 16.0,
                    None, OP.min),
                  waits=[(sS, sS.n), As["hq_free"] if qd_ == 0 else None,
                         As["dbg_free"] if qd_ == 0 else None], inc=sV)
                if ti == 0:
                    emit_wjobs(8)
            As["hq_ready"] = (sV, sV.n)
            if DBG:
                o('Q', lambda e, ti=ti: e.dma_start(dbg_hq[ti], hqb),
                  waits=[As["hq_ready"], (qd, qd.n)], inc=qd, inc_k=16)
                o('Q', lambda e, ti=ti: e.dma_start(dbg_xq[ti], xqb),
                  waits=[(qd, qd.n)], inc=qd, inc_k=16)
                As["dbg_free"] = (qd, qd.n)
            As["xq_free"] = (sT, sT.n)

        def emit_ffn2(ti, xn_gate):
            s, t0 = supertiles[ti]
            for hb in range(HB):
                w_m = [As["hq_ready"], bank_free[0], bank_free[1],
                       bank_free[2], bank_free[3]]
                if ti == 0:
                    w_m.append(w2_ready[hb])
                for db in range(FB):
                    gps = PA if db < 4 else PB
                    c0 = (db % 4) * ST
                    o('T', lambda e, db=db, hb=hb, gps=gps, c0=c0:
                      e.matmul(
                        gps[:, c0:c0 + ST],
                        w2q[:, hb * D + db * 128:hb * D + (db + 1) * 128],
                        hqb[:, hb * ST:(hb + 1) * ST],
                        start=(hb == 0 and db % 2 == 0), stop=False),
                      waits=(w_m if db == 0 else []))
            paw = None
            for db in range(FB):
                gps = PA if db < 4 else PB
                c0 = (db % 4) * ST
                g2 = 64 * (db // FBH)
                o('T', lambda e, db=db, gps=gps, c0=c0, g2=g2: e.matmul(
                    gps[:, c0:c0 + ST],
                    b2row(db), ones_row[g2:g2 + 1, 0:ST], start=False,
                    stop=(db % 2 == 1)),
                  inc=(sT if db in (NPA - 1, FB - 1) else None))
                if db == NPA - 1:
                    paw = (sT, sT.n)
            As["hq_free"] = (sT, sT.n)
            o('S', lambda e: e.activation(
                gsA[:, 0:NPA * ST], PA[:, 0:NPA * ST], AF.Silu),
              waits=[paw, As["gs_free"]], inc=sS)
            bank_free[0] = (sS, sS.n)
            bank_free[1] = (sS, sS.n)
            o('Q', lambda e, s=s, t0=t0: e.dma_start(
                g_scr[s, 0:NPA * 128, t0:t0 + ST]
                .rearrange("(f p) t -> p f t", p=128),
                gsA[:, 0:NPA * ST].rearrange(
                    "p (f c) -> p f c", f=NPA, c=ST)),
              waits=[(sS, sS.n)], inc=qg[0], inc_k=16)
            As["gs_free"] = (qg[0], qg[0].n)
            if NPB:
                o('S', lambda e: e.activation(
                    gsB[:, 0:NPB * ST], PB[:, 0:NPB * ST], AF.Silu),
                  waits=[(sT, sT.n), As["gsB_free"], xn_gate], inc=sS)
                bank_free[2] = (sS, sS.n)
                bank_free[3] = (sS, sS.n)
                o('Q', lambda e, s=s, t0=t0: e.dma_start(
                    g_scr[s, NPA * 128:FB * 128, t0:t0 + ST]
                    .rearrange("(f p) t -> p f t", p=128),
                    gsB[:, 0:NPB * ST].rearrange(
                        "p (f c) -> p f c", f=NPB, c=ST)),
                  waits=[(sS, sS.n)], inc=qg[1], inc_k=16)
                As["gsB_free"] = (qg[1], qg[1].n)

        emit_ln_xq(0)
        for ti in range(NT):
            emit_ffn1(ti)
            if ti + 1 < NT:
                emit_ln_xq(ti + 1)
            emit_ffn2(ti, None)

        a_done_T = (sT, sT.n)
        a_done_S = (sS, sS.n)
        a_done_V = (sV, sV.n)
        a_qdone = [(w.h if False else w, w.n) for w in (qn[0], qn[1], qg[0], qg[1], qg[2], qg[3])]

        wq_sb = AB[:, c_wq + 0 * FB * D:c_wq + 1 * FB * D]
        wk_sb = AB[:, c_wq + 1 * FB * D:c_wq + 2 * FB * D]
        wv_sb = AB[:, c_wq + 2 * FB * D:c_wq + 3 * FB * D]
        wo_sb = AB[:, c_wq + 3 * FB * D:c_wq + 4 * FB * D]
        for wsb, wdr in ((wq_sb, wqT), (wk_sb, wkT), (wv_sb, wvT),
                         (wo_sb, woT)):
            o('G', lambda e, wsb=wsb, wdr=wdr: e.dma_start(
                _fv(wsb, FB, D), wdr.rearrange("(f p) c -> p f c", p=128)),
              waits=[a_done_T, a_done_S, a_done_V] + a_qdone +
                    ([(qd, qd.n)] if DBG else []),
              inc=qa, inc_k=16)
        attw_ready = (qa, qa.n)

        xnc = AB[:, c_xnc:c_xnc + FB * QB]
        kt_sb = AB[:, c_kt:c_kt + NH * S]
        v_sb = AB[:, c_v:c_v + KT * D]
        qt_sb = [AB[:, c_qt:c_qt + NH * QB],
                 AB[:, c_qt + NH * QB:c_qt + 2 * NH * QB]]
        e_sb = [AB[:, c_e:c_e + 2 * QB], AB[:, c_e + 2 * QB:c_e + 4 * QB]]
        sqr_sb = AB[:, c_sqr:c_sqr + QB]
        u_all = AF4[:, f_uall:f_uall + 2 * HPG * QB]
        gs2 = [AF4[:, f_gs2 + i * QB:f_gs2 + (i + 1) * QB] for i in range(2)]
        xts2 = [AF4[:, f_xts2 + i * QB:f_xts2 + (i + 1) * QB] for i in range(2)]
        outst = [AF4[:, f_outst + i * QB:f_outst + (i + 1) * QB]
                 for i in range(2)]
        gs2p = AF4[:, f_gs2:f_gs2 + 2 * QB]
        xts2p = AF4[:, f_xts2:f_xts2 + 2 * QB]
        outstp = AF4[:, f_outst:f_outst + 2 * QB]

        st = {
            "xnc_load": None, "xnc_free": None,
            "qt_free": [None, None], "obf_free": [None, None],
            "obf_dump": [None, None],
            "e_free": [None, None],
            "U_free": None, "D_free": None, "M_free": None,
            "u_free": None, "sq_free": None,
            "gx_free": [None, None], "out_free": [None, None],
            "kv_ready": {},
        }

        def setup_chunk(sq_, c):
            if DBG:
                o('Q', lambda e, sq_=sq_, c=c: e.dma_start(
                    dbg_xc[sq_, c], xnc),
                  waits=[st["xnc_load"], (qd, qd.n)], inc=qd, inc_k=16)
            for hb in range(NH):
                pw = hb % 2
                kps = ps[pw * 2]
                for kb in range(FB):
                    o('T', lambda e, hb=hb, kb=kb, kps=kps: e.matmul(
                        kps[:, 0:QB],
                        wk_sb[:, kb * D + hb * 128:kb * D + (hb + 1) * 128],
                        xnc[:, kb * QB:(kb + 1) * QB],
                        start=(kb == 0), stop=(kb == FB - 1)),
                      waits=([st["xnc_load"], attw_ready, bank_free[pw * 2]]
                             if kb == 0 else []),
                      inc=(sT if kb == FB - 1 else None))
                o('S', lambda e, hb=hb, c=c, kps=kps: e.copy(
                    kt_sb[:, hb * S + c * QB:hb * S + (c + 1) * QB],
                    kps[:, 0:QB]), waits=[(sT, sT.n)], inc=sS)
                bank_free[pw * 2] = (sS, sS.n)
            for tt4 in range(TPQ):
                for half in range(NV):
                    pw = half % 2
                    vps = ps[pw * 2 + 1]
                    for kb in range(FB):
                        o('T', lambda e, tt4=tt4, half=half, kb=kb, vps=vps:
                          e.matmul(
                            vps[:, 0:VCW],
                            xnc[:, kb * QB + tt4 * 128:
                                kb * QB + tt4 * 128 + 128],
                            wv_sb[:, kb * D + half * VCW:
                                  kb * D + half * VCW + VCW],
                            start=(kb == 0), stop=(kb == FB - 1)),
                          waits=([st["xnc_load"], bank_free[pw * 2 + 1]]
                                 if kb == 0 else []),
                          inc=(sT if kb == FB - 1 else None))
                    o('S', lambda e, c=c, tt4=tt4, half=half, vps=vps: e.copy(
                        v_sb[:, (c * TPQ + tt4) * D + half * VCW:
                             (c * TPQ + tt4) * D + half * VCW + VCW],
                        vps[:, 0:VCW]), waits=[(sT, sT.n)], inc=sS)
                    bank_free[pw * 2 + 1] = (sS, sS.n)
            qi = c % 2
            for hb in range(NH):
                pw = hb % 2
                qps = ps[pw * 2]
                w0 = [st["xnc_load"], bank_free[pw * 2]]
                if hb == 0:
                    w0 += [st["qt_free"][qi], st["obf_free"][qi],
                           st["obf_dump"][qi]]
                for kb in range(FB):
                    o('T', lambda e, hb=hb, kb=kb, qps=qps: e.matmul(
                        qps[:, 0:QB],
                        wq_sb[:, kb * D + hb * 128:kb * D + (hb + 1) * 128],
                        xnc[:, kb * QB:(kb + 1) * QB],
                        start=(kb == 0), stop=(kb == FB - 1)),
                      waits=(w0 if kb == 0 else []),
                      inc=(sT if kb == FB - 1 else None))
                o('S', lambda e, hb=hb, qi=qi, qps=qps: e.copy(
                    qt_sb[qi][:, hb * QB:(hb + 1) * QB], qps[:, 0:QB]),
                  waits=[(sT, sT.n)], inc=sS)
                bank_free[pw * 2] = (sS, sS.n)
            st["xnc_free"] = (sT, sT.n)
            st["kv_ready"][c] = (sS, sS.n)

        if DBG:
            o('V', lambda e: e.memset(u_all, 0.0),
              waits=[a_done_V], inc=sV)
            o('V', lambda e: e.memset(AF4[0:8, f_d8:f_d8 + QB], 0.0), inc=sV)
            st["u_free"] = (sV, sV.n)
        for s in range(BPC):
            st["kv_ready"].clear()
            for fb in range(FB):
                o('Q', lambda e, s=s, fb=fb: e.dma_start(
                    xnc[:, fb * QB:(fb + 1) * QB],
                    xn_scr[s, fb * 128:(fb + 1) * 128, 0:QB]),
                  waits=(([st["xnc_free"], a_done_S] + a_qdone)
                         if fb == 0 else []), inc=qc, inc_k=16)
            st["xnc_load"] = (qc, qc.n)
            setup_chunk(s, 0)

            for qb in range(NQB):
                q0 = qb * QB
                nk = q0 // 128 + TPQ
                qi = qb % 2
                o('Q', lambda e, s=s, q0=q0: e.dma_start(
                    gs2p.rearrange("p (f c) -> p f c", f=2, c=QB),
                    g_scr[s, 0:256, q0:q0 + QB]
                    .rearrange("(f p) t -> p f t", p=128)),
                  waits=[st["gx_free"][0], st["gx_free"][1]] + a_qdone,
                  inc=qe[0], inc_k=16)
                o('Q', lambda e, s=s, q0=q0: e.dma_start(
                    xts2p.rearrange("p (f c) -> p f c", f=2, c=QB),
                    xT[s, 0:256, q0:q0 + QB]
                    .rearrange("(f p) t -> p f t", p=128)),
                  inc=qe[0], inc_k=16)
                gx_load = [(qe[0], qe[0].n), (qe[0], qe[0].n)]

                obf = qt_sb[qi]
                if DBG:
                    o('Q', lambda e, s=s, qb=qb, qi=qi: e.dma_start(
                        dbg_q[s, qb], qt_sb[qi]),
                      waits=[st["kv_ready"][qb], (qd, qd.n)], inc=qd, inc_k=16)
                    if qb == NQB - 1:
                        o('Q', lambda e, s=s: e.dma_start(dbg_k[s], kt_sb),
                          waits=[(qd, qd.n)], inc=qd, inc_k=16)
                        o('Q', lambda e, s=s: e.dma_start(dbg_v[s], v_sb),
                          waits=[(qd, qd.n)], inc=qd, inc_k=16)
                for g in range(NG):
                    pend = [None]

                    def flush_pend():
                        if pend[0] is not None:
                            fn = pend[0]
                            pend[0] = None
                            fn()

                    for hl in range(HPG):
                        h = g * HPG + hl
                        for j in range(nk):
                            p = j % 2
                            w_s = [st["kv_ready"][qb], bank_free[p * 2]]
                            diag = j >= nk - TPQ
                            soff = 384 + (q0 - j * 128)
                            for sub in range(2):
                                last = sub == 1 and not diag
                                o('T', lambda e, h=h, j=j, p=p, qi=qi,
                                  sub=sub: e.matmul(
                                    ps[p * 2 + sub][:, 0:QB],
                                    kt_sb[sub * 64:sub * 64 + 64,
                                          h * S + j * 128:h * S + j * 128 + 128],
                                    qt_sb[qi][sub * 64:sub * 64 + 64,
                                              h * QB:(h + 1) * QB],
                                    start=True, stop=not diag),
                                  waits=((w_s + [bank_free[p * 2 + 1]])
                                         if sub == 0 else []),
                                  inc=(sT if last else None))
                                if diag:
                                    mw = min(QB, 512 - soff)
                                    o('T', lambda e, p=p, sub=sub, soff=soff,
                                      mw=mw: e.matmul(
                                        ps[p * 2 + sub][:, 0:mw], tril_sb,
                                        mask_sb[:, soff:soff + mw],
                                        start=False, stop=True),
                                      inc=(sT if sub == 1 else None))
                            scores_done = (sT, sT.n)
                            PSpair = PA if p == 0 else PB
                            o('S', lambda e, p=p, PSpair=PSpair: e.activation(
                                e_sb[p], PSpair, AF.Exp, scale=0.125),
                              waits=[scores_done, st["e_free"][p]], inc=sS)
                            bank_free[p * 2] = (sS, sS.n)
                            bank_free[p * 2 + 1] = (sS, sS.n)
                            e_ready = (sS, sS.n)
                            flush_pend()

                            def mkpv(h=h, j=j, p=p, hl=hl, nk=nk,
                                     e_ready=e_ready):
                                def fn():
                                    w_u = [e_ready]
                                    if j == 0:
                                        w_u += [st["U_free"], bank_free[4],
                                                bank_free[5]]
                                    o('T', lambda e: e.matmul(
                                        psU0[:, 0:QB],
                                        v_sb[:, j * D + h * 128:
                                             j * D + (h + 1) * 128],
                                        e_sb[p][:, 0:QB],
                                        start=(j == 0), stop=(j == nk - 1)),
                                      waits=w_u)
                                    o('T', lambda e: e.matmul(
                                        psU1[:, 0:QB],
                                        v_sb[:, j * D + h * 128:
                                             j * D + (h + 1) * 128],
                                        e_sb[p][:, QB:2 * QB],
                                        start=(j == 0), stop=(j == nk - 1)))
                                    m0, m1 = HPG + hl, hl
                                    o('T', lambda e: e.matmul(
                                        psD[0:8, 0:QB],
                                        ind_sb[:, m0 * 8:m0 * 8 + 8],
                                        e_sb[p][:, 0:QB],
                                        start=(hl == 0 and j == 0),
                                        stop=False),
                                      waits=([st["D_free"]]
                                             if (hl == 0 and j == 0) else []))
                                    o('T', lambda e: e.matmul(
                                        psD[0:8, 0:QB],
                                        ind_sb[:, m1 * 8:m1 * 8 + 8],
                                        e_sb[p][:, QB:2 * QB],
                                        start=False,
                                        stop=(hl == HPG - 1 and j == nk - 1)),
                                      inc=sT)
                                    st["e_free"][p] = (sT, sT.n)
                                    if j == nk - 1:
                                        ucol = hl * 2
                                        o('S', lambda e: e.copy(
                                            u_all[:, ucol * QB:
                                                  (ucol + 1) * QB],
                                            psU0[:, 0:QB]),
                                          waits=[(sT, sT.n), st["u_free"]])
                                        o('S', lambda e: e.copy(
                                            u_all[:, (ucol + 1) * QB:
                                                  (ucol + 2) * QB],
                                            psU1[:, 0:QB]), inc=sS)
                                        st["U_free"] = (sS, sS.n)
                                return fn
                            pend[0] = mkpv()
                    flush_pend()
                    o('S', lambda e: e.copy(d8[0:2 * HPG, :],
                                            psD[0:2 * HPG, 0:QB]),
                      waits=[(sT, sT.n)], inc=sS)
                    st["D_free"] = (sS, sS.n)
                    if DBG:
                        o('Q', lambda e, s=s, qb=qb, g=g: e.dma_start(
                            dbg_u[s, qb, g], u_all),
                          waits=[(sS, sS.n), (qd, qd.n)], inc=qd, inc_k=16)
                        o('Q', lambda e, s=s, qb=qb, g=g: e.dma_start(
                            dbg_d[s, qb, g], d8), waits=[(qd, qd.n)],
                          inc=qd, inc_k=16)
                    dbg_done = (qd, qd.n)
                    o('V', lambda e: e.reciprocal(
                        al8[0:2 * HPG, :], d8[0:2 * HPG, :]),
                      waits=[(sS, sS.n)])
                    o('V', lambda e: e.tensor_scalar(
                        al8[0:HPG, :], al8[0:HPG, :],
                        lam_col[0:HPG, 0:1], None, OP.mult), inc=sV)
                    alpha_ready = (sV, sV.n)
                    for hl in range(HPG):
                        h = g * HPG + hl
                        ucol = hl * 2
                        o('T', lambda e, hl=hl: e.matmul(
                            ps[0][:, 0:QB],
                            sel_sb[0:2 * HPG,
                                   (HPG + hl) * 128:(HPG + hl + 1) * 128],
                            al8[0:2 * HPG, :], start=True, stop=True),
                          waits=[alpha_ready, bank_free[0], bank_free[1]])
                        o('T', lambda e, hl=hl: e.matmul(
                            ps[1][:, 0:QB],
                            sel_sb[0:2 * HPG, hl * 128:(hl + 1) * 128],
                            al8[0:2 * HPG, :], start=True, stop=True), inc=sT)
                        bc_a = (sT, sT.n)
                        o('V', lambda e, ucol=ucol: e.tensor_tensor(
                            u_all[:, ucol * QB:(ucol + 2) * QB],
                            u_all[:, ucol * QB:(ucol + 2) * QB],
                            PA, OP.mult),
                          waits=[bc_a] + ([dbg_done] if DBG else []))
                        o('V', lambda e, ucol=ucol: e.tensor_tensor(
                            u_all[:, ucol * QB:(ucol + 1) * QB],
                            u_all[:, ucol * QB:(ucol + 1) * QB],
                            u_all[:, (ucol + 1) * QB:(ucol + 2) * QB],
                            OP.subtract), inc=sV)
                        bank_free[0] = (sV, sV.n)
                        bank_free[1] = (sV, sV.n)
                        o_done = (sV, sV.n)
                        o('S', lambda e, ucol=ucol: e.activation(
                            sqr_sb, u_all[:, ucol * QB:(ucol + 1) * QB],
                            AF.Square), waits=[o_done, st["sq_free"]], inc=sS)
                        w_ms = [(sS, sS.n)]
                        if hl == 0:
                            w_ms.append(st["M_free"])
                        o('T', lambda e, hl=hl: e.matmul(
                            psM[0:8, 0:QB], ind_sb[:, hl * 8:hl * 8 + 8],
                            sqr_sb, start=(hl == 0), stop=(hl == HPG - 1)),
                          waits=w_ms, inc=sT)
                        st["sq_free"] = (sT, sT.n)
                    o('S', lambda e: e.copy(ms8[0:HPG, :], psM[0:HPG, 0:QB]),
                      waits=[(sT, sT.n)], inc=sS)
                    st["M_free"] = (sS, sS.n)
                    o('V', lambda e: e.tensor_scalar(
                        bt8[0:HPG, :], ms8[0:HPG, :], 1.0 / 128.0, float(EPS),
                        OP.mult, OP.add), waits=[(sS, sS.n)], inc=sV)
                    o('S', lambda e: e.activation(
                        bt8[0:HPG, :], bt8[0:HPG, :], AF.Sqrt),
                      waits=[(sV, sV.n)], inc=sS)
                    o('V', lambda e: e.reciprocal(bt8[0:HPG, :], bt8[0:HPG, :]),
                      waits=[(sS, sS.n)], inc=sV)
                    beta_ready = (sV, sV.n)
                    for hl in range(HPG):
                        h = g * HPG + hl
                        ucol = hl * 2
                        o('T', lambda e, hl=hl: e.matmul(
                            ps[2][:, 0:QB],
                            sel_sb[0:HPG, hl * 128:(hl + 1) * 128],
                            bt8[0:HPG, :], start=True, stop=True),
                          waits=[beta_ready, bank_free[2]], inc=sT)
                        bc_b = (sT, sT.n)
                        o('V', lambda e, h=h, ucol=ucol:
                          e.scalar_tensor_tensor(
                            obf[:, h * QB:(h + 1) * QB],
                            u_all[:, ucol * QB:(ucol + 1) * QB],
                            1.0 - LAMBDA_INIT, ps[2][:, 0:QB],
                            OP.mult, OP.mult), waits=[bc_b], inc=sV)
                        bank_free[2] = (sV, sV.n)
                    st["u_free"] = (sV, sV.n)
                obf_done = (sV, sV.n)
                if DBG:
                    o('Q', lambda e, s=s, qb=qb: e.dma_start(
                        dbg_o[s, qb], obf), waits=[obf_done, (qd, qd.n)],
                      inc=qd, inc_k=16)
                    st["obf_dump"][qi] = (qd, qd.n)
                st["qt_free"][qi] = (sT, sT.n)

                for dp in range(FB // 2):
                    db0 = dp * 2
                    for half in range(2):
                        db = db0 + half
                        for kb in range(NH):
                            o('T', lambda e, db=db, kb=kb, half=half:
                              e.matmul(
                                PC[:, half * QB:(half + 1) * QB],
                                wo_sb[:, kb * D + db * 128:
                                      kb * D + (db + 1) * 128],
                                obf[:, kb * QB:(kb + 1) * QB],
                                start=(kb == 0), stop=(kb == NH - 1)),
                              waits=([obf_done, bank_free[4], bank_free[5]]
                                     if (half == 0 and kb == 0) else []),
                              inc=(sT if (half == 1 and kb == NH - 1)
                                   else None))
                    wo_done = (sT, sT.n)
                    o('V', lambda e: e.tensor_tensor(
                        outstp, PC, gs2p, OP.mult),
                      waits=[wo_done, gx_load[0], gx_load[1],
                             st["out_free"][0], st["out_free"][1]])
                    o('V', lambda e: e.tensor_tensor(
                        outstp, outstp, xts2p, OP.add), inc=sV)
                    bank_free[4] = (sV, sV.n)
                    bank_free[5] = (sV, sV.n)
                    st["gx_free"][0] = (sV, sV.n)
                    st["gx_free"][1] = (sV, sV.n)
                    o('Q', lambda e, s=s, q0=q0, dp=dp: e.dma_start(
                        outT[s, dp * 256:(dp + 1) * 256, q0:q0 + QB]
                        .rearrange("(f p) t -> p f t", p=128),
                        outstp.rearrange("p (f c) -> p f c", f=2, c=QB)),
                      waits=[(sV, sV.n)], inc=qo[dp % 2], inc_k=16)
                    st["out_free"][0] = (qo[dp % 2], qo[dp % 2].n)
                    st["out_free"][1] = st["out_free"][0]
                    if db0 + 2 < FB:
                        o('Q', lambda e, s=s, q0=q0, dp=dp: e.dma_start(
                            gs2p.rearrange("p (f c) -> p f c", f=2, c=QB),
                            g_scr[s, (dp + 1) * 256:(dp + 2) * 256,
                                  q0:q0 + QB]
                            .rearrange("(f p) t -> p f t", p=128)),
                          waits=[(sV, sV.n)], inc=qe[dp % 2], inc_k=16)
                        o('Q', lambda e, s=s, q0=q0, dp=dp: e.dma_start(
                            xts2p.rearrange("p (f c) -> p f c", f=2, c=QB),
                            xT[s, (dp + 1) * 256:(dp + 2) * 256, q0:q0 + QB]
                            .rearrange("(f p) t -> p f t", p=128)),
                          inc=qe[dp % 2], inc_k=16)
                        gx_load = [(qe[dp % 2], qe[dp % 2].n)] * 2
                st["U_free"] = (sV, sV.n)
                st["obf_free"][qi] = (sT, sT.n)

                if qb + 1 < NQB:
                    for fb in range(FB):
                        o('Q', lambda e, s=s, qb=qb, fb=fb: e.dma_start(
                            xnc[:, fb * QB:(fb + 1) * QB],
                            xn_scr[s, fb * 128:(fb + 1) * 128,
                                   (qb + 1) * QB:(qb + 2) * QB]),
                          waits=([st["xnc_free"]] if fb == 0 else []),
                          inc=qc, inc_k=16)
                    st["xnc_load"] = (qc, qc.n)
                    setup_chunk(s, qb + 1)

        def _final_waits_q(e):
            for sm in (qx, qc, qn[0], qn[1], qg[0], qg[1], qg[2], qg[3],
                       qe[0], qe[1], qo[0], qo[1], qd):
                if sm.n:
                    e.wait_ge(sm.h, sm.n)
            return None
        def _final_waits_g(e):
            for sm in (qk, qw[0], qw[1], qa):
                if sm.n:
                    e.wait_ge(sm.h, sm.n)
            return None
        o('Q', _final_waits_q)
        o('G', _final_waits_g)

    sems = [sT, sS, sV, qx, qc, qk, qa, qd] + qn + qg + qe + qo + qw
    with nc.Block() as blk:
        def _run(name):
            def body(eng):
                for sm in sems:
                    sm.n = 0
                ctx.begin(name, eng)
                schedule()
            return body
        blk.sync(_run('Q'))
        blk.gpsimd(_run('G'))
        blk.tensor(_run('T'))
        blk.scalar(_run('S'))
        blk.vector(_run('V'))
    return nc


def _host_constants():
    import ml_dtypes
    ramp = (np.arange(896)[None, :] <= np.arange(128)[:, None] + 383)
    ramp = (MASKC * ramp.astype(np.float32)).astype(ml_dtypes.bfloat16)
    tril = (np.arange(128)[:, None] <= np.arange(128)[None, :])
    tril = tril.astype(ml_dtypes.bfloat16)
    ind = np.zeros((128, 64), dtype=np.float32)
    for m in range(8):
        ind[:, m * 8 + m] = 1.0
    sel = np.zeros((8, 8 * 128), dtype=np.float32)
    for m in range(8):
        sel[m, m * 128:(m + 1) * 128] = 1.0
    return ramp, tril, ind.astype(ml_dtypes.bfloat16), sel


def kernel(**inputs):
    cfg = CFG
    BPC = cfg["BPC"]
    x = np.ascontiguousarray(np.asarray(inputs["L0_attn_x"], dtype=np.float32))
    ncores = x.shape[0] // BPC
    mask, tril, ind, sel = _host_constants()
    shared = {
        "wqT": np.ascontiguousarray(np.asarray(inputs["L0_attn_Wq"], np.float32).T),
        "wkT": np.ascontiguousarray(np.asarray(inputs["L0_attn_Wk"], np.float32).T),
        "wvT": np.ascontiguousarray(np.asarray(inputs["L0_attn_Wv"], np.float32).T),
        "woT": np.ascontiguousarray(np.asarray(inputs["L0_attn_Wo"], np.float32).T),
        "w1T": np.ascontiguousarray(np.asarray(inputs["L0_ffn_w1"], np.float32).T),
        "w2T": np.ascontiguousarray(np.asarray(inputs["L0_ffn_w2"], np.float32).T),
        "b1": np.ascontiguousarray(np.asarray(inputs["L0_ffn_b1"], np.float32)),
        "b2": np.ascontiguousarray(np.asarray(inputs["L0_ffn_b2"], np.float32)),
        "lqk": np.ascontiguousarray(np.concatenate([
            np.asarray(inputs["L0_attn_lq1"], np.float32),
            np.asarray(inputs["L0_attn_lk1"], np.float32),
            np.asarray(inputs["L0_attn_lq2"], np.float32),
            np.asarray(inputs["L0_attn_lk2"], np.float32)])[None, :]),
        "mask": mask, "tril": tril, "ind": ind, "sel": sel,
    }
    nc = build_nc(cfg)
    in_maps = []
    for c in range(ncores):
        xs = x[c * BPC:(c + 1) * BPC]
        m = dict(shared)
        m["xT"] = np.ascontiguousarray(xs.transpose(0, 2, 1))
        in_maps.append(m)
    res = run_bass_kernel_spmd(nc, in_maps, core_ids=list(range(ncores)))
    outs = [np.asarray(r["outT"]).transpose(0, 2, 1) for r in res.results]
    return np.ascontiguousarray(np.concatenate(outs, axis=0)).astype(np.float32)


# version 10
# speedup vs baseline: 1.9224x; 1.0595x over previous
import math
import numpy as np

import concourse.bass as bass
import concourse.mybir as mybir
from concourse.bass_utils import run_bass_kernel_spmd

F32 = mybir.dt.float32
BF16 = mybir.dt.bfloat16
I32 = mybir.dt.int32
AF = mybir.ActivationFunctionType
OP = mybir.AluOpType
AX = mybir.AxisListType

LAMBDA_INIT = 0.8 - 0.6 * math.exp(-0.3 * 6)
EPS = 1e-5
MAGIC = 12582912.0
MASKC = -400.0

CFG = dict(D=1024, S=2048, HID=4096, BPC=2, ST=256, QB=512)


class Sem:
    __slots__ = ("h", "n", "name")

    def __init__(self, nc, name):
        self.h = nc.alloc_semaphore(name)
        self.n = 0
        self.name = name


class Ctx:

    def __init__(self):
        self.active = None
        self.eng = None
        self._waited = {}
        self._need_drain = False

    def begin(self, name, eng):
        self.active = name
        self.eng = eng
        self._waited = {}
        self._need_drain = False

    def op(self, engine, emit, waits=(), inc=None, inc_k=1):
        if engine == self.active:
            if self.active in ('V', 'S') and self._need_drain:
                self.eng.drain()
                self._need_drain = False
            for w in waits:
                if w is None:
                    continue
                sem, val = w
                assert val <= sem.n, (
                    f"wait-ahead on {sem.name}: {val} > {sem.n}")
                if val <= 0 or self._waited.get(sem.name, -1) >= val:
                    continue
                self.eng.wait_ge(sem.h, val)
                self._waited[sem.name] = val
            inst = emit(self.eng)
            self._need_drain = True
            if inc is not None:
                if isinstance(inst, list):
                    inst = inst[-1]
                inst.then_inc(inc.h, inc_k)
        if inc is not None:
            inc.n += inc_k


def _fv(ap2d, nblk, blk_cols):
    return ap2d.rearrange("p (f t) -> p f t", f=nblk, t=blk_cols)


def build_nc(cfg):
    D, S, HID, BPC = cfg["D"], cfg["S"], cfg["HID"], cfg["BPC"]
    ST, QB = cfg["ST"], cfg["QB"]
    FB = D // 128
    HB = HID // 128
    NH = D // 128
    NQB = S // QB
    KT = S // 128
    TPQ = QB // 128
    HPG = min(4, NH)
    NG = NH // HPG
    VCW = min(512, D)
    NV = D // VCW
    assert 2 * HPG <= 8 and NH <= 8 and S % QB == 0 and D % VCW == 0

    nc = bass.Bass("TRN2", target_bir_lowering=False, debug=False)

    xT = nc.dram_tensor("xT", [BPC, D, S], F32, kind="ExternalInput").ap()
    wqT = nc.dram_tensor("wqT", [D, D], F32, kind="ExternalInput").ap()
    wkT = nc.dram_tensor("wkT", [D, D], F32, kind="ExternalInput").ap()
    wvT = nc.dram_tensor("wvT", [D, D], F32, kind="ExternalInput").ap()
    woT = nc.dram_tensor("woT", [D, D], F32, kind="ExternalInput").ap()
    w1T = nc.dram_tensor("w1T", [D, HID], F32, kind="ExternalInput").ap()
    w2T = nc.dram_tensor("w2T", [HID, D], F32, kind="ExternalInput").ap()
    b1d = nc.dram_tensor("b1", [HID], F32, kind="ExternalInput").ap()
    b2d = nc.dram_tensor("b2", [D], F32, kind="ExternalInput").ap()
    lqkd = nc.dram_tensor("lqk", [1, 256], F32, kind="ExternalInput").ap()
    maskd = nc.dram_tensor("mask", [128, 896], BF16, kind="ExternalInput").ap()
    trild = nc.dram_tensor("tril", [128, 128], BF16, kind="ExternalInput").ap()
    indd = nc.dram_tensor("ind", [128, 64], BF16, kind="ExternalInput").ap()
    seld = nc.dram_tensor("sel", [8, 8 * 128], F32, kind="ExternalInput").ap()

    outT = nc.dram_tensor("outT", [BPC, D, S], F32, kind="ExternalOutput").ap()
    g_scr = nc.dram_tensor("g_scr", [BPC, D, S], F32, kind="Internal").ap()
    xn_scr = nc.dram_tensor("xn_scr", [BPC, D, S], BF16, kind="Internal").ap()
    b1_scr = nc.dram_tensor("b1_scr", [HID], BF16, kind="Internal").ap()
    b2_scr = nc.dram_tensor("b2_scr", [D], BF16, kind="Internal").ap()
    DBG = bool(cfg.get("debug"))
    if DBG:
        NQB_ = S // QB
        NG_ = (D // 128) // min(4, D // 128)
        HPG_ = min(4, D // 128)
        dbg_u = nc.dram_tensor("dbg_u", [BPC, NQB_, NG_, 128, 2 * HPG_ * QB],
                               F32, kind="ExternalOutput").ap()
        dbg_d = nc.dram_tensor("dbg_d", [BPC, NQB_, NG_, 8, QB], F32,
                               kind="ExternalOutput").ap()
        dbg_o = nc.dram_tensor("dbg_o", [BPC, NQB_, 128, (D // 128) * QB],
                               BF16, kind="ExternalOutput").ap()
        dbg_k = nc.dram_tensor("dbg_k", [BPC, 128, (D // 128) * S], BF16,
                               kind="ExternalOutput").ap()
        dbg_v = nc.dram_tensor("dbg_v", [BPC, 128, (S // 128) * D], BF16,
                               kind="ExternalOutput").ap()
        dbg_q = nc.dram_tensor("dbg_q", [BPC, NQB_, 128, (D // 128) * QB],
                               BF16, kind="ExternalOutput").ap()
        dbg_xc = nc.dram_tensor("dbg_xc", [BPC, NQB_, 128, (D // 128) * QB],
                                BF16, kind="ExternalOutput").ap()
        NT_ = (BPC * S) // ST
        dbg_hq = nc.dram_tensor("dbg_hq", [NT_, 128, (HID // 128) * ST],
                                BF16, kind="ExternalOutput").ap()
        dbg_xq = nc.dram_tensor("dbg_xq", [NT_, 128, (D // 128) * ST],
                                BF16, kind="ExternalOutput").ap()
        dbg_ub = nc.dram_tensor("dbg_ub", [4, 128, ST], F32,
                                kind="ExternalOutput").ap()

    c_w1q = 0
    c_w2q = c_w1q + FB * HID
    c_hqb = c_w2q + HB * D
    c_xqb = c_hqb + HB * ST
    c_xnst = c_xqb + FB * ST
    a_cols = c_xnst + 8 * ST

    c_wq = 0
    c_xnc = c_wq + 4 * FB * D
    c_kt = c_xnc + FB * QB
    c_v = c_kt + NH * S
    c_qt = c_v + KT * D
    c_e = c_qt + 2 * NH * QB
    c_sqr = c_e + 4 * QB
    b_cols = c_sqr + QB
    BFC = max(a_cols, b_cols)

    QW = min(4 * ST, 1024)
    f_xts = 0
    f_c = f_xts + FB * ST
    f_u = f_c + QW
    f_gs = f_u + 2 * QW
    f_sq = f_gs + 4 * ST
    f_mbc = f_sq + 4 * ST
    f_wst = f_mbc + 2 * ST
    a_fcols = f_wst + 1024

    f_uall = 0
    f_gs2 = f_uall + 2 * HPG * QB
    f_xts2 = f_gs2 + 2 * QB
    f_outst = f_xts2 + 2 * QB
    f_d8 = f_outst + 2 * QB
    f_al8 = f_d8 + QB
    f_ms8 = f_al8 + QB
    f_bt8 = f_ms8 + QB
    b_fcols = f_bt8 + QB
    f_mv = a_fcols
    f_lnt = f_mv + ST
    a_fcols = f_mv + 3 * ST
    F32C = max(a_fcols, b_fcols)

    AB = nc.alloc_sbuf_tensor("AB", [128, BFC], BF16).ap()
    AF4 = nc.alloc_sbuf_tensor("AF4", [128, F32C], F32).ap()
    KI = AB[:, c_e:c_e + 2048].bitcast(I32)

    b1h = (HID // 2 + 63) // 64 * 64
    b2h = (D // 2 + 63) // 64 * 64
    scb_w = 896 + 128 + 64 + 16 + ST + 32 + 16 + b1h + b2h
    SCB = nc.alloc_sbuf_tensor("SCB", [128, scb_w], BF16).ap()
    _o = [0]

    def _scb(w):
        a = _o[0]
        _o[0] += w
        return SCB[:, a:a + w]
    mask_sb = _scb(896)
    tril_sb = _scb(128)
    ind_sb = _scb(64)
    ones_bf = _scb(16)[:, 0:1]
    ones_row = _scb(ST)
    b1qb = _scb(32)[:, 0:HB]
    b2qb = _scb(16)[:, 0:FB]
    b1r2 = _scb(b1h)
    b2r2 = _scb(b2h)
    scf_w = 1024 + 16 + 128 + 32 + 16 + 16 + 16 + 16
    SCF = nc.alloc_sbuf_tensor("SCF", [128, scf_w], F32).ap()
    _o[0] = 0

    def _scf(w):
        a = _o[0]
        _o[0] += w
        return SCF[:, a:a + w]
    sel_sb = _scf(1024)[0:8, :]
    ones_cf = _scf(16)[:, 0:1]
    ones1f = _scf(128)[0:1, :]
    b1q = _scf(32)[:, 0:HB]
    b2q = _scf(16)[:, 0:FB]
    lam_sb = _scf(16)[0:1, 0:8]
    lam_col = _scf(16)[:, 0:1]
    fc = _scf(16)[:, 0:8]
    lam_in = AF4[0:1, f_mv:f_mv + 256]
    d8 = AF4[0:8, f_d8:f_d8 + QB]
    al8 = AF4[0:8, f_al8:f_al8 + QB]
    ms8 = AF4[0:8, f_ms8:f_ms8 + QB]
    bt8 = AF4[0:8, f_bt8:f_bt8 + QB]
    mv_sb = AF4[0:1, f_mv:f_mv + 3 * ST]
    lnt = AF4[0:1, f_lnt:f_lnt + ST]

    PA = nc.alloc_psum_tensor("PA", [128, 1024], F32).ap()
    PB = nc.alloc_psum_tensor("PB", [128, 1024], F32).ap()
    PC = nc.alloc_psum_tensor("PC", [128, 1024], F32).ap()
    ps = [PA[:, 0:512], PA[:, 512:1024], PB[:, 0:512], PB[:, 512:1024],
          PC[:, 0:512], PC[:, 512:1024]]
    ps += [nc.alloc_psum_tensor(f"ps{i}", [128, 512], F32).ap()
           for i in range(6, 8)]
    psU0, psU1, psD, psM = ps[4], ps[5], ps[6], ps[7]

    sT = Sem(nc, "sT")
    sS = Sem(nc, "sS")
    sV = Sem(nc, "sV")
    qx = Sem(nc, "qx")
    qn = [Sem(nc, "qn0"), Sem(nc, "qn1")]
    qg = [Sem(nc, f"qg{i}") for i in range(4)]
    qc = Sem(nc, "qc")
    qe = [Sem(nc, "qe0"), Sem(nc, "qe1")]
    qo = [Sem(nc, "qo0"), Sem(nc, "qo1")]
    qk = Sem(nc, "qk")
    qw = [Sem(nc, "qw0"), Sem(nc, "qw1")]
    qa = Sem(nc, "qa")
    qd = Sem(nc, "qd")

    ctx = Ctx()

    bank_free = [None] * 8

    def schedule():
        o = ctx.op
        for i in range(8):
            bank_free[i] = None

        o('G', lambda e: e.dma_start(mask_sb, maskd), inc=qk, inc_k=16)
        o('G', lambda e: e.dma_start(tril_sb, trild), inc=qk, inc_k=16)
        o('G', lambda e: e.dma_start(ind_sb, indd), inc=qk, inc_k=16)
        o('G', lambda e: e.dma_start(sel_sb, seld), inc=qk, inc_k=16)
        o('G', lambda e: e.dma_start(lam_in, lqkd), inc=qk, inc_k=16)
        with nc.allow_non_contiguous_dma("tiny bias loads"):
            o('G', lambda e: e.dma_start(
                b1q, b1d.rearrange("(c p) -> p c", p=128)), inc=qk, inc_k=16)
            o('G', lambda e: e.dma_start(
                b2q, b2d.rearrange("(c p) -> p c", p=128)), inc=qk, inc_k=16)
        const_loaded = (qk, qk.n)
        o('V', lambda e: e.memset(ones_bf, 1.0))
        o('V', lambda e: e.memset(ones_cf, 1.0))
        o('V', lambda e: e.memset(ones_row, 1.0))
        o('V', lambda e: e.memset(fc[:, 0:1], float(EPS)))
        o('V', lambda e: e.memset(fc[:, 1:2], -32.0))
        o('V', lambda e: e.memset(fc[:, 2:3], 512.0 + MAGIC))
        o('V', lambda e: e.memset(fc[:, 3:4], MAGIC))
        o('V', lambda e: e.memset(fc[:, 4:5], -MAGIC / 64.0))
        o('V', lambda e: e.memset(fc[:, 5:6], 512.0))
        o('V', lambda e: e.memset(fc[:, 6:7], -MAGIC))
        o('V', lambda e: e.memset(fc[:, 7:8], -(MAGIC + 512.0) / 16.0))
        o('V', lambda e: e.memset(ones1f, 1.0), inc=sV)
        fc_ready = (sV, sV.n)

        def quantw(dst, src, n, waits=()):
            ii = KI[:, 0:n]
            o('S', lambda e: e.activation(
                ii, src, AF.Identity, bias=fc[:, 3:4], scale=64.0),
              waits=list(waits) + [fc_ready])
            o('S', lambda e: e.activation(
                src, ii, AF.Identity, bias=fc[:, 4:5], scale=1.0 / 64.0),
              inc=sS)
            o('V', lambda e: e.tensor_scalar(
                dst, src, -2.0, 127.0 / 64.0, OP.max, OP.min),
              waits=[(sS, sS.n)], inc=sV)

        quantw(b1qb, b1q, HB, waits=[const_loaded])
        quantw(b2qb, b2q, FB)
        HBH = HB // 2
        FBH = FB // 2

        def b1row(hb):
            g = 64 * (hb // HBH)
            c = (hb % HBH) * 128
            return b1r2[g:g + 1, c:c + 128]

        def b2row(db):
            g = 64 * (db // FBH)
            c = (db % FBH) * 128
            return b2r2[g:g + 1, c:c + 128]

        with nc.allow_non_contiguous_dma("bias row gathers"):
            o('G', lambda e: e.dma_start(
                b1_scr.rearrange("(c p) -> p c", p=128), b1qb),
              waits=[(sV, sV.n)], inc=qk, inc_k=16)
            o('G', lambda e: e.dma_start(
                b2_scr.rearrange("(c p) -> p c", p=128), b2qb),
              inc=qk, inc_k=16)
            b1v = b1_scr.rearrange("(a b) -> a b", a=2)
            b2v = b2_scr.rearrange("(a b) -> a b", a=2)
            o('G', lambda e: e.dma_start(b1r2[0:1, 0:HID // 2], b1v[0:1, :]),
              waits=[(qk, qk.n)], inc=qk, inc_k=16)
            o('G', lambda e: e.dma_start(b1r2[64:65, 0:HID // 2], b1v[1:2, :]),
              inc=qk, inc_k=16)
            o('G', lambda e: e.dma_start(b2r2[0:1, 0:D // 2], b2v[0:1, :]),
              inc=qk, inc_k=16)
            o('G', lambda e: e.dma_start(b2r2[64:65, 0:D // 2], b2v[1:2, :]),
              inc=qk, inc_k=16)
        brows_ready = (qk, qk.n)

        o('V', lambda e: e.tensor_tensor(
            lam_in[:, 0:64], lam_in[:, 0:64], lam_in[:, 64:128], OP.mult))
        o('V', lambda e: e.tensor_tensor(
            lam_in[:, 128:192], lam_in[:, 128:192], lam_in[:, 192:256], OP.mult))
        o('V', lambda e: e.reduce_sum(lam_sb[:, 0:1], lam_in[:, 0:64], axis=AX.X))
        o('V', lambda e: e.reduce_sum(lam_sb[:, 1:2], lam_in[:, 128:192], axis=AX.X),
          inc=sV)
        o('S', lambda e: e.activation(lam_sb[:, 2:3], lam_sb[:, 0:1], AF.Exp),
          waits=[(sV, sV.n)])
        o('S', lambda e: e.activation(lam_sb[:, 3:4], lam_sb[:, 1:2], AF.Exp),
          inc=sS)
        o('V', lambda e: e.tensor_tensor(
            lam_sb[:, 4:5], lam_sb[:, 2:3], lam_sb[:, 3:4], OP.subtract),
          waits=[(sS, sS.n)])
        o('V', lambda e: e.tensor_scalar(
            lam_sb[:, 4:5], lam_sb[:, 4:5], LAMBDA_INIT, None, OP.add), inc=sV)
        o('T', lambda e: e.matmul(ps[0][:, 0:1], ones1f, lam_sb[:, 4:5],
                                  start=True, stop=True),
          waits=[(sV, sV.n)], inc=sT)
        o('S', lambda e: e.copy(lam_col, ps[0][:, 0:1]),
          waits=[(sT, sT.n)], inc=sS)
        bank_free[0] = (sS, sS.n)

        w1q = AB[:, c_w1q:c_w1q + FB * HID]
        w2q = AB[:, c_w2q:c_w2q + HB * D]
        CW1 = min(1024, HID)
        assert D <= 1024
        wst = [AF4[:, f_wst:f_wst + 1024],
               AF4[:, f_gs:f_gs + 1024]]
        wst_free = [None, None]
        wjobs = ([("w1", kb, cc) for cc in range(HID // CW1)
                  for kb in range(FB)] +
                 [("w2", hb, 0) for hb in range(HB)])
        w1cc_ready = {}
        w2_ready = {}
        wstate = {"i": 0}

        def emit_wjobs(k):
            for _ in range(k):
                i = wstate["i"]
                if i >= len(wjobs):
                    return
                wstate["i"] = i + 1
                w, b, cc = wjobs[i]
                sb = i % 2
                if w == "w1":
                    nw = CW1
                    src = w1T[b * 128:(b + 1) * 128, cc * CW1:(cc + 1) * CW1]
                    dst = w1q[:, b * HID + cc * CW1:b * HID + (cc + 1) * CW1]
                else:
                    nw = D
                    src = w2T[b * 128:(b + 1) * 128, 0:D]
                    dst = w2q[:, b * D:(b + 1) * D]
                o('G', lambda e, src=src, d=wst[sb], nw=nw: e.dma_start(
                    d[:, 0:nw], src),
                  waits=[wst_free[sb]], inc=qw[sb], inc_k=16)
                quantw(dst, wst[sb][:, 0:nw], nw,
                       waits=[(qw[sb], qw[sb].n)])
                wst_free[sb] = (sV, sV.n)
                if w == "w1" and b == FB - 1:
                    w1cc_ready[cc] = (sV, sV.n)
                elif w == "w2":
                    w2_ready[b] = (sV, sV.n)

        emit_wjobs(FB)

        xts = AF4[:, f_xts:f_xts + FB * ST]
        cbuf = AF4[:, f_c:f_c + QW]
        ubuf = [AF4[:, f_u:f_u + QW], AF4[:, f_u + QW:f_u + 2 * QW]]
        t1b = ubuf[1]
        t2b = ubuf[0]
        gsA = AF4[:, f_gs:f_gs + 4 * ST]
        gsB = AF4[:, f_wst:f_wst + 1024]
        xqb = AB[:, c_xqb:c_xqb + FB * ST]
        hqb = AB[:, c_hqb:c_hqb + HB * ST]
        xnst = [AB[:, c_xnst:c_xnst + 4 * ST],
                AB[:, c_xnst + 4 * ST:c_xnst + 8 * ST]]
        sqfb = [AF4[:, f_sq:f_sq + 2 * ST], AF4[:, f_sq + 2 * ST:f_sq + 4 * ST]]
        m_bc = AF4[:, f_mbc:f_mbc + ST]
        r_bc = AF4[:, f_mbc + ST:f_mbc + 2 * ST]

        WQF = min(4, FB)
        def bc4(ap):
            return ap.unsqueeze(1).to_broadcast([128, WQF, ST])

        def pv4(ap2d):
            return ap2d.rearrange("p (f c) -> p f c", f=WQF, c=ST)

        NQD = HB // 4

        supertiles = [(s, t0) for s in range(BPC) for t0 in range(0, S, ST)]
        NT = len(supertiles)
        o('Q', lambda e: e.dma_start(
            _fv(xts, FB, ST),
            xT[supertiles[0][0], :, supertiles[0][1]:supertiles[0][1] + ST]
            .rearrange("(f p) t -> p f t", p=128)), inc=qx, inc_k=16)
        As = {
            "xts_load": (qx, qx.n), "sq_mm_done": [None, None],
            "stats_bank_free": None, "xq_ready": None, "xq_free": None,
            "hq_ready": None, "hq_free": None, "xnst_free": [None, None],
            "xnst_i": 0, "gs_free": None, "gsB_free": None,
            "dbg_free": None,
        }
        W4 = WQF * ST
        NPA = min(FB, 4)
        NPB = FB - NPA

        def emit_ln_xq(ti):
            s, t0 = supertiles[ti]
            for fp in range(FB // 2):
                sqi = fp % 2
                o('S', lambda e, fp=fp, sqi=sqi: e.activation(
                    sqfb[sqi], xts[:, fp * 2 * ST:(fp + 1) * 2 * ST],
                    AF.Square),
                  waits=[As["xts_load"], As["sq_mm_done"][sqi]], inc=sS)
                w_t = [As["xts_load"]]
                if fp == 0:
                    w_t += [As["stats_bank_free"], bank_free[6],
                            bank_free[7]]
                sq_done = (sS, sS.n)
                for h2 in range(2):
                    fb = fp * 2 + h2
                    o('T', lambda e, fb=fb: e.matmul(
                        ps[6][0:1, 0:ST], ones_cf,
                        xts[:, fb * ST:(fb + 1) * ST],
                        start=(fb == 0), stop=(fb == FB - 1)),
                      waits=(w_t if fb == 0 else []))
                    o('T', lambda e, fb=fb, sqi=sqi, h2=h2: e.matmul(
                        ps[7][0:1, 0:ST], ones_cf,
                        sqfb[sqi][:, h2 * ST:(h2 + 1) * ST],
                        start=(fb == 0), stop=(fb == FB - 1)),
                      waits=([sq_done] if h2 == 0 else []), inc=sT)
                As["sq_mm_done"][sqi] = (sT, sT.n)
            m_row = mv_sb[:, 0:ST]
            r_row = mv_sb[:, ST:2 * ST]
            v_row = mv_sb[:, 2 * ST:3 * ST]
            o('V', lambda e: e.tensor_scalar(
                m_row, ps[6][0:1, 0:ST], 1.0 / D, None, OP.mult),
              waits=[(sT, sT.n)])
            o('V', lambda e: e.tensor_tensor(lnt, m_row, m_row, OP.mult))
            o('V', lambda e: e.scalar_tensor_tensor(
                v_row, ps[7][0:1, 0:ST], 1.0 / D, lnt, OP.mult, OP.subtract),
              inc=sV)
            o('S', lambda e: e.activation(
                r_row, v_row, AF.Sqrt, bias=fc[0:1, 0:1], scale=1.0),
              waits=[(sV, sV.n)], inc=sS)
            o('V', lambda e: e.reciprocal(r_row, r_row),
              waits=[(sS, sS.n)], inc=sV)
            o('T', lambda e: e.matmul(ps[6][:, 0:ST], ones1f, m_row,
                                      start=True, stop=True),
              waits=[(sV, sV.n)])
            o('T', lambda e: e.matmul(ps[7][:, 0:ST], ones1f, r_row,
                                      start=True, stop=True), inc=sT)
            o('S', lambda e: e.copy(m_bc, ps[6][:, 0:ST]),
              waits=[(sT, sT.n)])
            o('S', lambda e: e.copy(r_bc, ps[7][:, 0:ST]), inc=sS)
            As["stats_bank_free"] = (sS, sS.n)
            bcast_done = (sS, sS.n)
            bank_free[6] = (sS, sS.n)
            bank_free[7] = (sS, sS.n)
            for fp in range(FB // WQF):
                fb0 = fp * WQF
                xsl = xts[:, fb0 * ST:fb0 * ST + W4]
                o('V', lambda e, xsl=xsl: e.tensor_tensor(
                    pv4(cbuf[:, 0:W4]), pv4(xsl), bc4(m_bc), OP.subtract),
                  waits=[bcast_done])
                o('V', lambda e: e.tensor_tensor(
                    pv4(cbuf[:, 0:W4]), pv4(cbuf[:, 0:W4]), bc4(r_bc),
                    OP.mult))
                xi = As["xnst_i"]
                o('V', lambda e, xi=xi: e.tensor_copy(
                    xnst[xi][:, 0:W4], cbuf[:, 0:W4]),
                  waits=[As["xnst_free"][xi]], inc=sV)
                o('Q', lambda e, s=s, t0=t0, fb0=fb0, xi=xi: e.dma_start(
                    xn_scr[s, fb0 * 128:(fb0 + WQF) * 128, t0:t0 + ST]
                    .rearrange("(f p) t -> p f t", p=128),
                    pv4(xnst[xi][:, 0:W4])),
                  waits=[(sV, sV.n)], inc=qn[xi], inc_k=16)
                As["xnst_free"][xi] = (qn[xi], qn[xi].n)
                As["xnst_i"] ^= 1
                w_q = [As["xq_free"], As["dbg_free"]] if fp == 0 else []
                o('S', lambda e: e.activation(
                    t1b[:, 0:W4], cbuf[:, 0:W4], AF.Identity,
                    bias=fc[:, 5:6], scale=16.0), waits=[(sV, sV.n)])
                o('S', lambda e: e.activation(
                    KI[:, 0:W4], t1b[:, 0:W4], AF.Identity, scale=1.0))
                o('S', lambda e: e.activation(
                    t2b[:, 0:W4], KI[:, 0:W4], AF.Identity, scale=1.0),
                  inc=sS)
                o('V', lambda e: e.tensor_tensor(
                    t1b[:, 0:W4], t1b[:, 0:W4], t2b[:, 0:W4], OP.subtract),
                  waits=[(sS, sS.n)])
                o('V', lambda e: e.tensor_scalar(
                    t1b[:, 0:W4], t1b[:, 0:W4], 0.5, None, OP.is_ge))
                o('V', lambda e: e.tensor_tensor(
                    t1b[:, 0:W4], t1b[:, 0:W4], t2b[:, 0:W4], OP.add))
                o('V', lambda e: e.tensor_scalar(
                    t1b[:, 0:W4], t1b[:, 0:W4], -512.0, 1.0 / 16.0,
                    OP.add, OP.mult))
                o('V', lambda e, fb0=fb0: e.tensor_scalar(
                    xqb[:, fb0 * ST:fb0 * ST + W4], t1b[:, 0:W4],
                    -8.0, 127.0 / 16.0, OP.max, OP.min),
                  waits=w_q, inc=sV)
            As["xq_ready"] = (sV, sV.n)
            if ti + 1 < NT:
                s2, t02 = supertiles[ti + 1]
                o('Q', lambda e, s2=s2, t02=t02: e.dma_start(
                    _fv(xts, FB, ST),
                    xT[s2, :, t02:t02 + ST]
                    .rearrange("(f p) t -> p f t", p=128)),
                  waits=[(sV, sV.n)], inc=qx, inc_k=16)
                As["xts_load"] = (qx, qx.n)

        def emit_ffn1(ti):
            for qd_ in range(NQD):
                pw = qd_ % 2
                PQ = PA if pw == 0 else PB
                w_m = [As["xq_ready"], bank_free[pw * 2],
                       bank_free[pw * 2 + 1]]
                if qd_ == 0 and As["dbg_free"]:
                    w_m.append(As["dbg_free"])
                if ti == 0:
                    w_m += [w1cc_ready[((qd_ + 1) * 512 - 1) // CW1],
                            brows_ready]
                for j4 in range(4):
                    hb = qd_ * 4 + j4
                    for kb in range(FB):
                        o('T', lambda e, hb=hb, kb=kb, j4=j4, PQ=PQ:
                          e.matmul(
                            PQ[:, j4 * ST:(j4 + 1) * ST],
                            w1q[:, kb * HID + hb * 128:
                                kb * HID + (hb + 1) * 128],
                            xqb[:, kb * ST:(kb + 1) * ST],
                            start=(j4 % 2 == 0 and kb == 0), stop=False),
                          waits=(w_m if (j4 == 0 and kb == 0) else []))
                    g1 = 64 * (hb // HBH)
                    o('T', lambda e, hb=hb, j4=j4, PQ=PQ, g1=g1: e.matmul(
                        PQ[:, j4 * ST:(j4 + 1) * ST],
                        b1row(hb), ones_row[g1:g1 + 1, 0:ST],
                        start=False, stop=(j4 % 2 == 1)), inc=sT)
                o('S', lambda e, PQ=PQ: e.activation(
                    KI[:, 0:QW], PQ, AF.Identity, bias=fc[:, 2:3],
                    scale=16.0), waits=[(sT, sT.n)], inc=sS)
                bank_free[pw * 2] = (sS, sS.n)
                bank_free[pw * 2 + 1] = (sS, sS.n)
                o('S', lambda e: e.activation(
                    cbuf, KI[:, 0:QW], AF.Relu, bias=fc[:, 7:8],
                    scale=1.0 / 16.0),
                  waits=[(sV, sV.n)], inc=sS)
                o('V', lambda e, qd_=qd_: e.tensor_scalar(
                    hqb[:, qd_ * QW:(qd_ + 1) * QW], cbuf, 127.0 / 16.0,
                    None, OP.min),
                  waits=[(sS, sS.n), As["hq_free"] if qd_ == 0 else None,
                         As["dbg_free"] if qd_ == 0 else None], inc=sV)
                if ti == 0:
                    emit_wjobs(8)
            As["hq_ready"] = (sV, sV.n)
            if DBG:
                o('Q', lambda e, ti=ti: e.dma_start(dbg_hq[ti], hqb),
                  waits=[As["hq_ready"], (qd, qd.n)], inc=qd, inc_k=16)
                o('Q', lambda e, ti=ti: e.dma_start(dbg_xq[ti], xqb),
                  waits=[(qd, qd.n)], inc=qd, inc_k=16)
                As["dbg_free"] = (qd, qd.n)
            As["xq_free"] = (sT, sT.n)

        def emit_ffn2(ti, xn_gate):
            s, t0 = supertiles[ti]
            for hb in range(HB):
                w_m = [As["hq_ready"], bank_free[0], bank_free[1],
                       bank_free[2], bank_free[3]]
                if ti == 0:
                    w_m.append(w2_ready[hb])
                for db in range(FB):
                    gps = PA if db < 4 else PB
                    c0 = (db % 4) * ST
                    o('T', lambda e, db=db, hb=hb, gps=gps, c0=c0:
                      e.matmul(
                        gps[:, c0:c0 + ST],
                        w2q[:, hb * D + db * 128:hb * D + (db + 1) * 128],
                        hqb[:, hb * ST:(hb + 1) * ST],
                        start=(hb == 0 and db % 2 == 0), stop=False),
                      waits=(w_m if db == 0 else []))
            paw = None
            for db in range(FB):
                gps = PA if db < 4 else PB
                c0 = (db % 4) * ST
                g2 = 64 * (db // FBH)
                o('T', lambda e, db=db, gps=gps, c0=c0, g2=g2: e.matmul(
                    gps[:, c0:c0 + ST],
                    b2row(db), ones_row[g2:g2 + 1, 0:ST], start=False,
                    stop=(db % 2 == 1)),
                  inc=(sT if db in (NPA - 1, FB - 1) else None))
                if db == NPA - 1:
                    paw = (sT, sT.n)
            As["hq_free"] = (sT, sT.n)
            o('S', lambda e: e.activation(
                gsA[:, 0:NPA * ST], PA[:, 0:NPA * ST], AF.Silu),
              waits=[paw, As["gs_free"]], inc=sS)
            bank_free[0] = (sS, sS.n)
            bank_free[1] = (sS, sS.n)
            o('Q', lambda e, s=s, t0=t0: e.dma_start(
                g_scr[s, 0:NPA * 128, t0:t0 + ST]
                .rearrange("(f p) t -> p f t", p=128),
                gsA[:, 0:NPA * ST].rearrange(
                    "p (f c) -> p f c", f=NPA, c=ST)),
              waits=[(sS, sS.n)], inc=qg[0], inc_k=16)
            As["gs_free"] = (qg[0], qg[0].n)
            if NPB:
                o('S', lambda e: e.activation(
                    gsB[:, 0:NPB * ST], PB[:, 0:NPB * ST], AF.Silu),
                  waits=[(sT, sT.n), As["gsB_free"], xn_gate], inc=sS)
                bank_free[2] = (sS, sS.n)
                bank_free[3] = (sS, sS.n)
                o('Q', lambda e, s=s, t0=t0: e.dma_start(
                    g_scr[s, NPA * 128:FB * 128, t0:t0 + ST]
                    .rearrange("(f p) t -> p f t", p=128),
                    gsB[:, 0:NPB * ST].rearrange(
                        "p (f c) -> p f c", f=NPB, c=ST)),
                  waits=[(sS, sS.n)], inc=qg[1], inc_k=16)
                As["gsB_free"] = (qg[1], qg[1].n)

        emit_ln_xq(0)
        for ti in range(NT):
            emit_ffn1(ti)
            if ti + 1 < NT:
                emit_ln_xq(ti + 1)
            emit_ffn2(ti, None)

        a_done_T = (sT, sT.n)
        a_done_S = (sS, sS.n)
        a_done_V = (sV, sV.n)
        a_qdone = [(w.h if False else w, w.n) for w in (qn[0], qn[1], qg[0], qg[1], qg[2], qg[3])]

        wq_sb = AB[:, c_wq + 0 * FB * D:c_wq + 1 * FB * D]
        wk_sb = AB[:, c_wq + 1 * FB * D:c_wq + 2 * FB * D]
        wv_sb = AB[:, c_wq + 2 * FB * D:c_wq + 3 * FB * D]
        wo_sb = AB[:, c_wq + 3 * FB * D:c_wq + 4 * FB * D]
        for wsb, wdr in ((wq_sb, wqT), (wk_sb, wkT), (wv_sb, wvT),
                         (wo_sb, woT)):
            o('G', lambda e, wsb=wsb, wdr=wdr: e.dma_start(
                _fv(wsb, FB, D), wdr.rearrange("(f p) c -> p f c", p=128)),
              waits=[a_done_T, a_done_S, a_done_V] + a_qdone +
                    ([(qd, qd.n)] if DBG else []),
              inc=qa, inc_k=16)
        attw_ready = (qa, qa.n)

        xnc = AB[:, c_xnc:c_xnc + FB * QB]
        kt_sb = AB[:, c_kt:c_kt + NH * S]
        v_sb = AB[:, c_v:c_v + KT * D]
        qt_sb = [AB[:, c_qt:c_qt + NH * QB],
                 AB[:, c_qt + NH * QB:c_qt + 2 * NH * QB]]
        e_sb = [AB[:, c_e:c_e + 2 * QB], AB[:, c_e + 2 * QB:c_e + 4 * QB]]
        sqr_sb = AB[:, c_sqr:c_sqr + QB]
        u_all = AF4[:, f_uall:f_uall + 2 * HPG * QB]
        gs2 = [AF4[:, f_gs2 + i * QB:f_gs2 + (i + 1) * QB] for i in range(2)]
        xts2 = [AF4[:, f_xts2 + i * QB:f_xts2 + (i + 1) * QB] for i in range(2)]
        outst = [AF4[:, f_outst + i * QB:f_outst + (i + 1) * QB]
                 for i in range(2)]
        gs2p = AF4[:, f_gs2:f_gs2 + 2 * QB]
        xts2p = AF4[:, f_xts2:f_xts2 + 2 * QB]
        outstp = AF4[:, f_outst:f_outst + 2 * QB]

        st = {
            "xnc_load": None, "xnc_free": None,
            "qt_free": [None, None], "obf_free": [None, None],
            "obf_dump": [None, None],
            "e_free": [None, None],
            "U_free": None, "D_free": None, "M_free": None,
            "u_free": None, "sq_free": None,
            "gx_free": [None, None], "out_free": [None, None],
            "kv_ready": {},
        }

        def setup_chunk(sq_, c):
            if DBG:
                o('Q', lambda e, sq_=sq_, c=c: e.dma_start(
                    dbg_xc[sq_, c], xnc),
                  waits=[st["xnc_load"], (qd, qd.n)], inc=qd, inc_k=16)
            for hb in range(NH):
                pw = hb % 2
                kps = ps[pw * 2]
                for kb in range(FB):
                    o('T', lambda e, hb=hb, kb=kb, kps=kps: e.matmul(
                        kps[:, 0:QB],
                        wk_sb[:, kb * D + hb * 128:kb * D + (hb + 1) * 128],
                        xnc[:, kb * QB:(kb + 1) * QB],
                        start=(kb == 0), stop=(kb == FB - 1)),
                      waits=([st["xnc_load"], attw_ready, bank_free[pw * 2]]
                             if kb == 0 else []),
                      inc=(sT if kb == FB - 1 else None))
                o('S', lambda e, hb=hb, c=c, kps=kps: e.copy(
                    kt_sb[:, hb * S + c * QB:hb * S + (c + 1) * QB],
                    kps[:, 0:QB]), waits=[(sT, sT.n)], inc=sS)
                bank_free[pw * 2] = (sS, sS.n)
            for tt4 in range(TPQ):
                for half in range(NV):
                    pw = half % 2
                    vps = ps[pw * 2 + 1]
                    for kb in range(FB):
                        o('T', lambda e, tt4=tt4, half=half, kb=kb, vps=vps:
                          e.matmul(
                            vps[:, 0:VCW],
                            xnc[:, kb * QB + tt4 * 128:
                                kb * QB + tt4 * 128 + 128],
                            wv_sb[:, kb * D + half * VCW:
                                  kb * D + half * VCW + VCW],
                            start=(kb == 0), stop=(kb == FB - 1)),
                          waits=([st["xnc_load"], bank_free[pw * 2 + 1]]
                                 if kb == 0 else []),
                          inc=(sT if kb == FB - 1 else None))
                    o('S', lambda e, c=c, tt4=tt4, half=half, vps=vps: e.copy(
                        v_sb[:, (c * TPQ + tt4) * D + half * VCW:
                             (c * TPQ + tt4) * D + half * VCW + VCW],
                        vps[:, 0:VCW]), waits=[(sT, sT.n)], inc=sS)
                    bank_free[pw * 2 + 1] = (sS, sS.n)
            qi = c % 2
            for hb in range(NH):
                pw = hb % 2
                qps = ps[pw * 2]
                w0 = [st["xnc_load"], bank_free[pw * 2]]
                if hb == 0:
                    w0 += [st["qt_free"][qi], st["obf_free"][qi],
                           st["obf_dump"][qi]]
                for kb in range(FB):
                    o('T', lambda e, hb=hb, kb=kb, qps=qps: e.matmul(
                        qps[:, 0:QB],
                        wq_sb[:, kb * D + hb * 128:kb * D + (hb + 1) * 128],
                        xnc[:, kb * QB:(kb + 1) * QB],
                        start=(kb == 0), stop=(kb == FB - 1)),
                      waits=(w0 if kb == 0 else []),
                      inc=(sT if kb == FB - 1 else None))
                o('S', lambda e, hb=hb, qi=qi, qps=qps: e.copy(
                    qt_sb[qi][:, hb * QB:(hb + 1) * QB], qps[:, 0:QB]),
                  waits=[(sT, sT.n)], inc=sS)
                bank_free[pw * 2] = (sS, sS.n)
            st["xnc_free"] = (sT, sT.n)
            st["kv_ready"][c] = (sS, sS.n)

        if DBG:
            o('V', lambda e: e.memset(u_all, 0.0),
              waits=[a_done_V], inc=sV)
            o('V', lambda e: e.memset(AF4[0:8, f_d8:f_d8 + QB], 0.0), inc=sV)
            st["u_free"] = (sV, sV.n)
        for s in range(BPC):
            st["kv_ready"].clear()
            for fb in range(FB):
                o('Q', lambda e, s=s, fb=fb: e.dma_start(
                    xnc[:, fb * QB:(fb + 1) * QB],
                    xn_scr[s, fb * 128:(fb + 1) * 128, 0:QB]),
                  waits=(([st["xnc_free"], a_done_S] + a_qdone)
                         if fb == 0 else []), inc=qc, inc_k=16)
            st["xnc_load"] = (qc, qc.n)
            setup_chunk(s, 0)

            for qb in range(NQB):
                q0 = qb * QB
                nk = q0 // 128 + TPQ
                qi = qb % 2
                if qb + 1 < NQB:
                    for fb in range(FB):
                        o('Q', lambda e, s=s, qb=qb, fb=fb: e.dma_start(
                            xnc[:, fb * QB:(fb + 1) * QB],
                            xn_scr[s, fb * 128:(fb + 1) * 128,
                                   (qb + 1) * QB:(qb + 2) * QB]),
                          waits=([st["xnc_free"]] if fb == 0 else []),
                          inc=qc, inc_k=16)
                    st["xnc_load"] = (qc, qc.n)
                o('Q', lambda e, s=s, q0=q0: e.dma_start(
                    gs2p.rearrange("p (f c) -> p f c", f=2, c=QB),
                    g_scr[s, 0:256, q0:q0 + QB]
                    .rearrange("(f p) t -> p f t", p=128)),
                  waits=[st["gx_free"][0], st["gx_free"][1]] + a_qdone,
                  inc=qe[0], inc_k=16)
                o('Q', lambda e, s=s, q0=q0: e.dma_start(
                    xts2p.rearrange("p (f c) -> p f c", f=2, c=QB),
                    xT[s, 0:256, q0:q0 + QB]
                    .rearrange("(f p) t -> p f t", p=128)),
                  inc=qe[0], inc_k=16)
                gx_load = [(qe[0], qe[0].n), (qe[0], qe[0].n)]

                obf = qt_sb[qi]
                if DBG:
                    o('Q', lambda e, s=s, qb=qb, qi=qi: e.dma_start(
                        dbg_q[s, qb], qt_sb[qi]),
                      waits=[st["kv_ready"][qb], (qd, qd.n)], inc=qd, inc_k=16)
                    if qb == NQB - 1:
                        o('Q', lambda e, s=s: e.dma_start(dbg_k[s], kt_sb),
                          waits=[(qd, qd.n)], inc=qd, inc_k=16)
                        o('Q', lambda e, s=s: e.dma_start(dbg_v[s], v_sb),
                          waits=[(qd, qd.n)], inc=qd, inc_k=16)
                for g in range(NG):
                    pend = [None]

                    def flush_pend():
                        if pend[0] is not None:
                            fn = pend[0]
                            pend[0] = None
                            fn()

                    for hl in range(HPG):
                        h = g * HPG + hl
                        for j in range(nk):
                            p = j % 2
                            w_s = [st["kv_ready"][qb], bank_free[p * 2]]
                            diag = j >= nk - TPQ
                            soff = 384 + (q0 - j * 128)
                            for sub in range(2):
                                last = sub == 1 and not diag
                                o('T', lambda e, h=h, j=j, p=p, qi=qi,
                                  sub=sub: e.matmul(
                                    ps[p * 2 + sub][:, 0:QB],
                                    kt_sb[sub * 64:sub * 64 + 64,
                                          h * S + j * 128:h * S + j * 128 + 128],
                                    qt_sb[qi][sub * 64:sub * 64 + 64,
                                              h * QB:(h + 1) * QB],
                                    start=True, stop=not diag),
                                  waits=((w_s + [bank_free[p * 2 + 1]])
                                         if sub == 0 else []),
                                  inc=(sT if last else None))
                                if diag:
                                    mw = min(QB, 512 - soff)
                                    o('T', lambda e, p=p, sub=sub, soff=soff,
                                      mw=mw: e.matmul(
                                        ps[p * 2 + sub][:, 0:mw], tril_sb,
                                        mask_sb[:, soff:soff + mw],
                                        start=False, stop=True),
                                      inc=(sT if sub == 1 else None))
                            scores_done = (sT, sT.n)
                            PSpair = PA if p == 0 else PB
                            o('S', lambda e, p=p, PSpair=PSpair: e.activation(
                                e_sb[p], PSpair, AF.Exp, scale=0.125),
                              waits=[scores_done, st["e_free"][p]], inc=sS)
                            bank_free[p * 2] = (sS, sS.n)
                            bank_free[p * 2 + 1] = (sS, sS.n)
                            e_ready = (sS, sS.n)
                            flush_pend()

                            def mkpv(h=h, j=j, p=p, hl=hl, nk=nk,
                                     e_ready=e_ready):
                                def fn():
                                    w_u = [e_ready]
                                    if j == 0:
                                        w_u += [st["U_free"], bank_free[4],
                                                bank_free[5]]
                                    o('T', lambda e: e.matmul(
                                        psU0[:, 0:QB],
                                        v_sb[:, j * D + h * 128:
                                             j * D + (h + 1) * 128],
                                        e_sb[p][:, 0:QB],
                                        start=(j == 0), stop=(j == nk - 1)),
                                      waits=w_u)
                                    o('T', lambda e: e.matmul(
                                        psU1[:, 0:QB],
                                        v_sb[:, j * D + h * 128:
                                             j * D + (h + 1) * 128],
                                        e_sb[p][:, QB:2 * QB],
                                        start=(j == 0), stop=(j == nk - 1)))
                                    m0, m1 = HPG + hl, hl
                                    o('T', lambda e: e.matmul(
                                        psD[0:8, 0:QB],
                                        ind_sb[:, m0 * 8:m0 * 8 + 8],
                                        e_sb[p][:, 0:QB],
                                        start=(hl == 0 and j == 0),
                                        stop=False),
                                      waits=([st["D_free"]]
                                             if (hl == 0 and j == 0) else []))
                                    o('T', lambda e: e.matmul(
                                        psD[0:8, 0:QB],
                                        ind_sb[:, m1 * 8:m1 * 8 + 8],
                                        e_sb[p][:, QB:2 * QB],
                                        start=False,
                                        stop=(hl == HPG - 1 and j == nk - 1)),
                                      inc=sT)
                                    st["e_free"][p] = (sT, sT.n)
                                    if j == nk - 1:
                                        ucol = hl * 2
                                        o('V', lambda e: e.tensor_copy(
                                            u_all[:, ucol * QB:
                                                  (ucol + 2) * QB],
                                            PC),
                                          waits=[(sT, sT.n), st["u_free"]],
                                          inc=sV)
                                        st["U_free"] = (sV, sV.n)
                                return fn
                            pend[0] = mkpv()
                    flush_pend()
                    if DBG:
                        o('S', lambda e: e.copy(d8[0:2 * HPG, :],
                                                psD[0:2 * HPG, 0:QB]),
                          waits=[(sT, sT.n)], inc=sS)
                        o('Q', lambda e, s=s, qb=qb, g=g: e.dma_start(
                            dbg_u[s, qb, g], u_all),
                          waits=[(sS, sS.n), (qd, qd.n)], inc=qd, inc_k=16)
                        o('Q', lambda e, s=s, qb=qb, g=g: e.dma_start(
                            dbg_d[s, qb, g], d8), waits=[(qd, qd.n)],
                          inc=qd, inc_k=16)
                    dbg_done = (qd, qd.n)
                    o('V', lambda e: e.reciprocal(
                        al8[0:2 * HPG, :], psD[0:2 * HPG, 0:QB]),
                      waits=[(sT, sT.n)])
                    o('V', lambda e: e.tensor_scalar(
                        al8[0:HPG, :], al8[0:HPG, :],
                        lam_col[0:HPG, 0:1], None, OP.mult), inc=sV)
                    st["D_free"] = (sV, sV.n)
                    alpha_ready = (sV, sV.n)
                    for hl in range(HPG):
                        h = g * HPG + hl
                        ucol = hl * 2
                        pr = (hl % 2) * 2
                        PAB = PA if pr == 0 else PB
                        o('T', lambda e, hl=hl, pr=pr: e.matmul(
                            ps[pr][:, 0:QB],
                            sel_sb[0:2 * HPG,
                                   (HPG + hl) * 128:(HPG + hl + 1) * 128],
                            al8[0:2 * HPG, :], start=True, stop=True),
                          waits=[alpha_ready, bank_free[pr],
                                 bank_free[pr + 1]])
                        o('T', lambda e, hl=hl, pr=pr: e.matmul(
                            ps[pr + 1][:, 0:QB],
                            sel_sb[0:2 * HPG, hl * 128:(hl + 1) * 128],
                            al8[0:2 * HPG, :], start=True, stop=True),
                          inc=sT)
                        bc_a = (sT, sT.n)
                        o('V', lambda e, ucol=ucol, PAB=PAB: e.tensor_tensor(
                            u_all[:, ucol * QB:(ucol + 2) * QB],
                            u_all[:, ucol * QB:(ucol + 2) * QB],
                            PAB, OP.mult),
                          waits=[bc_a] + ([dbg_done] if DBG else []))
                        o('V', lambda e, ucol=ucol: e.tensor_tensor(
                            u_all[:, ucol * QB:(ucol + 1) * QB],
                            u_all[:, ucol * QB:(ucol + 1) * QB],
                            u_all[:, (ucol + 1) * QB:(ucol + 2) * QB],
                            OP.subtract), inc=sV)
                        bank_free[pr] = (sV, sV.n)
                        bank_free[pr + 1] = (sV, sV.n)
                        o_done = (sV, sV.n)
                        o('S', lambda e, ucol=ucol: e.activation(
                            sqr_sb, u_all[:, ucol * QB:(ucol + 1) * QB],
                            AF.Square), waits=[o_done, st["sq_free"]], inc=sS)
                        w_ms = [(sS, sS.n)]
                        if hl == 0:
                            w_ms.append(st["M_free"])
                        o('T', lambda e, hl=hl: e.matmul(
                            psM[0:8, 0:QB], ind_sb[:, hl * 8:hl * 8 + 8],
                            sqr_sb, start=(hl == 0), stop=(hl == HPG - 1)),
                          waits=w_ms, inc=sT)
                        st["sq_free"] = (sT, sT.n)
                    o('S', lambda e: e.activation(
                        bt8[0:HPG, :], psM[0:HPG, 0:QB], AF.Sqrt,
                        bias=fc[0:HPG, 0:1], scale=1.0 / 128.0),
                      waits=[(sT, sT.n)], inc=sS)
                    st["M_free"] = (sS, sS.n)
                    o('V', lambda e: e.reciprocal(bt8[0:HPG, :],
                                                  bt8[0:HPG, :]),
                      waits=[(sS, sS.n)], inc=sV)
                    beta_ready = (sV, sV.n)
                    for hl in range(HPG):
                        h = g * HPG + hl
                        ucol = hl * 2
                        pb_ = hl % 2
                        o('T', lambda e, hl=hl, pb_=pb_: e.matmul(
                            ps[pb_][:, 0:QB],
                            sel_sb[0:HPG, hl * 128:(hl + 1) * 128],
                            bt8[0:HPG, :], start=True, stop=True),
                          waits=[beta_ready, bank_free[pb_]], inc=sT)
                        bc_b = (sT, sT.n)
                        o('V', lambda e, h=h, ucol=ucol, pb_=pb_:
                          e.scalar_tensor_tensor(
                            obf[:, h * QB:(h + 1) * QB],
                            u_all[:, ucol * QB:(ucol + 1) * QB],
                            1.0 - LAMBDA_INIT, ps[pb_][:, 0:QB],
                            OP.mult, OP.mult), waits=[bc_b], inc=sV)
                        bank_free[pb_] = (sV, sV.n)
                    st["u_free"] = (sV, sV.n)
                obf_done = (sV, sV.n)
                if DBG:
                    o('Q', lambda e, s=s, qb=qb: e.dma_start(
                        dbg_o[s, qb], obf), waits=[obf_done, (qd, qd.n)],
                      inc=qd, inc_k=16)
                    st["obf_dump"][qi] = (qd, qd.n)
                st["qt_free"][qi] = (sT, sT.n)

                for dp in range(FB // 2):
                    db0 = dp * 2
                    for half in range(2):
                        db = db0 + half
                        for kb in range(NH):
                            o('T', lambda e, db=db, kb=kb, half=half:
                              e.matmul(
                                PC[:, half * QB:(half + 1) * QB],
                                wo_sb[:, kb * D + db * 128:
                                      kb * D + (db + 1) * 128],
                                obf[:, kb * QB:(kb + 1) * QB],
                                start=(kb == 0), stop=(kb == NH - 1)),
                              waits=([obf_done, bank_free[4], bank_free[5]]
                                     if (half == 0 and kb == 0) else []),
                              inc=(sT if (half == 1 and kb == NH - 1)
                                   else None))
                    wo_done = (sT, sT.n)
                    o('V', lambda e: e.tensor_tensor(
                        outstp, PC, gs2p, OP.mult),
                      waits=[wo_done, gx_load[0], gx_load[1],
                             st["out_free"][0], st["out_free"][1]])
                    o('V', lambda e: e.tensor_tensor(
                        outstp, outstp, xts2p, OP.add), inc=sV)
                    bank_free[4] = (sV, sV.n)
                    bank_free[5] = (sV, sV.n)
                    st["gx_free"][0] = (sV, sV.n)
                    st["gx_free"][1] = (sV, sV.n)
                    o('Q', lambda e, s=s, q0=q0, dp=dp: e.dma_start(
                        outT[s, dp * 256:(dp + 1) * 256, q0:q0 + QB]
                        .rearrange("(f p) t -> p f t", p=128),
                        outstp.rearrange("p (f c) -> p f c", f=2, c=QB)),
                      waits=[(sV, sV.n)], inc=qo[dp % 2], inc_k=16)
                    st["out_free"][0] = (qo[dp % 2], qo[dp % 2].n)
                    st["out_free"][1] = st["out_free"][0]
                    if db0 + 2 < FB:
                        o('Q', lambda e, s=s, q0=q0, dp=dp: e.dma_start(
                            gs2p.rearrange("p (f c) -> p f c", f=2, c=QB),
                            g_scr[s, (dp + 1) * 256:(dp + 2) * 256,
                                  q0:q0 + QB]
                            .rearrange("(f p) t -> p f t", p=128)),
                          waits=[(sV, sV.n)], inc=qe[dp % 2], inc_k=16)
                        o('Q', lambda e, s=s, q0=q0, dp=dp: e.dma_start(
                            xts2p.rearrange("p (f c) -> p f c", f=2, c=QB),
                            xT[s, (dp + 1) * 256:(dp + 2) * 256, q0:q0 + QB]
                            .rearrange("(f p) t -> p f t", p=128)),
                          inc=qe[dp % 2], inc_k=16)
                        gx_load = [(qe[dp % 2], qe[dp % 2].n)] * 2
                st["U_free"] = (sV, sV.n)
                st["obf_free"][qi] = (sT, sT.n)

                if qb + 1 < NQB:
                    setup_chunk(s, qb + 1)

        def _final_waits_q(e):
            for sm in (qx, qc, qn[0], qn[1], qg[0], qg[1], qg[2], qg[3],
                       qe[0], qe[1], qo[0], qo[1], qd):
                if sm.n:
                    e.wait_ge(sm.h, sm.n)
            return None
        def _final_waits_g(e):
            for sm in (qk, qw[0], qw[1], qa):
                if sm.n:
                    e.wait_ge(sm.h, sm.n)
            return None
        o('Q', _final_waits_q)
        o('G', _final_waits_g)

    sems = [sT, sS, sV, qx, qc, qk, qa, qd] + qn + qg + qe + qo + qw
    with nc.Block() as blk:
        def _run(name):
            def body(eng):
                for sm in sems:
                    sm.n = 0
                ctx.begin(name, eng)
                schedule()
            return body
        blk.sync(_run('Q'))
        blk.gpsimd(_run('G'))
        blk.tensor(_run('T'))
        blk.scalar(_run('S'))
        blk.vector(_run('V'))
    return nc


def _host_constants():
    import ml_dtypes
    ramp = (np.arange(896)[None, :] <= np.arange(128)[:, None] + 383)
    ramp = (MASKC * ramp.astype(np.float32)).astype(ml_dtypes.bfloat16)
    tril = (np.arange(128)[:, None] <= np.arange(128)[None, :])
    tril = tril.astype(ml_dtypes.bfloat16)
    ind = np.zeros((128, 64), dtype=np.float32)
    for m in range(8):
        ind[:, m * 8 + m] = 1.0
    sel = np.zeros((8, 8 * 128), dtype=np.float32)
    for m in range(8):
        sel[m, m * 128:(m + 1) * 128] = 1.0
    return ramp, tril, ind.astype(ml_dtypes.bfloat16), sel


def kernel(**inputs):
    cfg = CFG
    BPC = cfg["BPC"]
    x = np.ascontiguousarray(np.asarray(inputs["L0_attn_x"], dtype=np.float32))
    ncores = x.shape[0] // BPC
    mask, tril, ind, sel = _host_constants()
    shared = {
        "wqT": np.ascontiguousarray(np.asarray(inputs["L0_attn_Wq"], np.float32).T),
        "wkT": np.ascontiguousarray(np.asarray(inputs["L0_attn_Wk"], np.float32).T),
        "wvT": np.ascontiguousarray(np.asarray(inputs["L0_attn_Wv"], np.float32).T),
        "woT": np.ascontiguousarray(np.asarray(inputs["L0_attn_Wo"], np.float32).T),
        "w1T": np.ascontiguousarray(np.asarray(inputs["L0_ffn_w1"], np.float32).T),
        "w2T": np.ascontiguousarray(np.asarray(inputs["L0_ffn_w2"], np.float32).T),
        "b1": np.ascontiguousarray(np.asarray(inputs["L0_ffn_b1"], np.float32)),
        "b2": np.ascontiguousarray(np.asarray(inputs["L0_ffn_b2"], np.float32)),
        "lqk": np.ascontiguousarray(np.concatenate([
            np.asarray(inputs["L0_attn_lq1"], np.float32),
            np.asarray(inputs["L0_attn_lk1"], np.float32),
            np.asarray(inputs["L0_attn_lq2"], np.float32),
            np.asarray(inputs["L0_attn_lk2"], np.float32)])[None, :]),
        "mask": mask, "tril": tril, "ind": ind, "sel": sel,
    }
    nc = build_nc(cfg)
    in_maps = []
    for c in range(ncores):
        xs = x[c * BPC:(c + 1) * BPC]
        m = dict(shared)
        m["xT"] = np.ascontiguousarray(xs.transpose(0, 2, 1))
        in_maps.append(m)
    res = run_bass_kernel_spmd(nc, in_maps, core_ids=list(range(ncores)))
    outs = [np.asarray(r["outT"]).transpose(0, 2, 1) for r in res.results]
    return np.ascontiguousarray(np.concatenate(outs, axis=0)).astype(np.float32)
